# version 15
# baseline (speedup 1.0000x reference)
import contextlib
import numpy as np
import ml_dtypes
import concourse.bass as bass
import concourse.mybir as mybir
from concourse.bass_utils import run_bass_kernel_spmd

F32 = mybir.dt.float32
BF16 = mybir.dt.bfloat16
AF = mybir.ActivationFunctionType
ALU = mybir.AluOpType

B, T, D = 2, 4096, 2048
HQ, HKV, HD = 32, 8, 64
TOK = 1024
INTER = 5632
NIT = INTER // 128
NTC = T // 512
NKT = T // 128
C_SHIFT = 10.0
EPS = 1e-5
N_CORES = 8


class Sems:
    def __init__(self, nc, es):
        self.nc = nc
        self.es = es
        self.cnt = {}
        self.h = {}

    def new(self, *names):
        for n in names:
            self.h[n] = self.es.enter_context(self.nc.semaphore(n))
            self.cnt[n] = 0

    def inc(self, name, n=1):
        self.cnt[name] += n
        return self.h[name], n

    def __getitem__(self, name):
        return self.h[name]


def build_program(debug=None):
    nc = bass.Bass("TRN2", target_bir_lowering=False)

    d_xT = nc.dram_tensor("xT", [D, T], F32, kind="ExternalInput")
    d_xqT = nc.dram_tensor("xqT", [D, TOK], F32, kind="ExternalInput")
    d_xres = nc.dram_tensor("xres", [TOK, D], F32, kind="ExternalInput").ap()
    d_wqT = nc.dram_tensor("wqT", [D, D], F32, kind="ExternalInput")
    d_wkT = nc.dram_tensor("wkT", [D, 512], F32, kind="ExternalInput")
    d_wvT = nc.dram_tensor("wvT", [D, 512], F32, kind="ExternalInput")
    d_woT = nc.dram_tensor("woT", [D, D], F32, kind="ExternalInput").ap()
    d_bqt = nc.dram_tensor("bqt", [128, 16], F32, kind="ExternalInput").ap()
    d_bkt = nc.dram_tensor("bkt", [128, 4], F32, kind="ExternalInput").ap()
    d_bvb = nc.dram_tensor("bvb", [128, 512], F32, kind="ExternalInput").ap()
    d_bob = nc.dram_tensor("bob", [128, D], F32, kind="ExternalInput").ap()
    d_gwT = nc.dram_tensor("gwT", [D, INTER], F32, kind="ExternalInput")
    d_uwT = nc.dram_tensor("uwT", [D, INTER], F32, kind="ExternalInput")
    d_dwT = nc.dram_tensor("dwT", [INTER, D], F32, kind="ExternalInput").ap()
    d_gst = nc.dram_tensor("gst", [128, NIT], F32, kind="ExternalInput").ap()
    d_ust = nc.dram_tensor("ust", [128, NIT], F32, kind="ExternalInput").ap()
    d_dsb = nc.dram_tensor("dsb", [128, D], F32, kind="ExternalInput").ap()
    d_cosK = nc.dram_tensor("cosK", [128, T], F32, kind="ExternalInput").ap()
    d_sinK = nc.dram_tensor("sinK", [128, T], F32, kind="ExternalInput").ap()
    d_cosQ = nc.dram_tensor("cosQ", [128, TOK], F32, kind="ExternalInput").ap()
    d_sinQ = nc.dram_tensor("sinQ", [128, TOK], F32, kind="ExternalInput").ap()
    d_idf = nc.dram_tensor("identf", [128, 128], F32, kind="ExternalInput").ap()
    d_out = nc.dram_tensor("out", [TOK, D], F32, kind="ExternalOutput").ap()
    d_zb = nc.dram_tensor("zb", [64, 1024], F32)
    d_ydr = nc.dram_tensor("ydr", [TOK, D], F32)

    dbg_outs = {}

    def dbg(name, shape, dt):
        dbg_outs[name] = nc.dram_tensor(name, shape, dt,
                                        kind="ExternalOutput").ap()
        return dbg_outs[name]

    es = contextlib.ExitStack()
    S = Sems(nc, es)
    S.new("x", "xb", "mmKV", "kev", "vev", "sh", "rope",
          "wld", "wcast", "mmQ", "qev", "qsh", "qrope",
          "psS", "psE", "psA", "zr", "zbd", "bc", "aev",
          "wo_x", "wo_mm", "wo_ev", "ln1", "ysp", "f1c", "f1mm", "f1ev",
          "f1w", "f1wc", "gu_mm", "gu_ev", "hdn",
          "f2w", "f2c", "f2mm", "f2ev", "ln2", "outd", "misc",
          "x0", "x1", "x2", "x3", "wq0", "wq1", "wox0", "wox1", "f1w0", "f1w1",
          "f2w0", "f2w1", "f2w2", "zb0", "zb1", "bc0", "bc1", "ysp0", "ysp1",
          "pre", "tabK", "tabQ", "bobd", "dsbd", "wo0", "wo1",
          "ropev", "qropev", "xq0", "xq1", "xqc", "ropea", "tab0", "tab1",
          "zb", "wx0", "wx1", "tmm", "tev",
          "f1g0", "f1g1", "f1u0", "f1u1", "sdv")

    with es:
        bvb = es.enter_context(nc.sbuf_tensor("s_bvb", [128, 512], F32))
        bkt = es.enter_context(nc.sbuf_tensor("s_bkt", [128, 4], F32))
        bqt = es.enter_context(nc.sbuf_tensor("s_bqt", [128, 16], F32))
        gst = es.enter_context(nc.sbuf_tensor("s_gst", [128, NIT], F32))
        ust = es.enter_context(nc.sbuf_tensor("s_ust", [128, NIT], F32))
        cneg = es.enter_context(nc.sbuf_tensor("cneg", [128, 1], F32))
        epsp = es.enter_context(nc.sbuf_tensor("epsp", [128, 1], F32))
        ident = es.enter_context(nc.sbuf_tensor("ident", [128, 128], F32))
        lt2 = es.enter_context(nc.sbuf_tensor("lt2", [128, 64], F32))
        ngst = es.enter_context(nc.sbuf_tensor("ngst", [128, NIT], F32))

        for ap_d, t_s in ((d_bvb, bvb), (d_bkt, bkt), (d_bqt, bqt),
                          (d_gst, gst), (d_ust, ust), (d_idf, ident)):
            nc.scalar.dma_start(t_s[:], ap_d[:]).then_inc(*S.inc("pre", 16))
        nc.vector.memset(cneg[:], -C_SHIFT)
        nc.vector.memset(epsp[:], EPS)
        nc.vector.wait_ge(S["pre"], 96)
        nc.vector.tensor_scalar_mul(ngst[:], gst[:], -1.0
                                    ).then_inc(*S.inc("misc"))

        kv_stack = contextlib.ExitStack()
        xq_right = contextlib.ExitStack()
        xq_bf = xq_right.enter_context(
            nc.sbuf_tensor("xq_bf", [128, 16 * TOK], BF16, side="right"))
        KT = kv_stack.enter_context(nc.sbuf_tensor("KT", [128, 4 * T], BF16))
        Vp = kv_stack.enter_context(
            nc.sbuf_tensor("Vp", [128, HKV * NKT * 65], BF16))
        nc.vector.memset(
            bass.AP(Vp, 64, [[HKV * NKT * 65, 128], [65, HKV * NKT]]), 1.0)

        with contextlib.ExitStack() as pa1:
            cosb = [pa1.enter_context(
                nc.sbuf_tensor(f"cosb{i}", [128, 512], F32)) for i in range(2)]
            sinb = [pa1.enter_context(
                nc.sbuf_tensor(f"sinb{i}", [128, 512], F32)) for i in range(2)]
            wk_bf = pa1.enter_context(nc.sbuf_tensor("wk_bf", [128, 8192], BF16))
            wv_bf = pa1.enter_context(nc.sbuf_tensor("wv_bf", [128, 8192], BF16))
            Kstg = pa1.enter_context(nc.sbuf_tensor("Kstg", [128, 2048], F32))
            Ksh = pa1.enter_context(nc.sbuf_tensor("Ksh", [128, 2048], F32))
            xbb = [pa1.enter_context(nc.sbuf_tensor(f"xbb{i}", [128, 512], BF16))
                   for i in range(4)]
            kt1 = [pa1.enter_context(nc.sbuf_tensor(f"kt1{i}", [128, 512], F32))
                   for i in range(2)]
            kt2 = [pa1.enter_context(nc.sbuf_tensor(f"kt2{i}", [128, 512], F32))
                   for i in range(2)]
            psK = [pa1.enter_context(nc.psum_tensor(f"psK{i}", [128, 512], F32))
                   for i in range(4)]
            psV = [pa1.enter_context(nc.psum_tensor(f"psV{i}", [128, 512], F32))
                   for i in range(4)]


            for dsrc, bfdst in ((d_wkT, wk_bf), (d_wvT, wv_bf)):
                srcap = bass.AP(dsrc, 0, [[512, 128], [128 * 512, 16], [1, 512]])
                dstap = bass.AP(bfdst, 0, [[8192, 128], [512, 16], [1, 512]])
                nc.gpsimd.dma_start(dstap, srcap).then_inc(*S.inc("pre", 16))

            def kv_load(k):
                bi = k % 4
                tc_, d_ = divmod(k, 16)
                if k >= 4:
                    nc.gpsimd.wait_ge(S["mmKV"], k - 3)
                nc.gpsimd.dma_start(
                    xbb[bi][:],
                    d_xT.ap()[d_ * 128:(d_ + 1) * 128,
                              tc_ * 512:(tc_ + 1) * 512],
                ).then_inc(*S.inc(f"x{bi}", 16))

            def kv_mm(k):
                bi = k % 4
                tc_, d_ = divmod(k, 16)
                nc.tensor.wait_ge(S[f"x{bi}"], 16 * (k // 4 + 1))
                if k == 0:
                    nc.tensor.wait_ge(S["pre"], S.cnt["pre"])
                if tc_ > 0 and d_ == 0:
                    nc.tensor.wait_ge(S["kev"], tc_)
                    nc.tensor.wait_ge(S["vev"], tc_)
                for m in range(4):
                    nc.tensor.matmul(
                        psK[m][:],
                        wk_bf[:, d_ * 512 + m * 128:d_ * 512 + (m + 1) * 128],
                        xbb[bi][:], start=(d_ == 0), stop=(d_ == 15))
                mm = None
                for tt in range(4):
                    mm = nc.tensor.matmul(
                        psV[tt][:], xbb[bi][:, tt * 128:(tt + 1) * 128],
                        wv_bf[:, d_ * 512:(d_ + 1) * 512],
                        start=(d_ == 0), stop=(d_ == 15))
                mm.then_inc(*S.inc("mmKV"))

            def kv_evac_rope(tc_):
                nc.scalar.wait_ge(S["mmKV"], (tc_ + 1) * 16)
                if tc_ == 0:
                    nc.scalar.wait_ge(S["pre"], 96)
                else:
                    nc.scalar.wait_ge(S["ropev"], 4 * tc_)
                    nc.scalar.wait_ge(S["sh"], 64 * tc_)
                act = None
                for m in range(4):
                    act = nc.scalar.activation(
                        Kstg[:, m * 512:(m + 1) * 512], psK[m][:], AF.Identity,
                        bias=bkt[:, m:m + 1], scale=1.0)
                act.then_inc(*S.inc("kev"))
                nc.scalar.wait_ge(S["kev"], tc_ + 1)
                for gg in range(2):
                    for hswap in range(2):
                        so, do_ = (32, 0) if hswap == 0 else (0, 32)
                        nc.scalar.dma_start(
                            Ksh[gg * 64 + do_:gg * 64 + do_ + 32, :],
                            Kstg[gg * 64 + so:gg * 64 + so + 32, :],
                        ).then_inc(*S.inc("sh", 16))
                nc.vector.wait_ge(S["mmKV"], (tc_ + 1) * 16)
                if tc_ == 0:
                    nc.vector.wait_ge(S["pre"], 96)
                tt_last = None
                for tt in range(4):
                    kt_g = tc_ * 4 + tt
                    dst = bass.AP(Vp, kt_g * 65,
                                  [[HKV * NKT * 65, 128], [2080, HKV], [1, 64]])
                    src = bass.AP(psV[tt], 0, [[512, 128], [64, HKV], [1, 64]])
                    tt_last = nc.vector.tensor_tensor(dst, src, bvb[:], ALU.add)
                tt_last.then_inc(*S.inc("vev"))
                nc.vector.wait_ge(S["sh"], 64 * (tc_ + 1))
                nc.vector.wait_ge(S[f"tab{tc_ % 2}"], S.cnt[f"tab{tc_ % 2}"])
                for m in range(4):
                    gm = 4 * tc_ + m
                    if gm >= 2:
                        nc.vector.wait_ge(S["ropea"], gm - 1)
                    nc.vector.tensor_tensor(
                        kt1[m % 2][:], Ksh[:, m * 512:(m + 1) * 512],
                        sinb[tc_ % 2][:], ALU.mult)
                    nc.vector.tensor_tensor(
                        kt2[m % 2][:], Kstg[:, m * 512:(m + 1) * 512],
                        cosb[tc_ % 2][:], ALU.mult
                    ).then_inc(*S.inc("ropev"))
                last = None
                for m in range(4):
                    gm = 4 * tc_ + m
                    nc.gpsimd.wait_ge(S["ropev"], gm + 1)
                    last = nc.gpsimd.tensor_tensor(
                        KT[:, m * T + tc_ * 512:m * T + (tc_ + 1) * 512],
                        kt1[m % 2][:], kt2[m % 2][:], ALU.add
                    ).then_inc(*S.inc("ropea"))

            def tab_load(tc_):
                bi = tc_ % 2
                if tc_ >= 2:
                    nc.gpsimd.wait_ge(S["ropev"], 4 * (tc_ - 1))
                nc.gpsimd.dma_start(cosb[bi][:],
                                    d_cosK[:, tc_ * 512:(tc_ + 1) * 512]
                                    ).then_inc(*S.inc(f"tab{bi}", 16))
                nc.gpsimd.dma_start(sinb[bi][:],
                                    d_sinK[:, tc_ * 512:(tc_ + 1) * 512]
                                    ).then_inc(*S.inc(f"tab{bi}", 16))

            def xq_load(dq):
                nc.gpsimd.dma_start(xq_bf[:, dq * TOK:(dq + 1) * TOK],
                                    d_xqT.ap()[dq * 128:(dq + 1) * 128, :]
                                    ).then_inc(*S.inc("xqc", 16))

            tab_load(0)
            tab_load(1)
            for k in range(4):
                kv_load(k)
            xq_load(0)
            xq_load(1)
            for g in range(NTC * 16):
                tc, d = divmod(g, 16)
                if g + 4 < NTC * 16:
                    kv_load(g + 4)
                kv_mm(g)
                if d == 7 and tc < 8:
                    if 2 * tc + 2 < 16:
                        xq_load(2 * tc + 2)
                if d == 15:
                    kv_evac_rope(tc)
                    if tc + 2 < NTC:
                        tab_load(tc + 2)
                    if 2 * tc + 3 < 16:
                        xq_load(2 * tc + 3)
            nc.all_engine_barrier()
            if debug == "pa1":
                o1 = dbg("dbg_KT", [128, 4 * T], BF16)
                o2 = dbg("dbg_Vp", [128, HKV * NKT * 65], BF16)
                nc.sync.dma_start(o1[:], KT[:]).then_inc(*S.inc("outd", 16))
                nc.sync.dma_start(o2[:], Vp[:]).then_inc(*S.inc("outd", 16))
                nc.sync.wait_ge(S["outd"], S.cnt["outd"])
                nc.all_engine_barrier()

        q_stack = contextlib.ExitStack()
        QT2b = q_stack.enter_context(
            nc.sbuf_tensor("QT2b", [128, 16 * 512], BF16))
        QT2a = q_stack.enter_context(
            nc.sbuf_tensor("QT2a", [128, 16 * 512], BF16))
        QT2ab = [QT2a, QT2b]
        with contextlib.ExitStack() as pa2:
            cosQ = pa2.enter_context(nc.sbuf_tensor("s_cosQ", [128, TOK], F32))
            sinQ = pa2.enter_context(nc.sbuf_tensor("s_sinQ", [128, TOK], F32))
            wqs_bf = [pa2.enter_context(
                nc.sbuf_tensor(f"wqs_bf{i}", [128, 2048], BF16)) for i in range(2)]
            Qstg = [pa2.enter_context(
                nc.sbuf_tensor(f"Qstg{i}", [128, TOK], F32)) for i in range(2)]
            Qsh = [pa2.enter_context(
                nc.sbuf_tensor(f"Qsh{i}", [128, TOK], F32)) for i in range(2)]
            qt1 = [pa2.enter_context(nc.sbuf_tensor(f"qt1{i}", [128, TOK], F32))
                   for i in range(2)]
            qt2 = [pa2.enter_context(nc.sbuf_tensor(f"qt2{i}", [128, TOK], F32))
                   for i in range(2)]
            psQ = [pa2.enter_context(nc.psum_tensor(f"psQ{i}", [128, 512], F32))
                   for i in range(8)]

            nc.scalar.dma_start(cosQ[:], d_cosQ[:]).then_inc(*S.inc("tabQ", 16))
            nc.scalar.dma_start(sinQ[:], d_sinQ[:]).then_inc(*S.inc("tabQ", 16))
            mq_base = S.cnt["mmQ"]

            def q_load(j):
                bi = j % 2
                if j >= 2:
                    nc.gpsimd.wait_ge(S["mmQ"], mq_base + j - 1)
                for half in range(2):
                    src = bass.AP(d_wqT, half * 8 * 128 * D + j * 128,
                                  [[D, 128], [128 * D, 8], [1, 128]])
                    dst = bass.AP(wqs_bf[bi], half * 1024,
                                  [[2048, 128], [128, 8], [1, 128]])
                    nc.gpsimd.dma_start(dst, src).then_inc(*S.inc(f"wq{bi}", 16))

            def q_comp(j):
                bi = j % 2
                sbi = j % 2
                nc.tensor.wait_ge(S[f"wq{bi}"], 32 * (j // 2 + 1))
                if j == 0:
                    nc.tensor.wait_ge(S["xqc"], 256)
                pA, pB = psQ[(j % 4) * 2], psQ[(j % 4) * 2 + 1]
                if j >= 4:
                    nc.tensor.wait_ge(S["qev"], j - 3)
                mmq = None
                for d in range(16):
                    nc.tensor.matmul(pA[:],
                                     wqs_bf[bi][:, d * 128:(d + 1) * 128],
                                     xq_bf[:, d * TOK:d * TOK + 512],
                                     start=(d == 0), stop=(d == 15))
                    mmq = nc.tensor.matmul(
                        pB[:], wqs_bf[bi][:, d * 128:(d + 1) * 128],
                        xq_bf[:, d * TOK + 512:(d + 1) * TOK],
                        start=(d == 0), stop=(d == 15))
                mmq.then_inc(*S.inc("mmQ"))
                nc.scalar.wait_ge(S["mmQ"], mq_base + j + 1)
                if j == 0:
                    nc.scalar.wait_ge(S["pre"], 96)
                if j >= 2:
                    nc.scalar.wait_ge(S["qrope"], j - 1)
                nc.scalar.activation(Qstg[sbi][:, 0:512], pA[:], AF.Identity,
                                     bias=bqt[:, j:j + 1], scale=1.0)
                nc.scalar.activation(Qstg[sbi][:, 512:1024], pB[:],
                                     AF.Identity, bias=bqt[:, j:j + 1],
                                     scale=1.0).then_inc(*S.inc("qev"))
                nc.scalar.wait_ge(S["qev"], j + 1)
                if j >= 1:
                    nc.scalar.wait_ge(S["qrope"], j)
                for gg in range(2):
                    for hswap in range(2):
                        so, do_ = (32, 0) if hswap == 0 else (0, 32)
                        nc.scalar.dma_start(
                            Qsh[sbi][gg * 64 + do_:gg * 64 + do_ + 32, :],
                            Qstg[sbi][gg * 64 + so:gg * 64 + so + 32, :],
                        ).then_inc(*S.inc("qsh", 16))
                nc.vector.wait_ge(S["qsh"], S.cnt["qsh"])
                if j == 0:
                    nc.vector.wait_ge(S["tabQ"], 32)
                if j >= 2:
                    nc.vector.wait_ge(S["qrope"], j - 1)
                nc.vector.tensor_tensor(qt1[sbi][:], Qsh[sbi][:], sinQ[:],
                                        ALU.mult)
                nc.vector.tensor_tensor(qt2[sbi][:], Qstg[sbi][:], cosQ[:],
                                        ALU.mult).then_inc(*S.inc("qropev"))
                nc.gpsimd.wait_ge(S["qropev"], j + 1)
                last = None
                for H, rsrc in ((2 * j, 0), (2 * j + 1, 64)):
                    blk = 4 * (H // 8) + (H % 4)
                    r0 = 0 if (H % 8) < 4 else 64
                    for qb in range(2):
                        last = nc.gpsimd.tensor_tensor(
                            QT2ab[qb][r0:r0 + 64, blk * 512:(blk + 1) * 512],
                            qt1[sbi][rsrc:rsrc + 64, qb * 512:(qb + 1) * 512],
                            qt2[sbi][rsrc:rsrc + 64, qb * 512:(qb + 1) * 512],
                            ALU.add)
                last.then_inc(*S.inc("qrope"))

            q_load(0)
            q_load(1)
            for j in range(16):
                if j + 2 < 16:
                    q_load(j + 2)
                q_comp(j)
            nc.all_engine_barrier()
            if debug == "pa2":
                o1 = dbg("dbg_QT2", [128, 16 * 512], BF16)
                nc.sync.dma_start(o1[:], QT2a[:]).then_inc(*S.inc("outd", 16))
                nc.sync.wait_ge(S["outd"], S.cnt["outd"])
                nc.all_engine_barrier()

        xq_right.close()
        ds_stack = contextlib.ExitStack()
        hdnT_h0 = ds_stack.enter_context(
            nc.sbuf_tensor("hdnT0", [128, NIT * 512], BF16, side="right"))
        attnT_ab = [None, None]
        attnT_ab[1] = ds_stack.enter_context(
            nc.sbuf_tensor("attnTb", [128, 16 * 512], BF16, side="right"))
        attn_a_stack = contextlib.ExitStack()
        attnT_ab[0] = attn_a_stack.enter_context(
            nc.sbuf_tensor("attnTa", [128, 16 * 512], BF16, side="right"))
        psds_stack = contextlib.ExitStack()
        psDS = [psds_stack.enter_context(
            nc.psum_tensor(f"psDS{i}", [128, 512], F32)) for i in range(2)]
        with contextlib.ExitStack() as psx:
            Sps = [psx.enter_context(nc.psum_tensor(f"Sps{i}", [128, 1024], F32))
                   for i in range(2)]
            Aps = [psx.enter_context(nc.psum_tensor(f"Aps{i}", [65, 512], F32))
                   for i in range(2)]
            wT = [psx.enter_context(nc.sbuf_tensor(f"wT{i}", [128, 1024], BF16))
                  for i in range(2)]
            zrec = psx.enter_context(nc.sbuf_tensor("zrec", [1, 2048], F32))
            bcs = psx.enter_context(nc.sbuf_tensor("bcs", [128, 1024], F32))

            seq = [(qb, m, i) for qb in range(2) for m in range(4)
                   for i in range(4)]

            def ps_S(itx, kt):
                qb, m, i = seq[itx]
                blk = 4 * m + i
                QTx = QT2ab[qb]
                sbi = kt % 2
                wbi = kt % 2
                g = itx * NKT + kt
                if g >= 2:
                    nc.tensor.wait_ge(S["psE"], g - 1)
                nc.tensor.matmul(
                    Sps[sbi][:, 0:512],
                    KT[0:64, m * T + kt * 128:m * T + (kt + 1) * 128],
                    QTx[0:64, blk * 512:(blk + 1) * 512],
                    start=True, stop=True, tile_position=(0, 0))
                nc.tensor.matmul(
                    Sps[sbi][:, 512:1024],
                    KT[64:128, m * T + kt * 128:m * T + (kt + 1) * 128],
                    QTx[64:128, blk * 512:(blk + 1) * 512],
                    start=True, stop=True, tile_position=(64, 0)
                ).then_inc(*S.inc("psS"))
                nc.scalar.wait_ge(S["psS"], g + 1)
                if g >= 2:
                    nc.scalar.wait_ge(S["psA"], g - 1)
                nc.scalar.activation(
                    wT[wbi][:], Sps[sbi][:], AF.Exp,
                    bias=cneg[:], scale=0.125).then_inc(*S.inc("psE"))

            def ps_AV(itx, kt):
                qb, m, i = seq[itx]
                kve, kvo = 2 * m, 2 * m + 1
                A0, A1 = Aps[0], Aps[1]
                wbi = kt % 2
                g = itx * NKT + kt
                if kt == 0 and itx >= 1:
                    nc.tensor.wait_ge(S["aev"], itx)
                nc.tensor.wait_ge(S["psE"], g + 1)
                nc.tensor.matmul(
                    A0[:],
                    Vp[:, kve * 2080 + kt * 65:kve * 2080 + kt * 65 + 65],
                    wT[wbi][:, 0:512],
                    start=(kt == 0), stop=(kt == NKT - 1))
                nc.tensor.matmul(
                    A1[:],
                    Vp[:, kvo * 2080 + kt * 65:kvo * 2080 + kt * 65 + 65],
                    wT[wbi][:, 512:1024],
                    start=(kt == 0), stop=(kt == NKT - 1)
                ).then_inc(*S.inc("psA"))

            def ps_epilogue(itx):
                qb, m, i = seq[itx]
                He, Ho = 8 * m + i, 8 * m + 4 + i
                A0, A1 = Aps[0], Aps[1]
                attnT = attnT_ab[qb]
                zrow = itx % 2
                nc.vector.wait_ge(S["psA"], (itx + 1) * NKT)
                if itx >= 2:
                    nc.vector.wait_ge(S["zb"], 16 * (itx - 1))
                zo = zrow * 1024
                nc.vector.reciprocal(zrec[0:1, zo:zo + 512], A0[64:65, :])
                nc.vector.reciprocal(zrec[0:1, zo + 512:zo + 1024],
                                     A1[64:65, :]).then_inc(*S.inc("zr"))
                nc.gpsimd.wait_ge(S["zr"], S.cnt["zr"])
                nc.gpsimd.dma_start(
                    bass.AP(d_zb, zrow * 1024, [[1024, 1], [1, 1024]]),
                    zrec[0:1, zo:zo + 1024]).then_inc(*S.inc("zb", 16))
                nc.gpsimd.wait_ge(S["zb"], S.cnt["zb"])
                if itx >= 1:
                    nc.gpsimd.wait_ge(S["aev"], itx)
                nc.gpsimd.dma_start(
                    bcs[:],
                    bass.AP(d_zb, zrow * 1024, [[0, 128], [1, 1024]]),
                ).then_inc(*S.inc("bc", 16))
                nc.vector.wait_ge(S["bc"], S.cnt["bc"])
                lastev = None
                for H, A_, co in ((He, A0, 0), (Ho, A1, 512)):
                    bj = H // 2
                    r0 = (H % 2) * 64
                    lastev = nc.vector.tensor_tensor(
                        attnT[r0:r0 + 64, bj * 512:(bj + 1) * 512],
                        A_[0:64, :], bcs[0:64, co:co + 512], ALU.mult)
                lastev.then_inc(*S.inc("aev"))

            GPS = len(seq) * NKT
            ps_S(0, 0)
            ps_S(0, 1)
            for g in range(GPS):
                itx, kt = divmod(g, NKT)
                ps_AV(itx, kt)
                if g + 2 < GPS:
                    ps_S((g + 2) // NKT, (g + 2) % NKT)
                if kt == NKT - 1:
                    ps_epilogue(itx)
            nc.all_engine_barrier()
            if debug == "ps":
                o1 = dbg("dbg_attnTa", [128, 16 * 512], BF16)
                o2 = dbg("dbg_attnTb", [128, 16 * 512], BF16)
                nc.sync.dma_start(o1[:], attnT_ab[0][:]).then_inc(
                    *S.inc("outd", 16))
                nc.sync.dma_start(o2[:], attnT_ab[1][:]).then_inc(
                    *S.inc("outd", 16))
                nc.sync.wait_ge(S["outd"], S.cnt["outd"])
                nc.all_engine_barrier()

        q_stack.close()
        kv_stack.close()

        def ds_half(h, attnT, yT_h, hdnT_h):
            b_womm = S.cnt["wo_mm"]
            b_woev = S.cnt["wo_ev"]
            b_ln1 = S.cnt["ln1"]
            b_tmm = S.cnt["tmm"]
            b_tev = S.cnt["tev"]
            b_gumm = S.cnt["gu_mm"]
            b_guev = S.cnt["gu_ev"]
            b_sdv = S.cnt["sdv"]
            b_hdn = S.cnt["hdn"]
            b_wo = [S.cnt["wo0"], S.cnt["wo1"]]
            b_wx = [S.cnt["wx0"], S.cnt["wx1"]]
            b_f1g = [S.cnt["f1g0"], S.cnt["f1g1"]]
            b_f1u = [S.cnt["f1u0"], S.cnt["f1u1"]]

            def wo_load(u):
                bi = u % 2
                pi, j = divmod(u, 16)
                p, ds_ = divmod(pi, 4)
                if u >= 2:
                    nc.gpsimd.wait_ge(S["wo_mm"], b_womm + 2 * (u - 1))
                nc.gpsimd.dma_start(
                    wo_bb[bi][:],
                    d_woT[j * 128:(j + 1) * 128, ds_ * 512:(ds_ + 1) * 512]
                ).then_inc(*S.inc(f"wo{bi}", 16))

            def wo_mms(u):
                bi = u % 2
                pi, j = divmod(u, 16)
                p, ds_ = divmod(pi, 4)
                nc.tensor.wait_ge(S[f"wo{bi}"], b_wo[bi] + 16 * (u // 2 + 1))
                if j == 0 and pi >= 1:
                    nc.tensor.wait_ge(S["wo_ev"], b_woev + 2 * pi)
                for iq in range(2):
                    qt = 2 * p + iq
                    nc.tensor.matmul(
                        psDS[iq][:],
                        attnT[:, j * 512 + qt * 128:j * 512 + (qt + 1) * 128],
                        wo_bb[bi][:],
                        start=(j == 0), stop=(j == 15)
                    ).then_inc(*S.inc("wo_mm"))

            def wo_xr(pi):
                p, ds_ = divmod(pi, 4)
                if pi >= 1:
                    nc.sync.wait_ge(S["wo_ev"], b_woev + 2 * pi)
                for iq in range(2):
                    qt_g = h * 4 + 2 * p + iq
                    nc.sync.dma_start(
                        xr_s[iq][:],
                        d_xres[qt_g * 128:(qt_g + 1) * 128,
                               ds_ * 512:(ds_ + 1) * 512]
                    ).then_inc(*S.inc(f"wx{iq}", 16))

            def wo_evac(pi):
                p, ds_ = divmod(pi, 4)
                nc.vector.wait_ge(S["wo_mm"], b_womm + 32 * (pi + 1))
                nc.vector.wait_ge(S["wx0"], b_wx[0] + 16 * (pi + 1))
                nc.vector.wait_ge(S["wx1"], b_wx[1] + 16 * (pi + 1))
                for iq in range(2):
                    qt = 2 * p + iq
                    nc.vector.tensor_tensor(wtmp[:], psDS[iq][:], xr_s[iq][:],
                                            ALU.add)
                    nc.vector.drain()
                    nc.vector.tensor_tensor(
                        ytile[:, qt * 2048 + ds_ * 512:
                              qt * 2048 + (ds_ + 1) * 512],
                        wtmp[:], bob[:, ds_ * 512:(ds_ + 1) * 512], ALU.add
                    ).then_inc(*S.inc("wo_ev"))

            def ln1_stats():
                junk = hdnT_h[:, 0:2048]
                nc.scalar.wait_ge(S["wo_ev"], b_woev + 16)
                last = None
                for qt in range(4):
                    seg = ytile[:, qt * 2048:(qt + 1) * 2048]
                    last = nc.scalar.activation(
                        junk, seg, AF.Square,
                        accum_out=lt2[:, 8 + qt:9 + qt])
                    nc.vector.tensor_reduce(lt2[:, qt:qt + 1], seg,
                                            mybir.AxisListType.X, ALU.add)
                last.then_inc(*S.inc("ln1"))
                nc.vector.drain().then_inc(*S.inc("ln1"))

            def ln1_chain():
                nc.vector.wait_ge(S["ln1"], b_ln1 + 2)
                nc.vector.tensor_scalar_mul(lt2[:, 16:20], lt2[:, 0:4],
                                            -1.0 / D)
                nc.vector.tensor_scalar_mul(lt2[:, 24:28], lt2[:, 8:12],
                                            1.0 / D)
                nc.vector.drain()
                nc.vector.tensor_tensor(lt2[:, 32:36], lt2[:, 16:20],
                                        lt2[:, 16:20], ALU.mult)
                nc.vector.drain()
                nc.vector.tensor_tensor(lt2[:, 40:44], lt2[:, 24:28],
                                        lt2[:, 32:36], ALU.subtract)
                nc.vector.drain()
                nc.vector.tensor_scalar_add(lt2[:, 48:52], lt2[:, 40:44], EPS
                                            ).then_inc(*S.inc("ln1"))
                nc.scalar.wait_ge(S["ln1"], b_ln1 + 3)
                nc.scalar.activation(lt2[:, 40:44], lt2[:, 48:52], AF.Ln)
                nc.scalar.drain()
                nc.scalar.activation(lt2[:, 48:52], lt2[:, 40:44], AF.Exp,
                                     scale=-0.5).then_inc(*S.inc("ln1"))
                nc.vector.wait_ge(S["ln1"], b_ln1 + 4)
                nc.vector.tensor_tensor(lt2[:, 56:60], lt2[:, 16:20],
                                        lt2[:, 48:52], ALU.mult
                                        ).then_inc(*S.inc("ln1"))

            def ln1_norm(qt):
                seg = ytile[:, qt * 2048:(qt + 1) * 2048]
                if qt == 0:
                    nc.scalar.wait_ge(S["ln1"], b_ln1 + 5)
                nc.scalar.activation(seg, seg, AF.Identity,
                                     bias=lt2[:, 56 + qt:57 + qt],
                                     scale=lt2[:, 48 + qt:49 + qt]
                                     ).then_inc(*S.inc("ln1"))
                qt_g = h * 4 + qt
                nc.sync.wait_ge(S["ln1"], S.cnt["ln1"])
                nc.sync.dma_start(d_ydr.ap()[qt_g * 128:(qt_g + 1) * 128, :],
                                  seg).then_inc(*S.inc("ysp0", 16))

            def transp(qt, f):
                fi = qt * 4 + f
                if fi == 0:
                    nc.tensor.wait_ge(S["wo_ev"], b_woev + 16)
                nc.tensor.wait_ge(S["ln1"], b_ln1 + 6 + qt)
                if fi >= 2:
                    nc.tensor.wait_ge(S["tev"], b_tev + fi - 1)
                last = None
                for c in range(4):
                    hc = 4 * f + c
                    last = nc.tensor.transpose(
                        psDS[fi % 2][:, c * 128:(c + 1) * 128],
                        ytile[:, qt * 2048 + hc * 128:
                              qt * 2048 + (hc + 1) * 128],
                        ident[:])
                last.then_inc(*S.inc("tmm"))
                nc.vector.wait_ge(S["tmm"], b_tmm + fi + 1)
                dst = bass.AP(yT_h, (4 * f) * 512 + qt * 128,
                              [[16 * 512, 128], [512, 4], [1, 128]])
                nc.vector.tensor_copy(dst, psDS[fi % 2][:]
                                      ).then_inc(*S.inc("tev"))

            def f1_load(k):
                bi = k % 2
                if k >= 2:
                    nc.gpsimd.wait_ge(S["gu_mm"], b_gumm + 16 * (k - 1))
                for w_, dsrc, bft in (("g", d_gwT, g_bf), ("u", d_uwT, u_bf)):
                    for half in range(2):
                        srcap = bass.AP(dsrc,
                                        half * 8 * 128 * INTER + k * 128,
                                        [[INTER, 128], [128 * INTER, 8],
                                         [1, 128]])
                        dstap = bass.AP(bft[bi], half * 1024,
                                        [[2048, 128], [128, 8], [1, 128]])
                        nc.gpsimd.dma_start(dstap, srcap).then_inc(
                            *S.inc(f"f1{w_}{bi}", 16))

            def f1_mms(k, hc):
                bi = k % 2
                if hc == 0:
                    nc.tensor.wait_ge(S[f"f1g{bi}"],
                                      b_f1g[bi] + 32 * (k // 2 + 1))
                    nc.tensor.wait_ge(S[f"f1u{bi}"],
                                      b_f1u[bi] + 32 * (k // 2 + 1))
                    if k == 0:
                        nc.tensor.wait_ge(S["tev"], b_tev + 16)
                    if k >= 1:
                        nc.tensor.wait_ge(S["gu_ev"], b_guev + 3 * k)
                mm = None
                for pp, bft in ((psDS[0], g_bf), (psDS[1], u_bf)):
                    mm = nc.tensor.matmul(
                        pp[:], bft[bi][:, hc * 128:(hc + 1) * 128],
                        yT_h[:, hc * 512:(hc + 1) * 512],
                        start=(hc == 0), stop=(hc == 15))
                mm.then_inc(*S.inc("gu_mm"))

            def f1_silu(k):
                nc.scalar.wait_ge(S["gu_mm"], b_gumm + 16 * (k + 1))
                if k == 0:
                    nc.scalar.wait_ge(S["pre"], 96)
                    nc.scalar.wait_ge(S["misc"], 1)
                if k >= 1:
                    nc.scalar.wait_ge(S["hdn"], b_hdn + k)
                nc.scalar.activation(se[:], psDS[0][:], AF.Exp,
                                     scale=ngst[:, k:k + 1]
                                     ).then_inc(*S.inc("gu_ev"))
                nc.scalar.activation(sg2[:], psDS[0][:], AF.Identity,
                                     scale=gst[:, k:k + 1]
                                     ).then_inc(*S.inc("gu_ev"))
                nc.scalar.activation(su2[:], psDS[1][:], AF.Identity,
                                     scale=ust[:, k:k + 1]
                                     ).then_inc(*S.inc("gu_ev"))
                nc.vector.wait_ge(S["gu_ev"], b_guev + 3 * (k + 1))
                nc.vector.tensor_scalar_add(sr[:], se[:], 1.0)
                nc.vector.drain()
                nc.vector.reciprocal(se[:], sr[:]).then_inc(*S.inc("sdv"))
                nc.gpsimd.wait_ge(S["sdv"], b_sdv + k + 1)
                nc.gpsimd.tensor_tensor(ptm[:], sg2[:], su2[:], ALU.mult)
                nc.gpsimd.drain()
                nc.gpsimd.tensor_tensor(
                    hdnT_h[:, k * 512:(k + 1) * 512], ptm[:], se[:], ALU.mult
                ).then_inc(*S.inc("hdn"))

            wo_load(0)
            wo_load(1)
            for u in range(128):
                if u + 2 < 128:
                    wo_load(u + 2)
                wo_mms(u)
                pi, j = divmod(u, 16)
                if j == 8:
                    wo_xr(pi)
                if j == 15:
                    wo_evac(pi)
                yield
            ln1_stats()
            yield
            ln1_chain()
            yield
            for qt in range(4):
                ln1_norm(qt)
                yield
            f1_load(0)
            f1_load(1)
            for qt in range(4):
                for f in range(4):
                    transp(qt, f)
                yield
            for k in range(NIT):
                if k + 2 < NIT:
                    f1_load(k + 2)
                for hc in range(16):
                    f1_mms(k, hc)
                    if hc % 4 == 3:
                        yield
                f1_silu(k)
                yield

        with contextlib.ExitStack() as dsw:
            ytile = dsw.enter_context(
                nc.sbuf_tensor("ytile", [128, 4 * 2048], F32))
            wo_bb = [dsw.enter_context(
                nc.sbuf_tensor(f"wo_bb{i}", [128, 512], BF16))
                for i in range(2)]
            xr_s = [dsw.enter_context(
                nc.sbuf_tensor(f"xr_s{i}", [128, 512], F32)) for i in range(2)]
            wtmp = dsw.enter_context(nc.sbuf_tensor("wtmp", [128, 512], F32))
            bob = dsw.enter_context(nc.sbuf_tensor("s_bob", [128, D], F32))
            se = dsw.enter_context(nc.sbuf_tensor("se", [128, 512], F32))
            sr = wtmp
            sg2 = dsw.enter_context(nc.sbuf_tensor("sg2", [128, 512], BF16))
            su2 = dsw.enter_context(nc.sbuf_tensor("su2", [128, 512], BF16))
            ptm = wo_bb[0]
            g_bf = [dsw.enter_context(
                nc.sbuf_tensor(f"g_bf{i}", [128, 2048], BF16))
                for i in range(2)]
            u_bf = [dsw.enter_context(
                nc.sbuf_tensor(f"u_bf{i}", [128, 2048], BF16))
                for i in range(2)]

            nc.scalar.dma_start(bob[:], d_bob[:]).then_inc(*S.inc("bobd", 16))
            nc.scalar.wait_ge(S["bobd"], S.cnt["bobd"])
            nc.all_engine_barrier()

            with nc.sbuf_tensor("yT0", [128, 16 * 512], BF16) as yT_h0:
                for _ in ds_half(0, attnT_ab[0], yT_h0, hdnT_h0):
                    pass
                nc.all_engine_barrier()
                if debug == "wo":
                    o1 = dbg("dbg_yT0", [128, 16 * 512], BF16)
                    nc.sync.dma_start(o1[:], yT_h0[:]).then_inc(
                        *S.inc("outd", 16))
                    nc.sync.wait_ge(S["outd"], S.cnt["outd"])
                    nc.all_engine_barrier()
            attn_a_stack.close()
            hdnT_h1 = ds_stack.enter_context(
                nc.sbuf_tensor("hdnT1", [128, NIT * 512], BF16, side="right"))
            with nc.sbuf_tensor("yT1", [128, 16 * 512], BF16) as yT_h1:
                for _ in ds_half(1, attnT_ab[1], yT_h1, hdnT_h1):
                    pass
                nc.all_engine_barrier()
            if debug == "ffn1":
                o1 = dbg("dbg_hdnT0", [128, NIT * 512], BF16)
                nc.sync.dma_start(o1[:], hdnT_h0[:]).then_inc(
                    *S.inc("outd", 16))
                nc.sync.wait_ge(S["outd"], S.cnt["outd"])
                nc.all_engine_barrier()

        psds_stack.close()

        with contextlib.ExitStack() as f2:
            res2 = f2.enter_context(nc.sbuf_tensor("res2", [128, 8 * D], F32))
            dsb = f2.enter_context(nc.sbuf_tensor("s_dsb", [128, D], F32))
            dw_st = [f2.enter_context(
                nc.sbuf_tensor(f"dw_st{i}", [128, 512], F32)) for i in range(3)]
            dw_bf = [f2.enter_context(
                nc.sbuf_tensor(f"dw_bf{i}", [128, 512], BF16)) for i in range(3)]
            dtmp = [f2.enter_context(
                nc.sbuf_tensor(f"dtmp{i}", [128, 512], F32)) for i in range(2)]
            psD = [f2.enter_context(nc.psum_tensor(f"psD{i}", [128, 512], F32))
                   for i in range(8)]

            nc.scalar.dma_start(dsb[:], d_dsb[:]).then_inc(*S.inc("dsbd", 16))
            for qt in range(8):
                nc.scalar.dma_start(res2[:, qt * D:(qt + 1) * D],
                                    d_ydr.ap()[qt * 128:(qt + 1) * 128, :]
                                    ).then_inc(*S.inc("dsbd", 16))
            nc.scalar.wait_ge(S["dsbd"], S.cnt["dsbd"])
            nc.all_engine_barrier()

            f2c_base = S.cnt["f2c"]
            f2m_base = S.cnt["f2mm"]

            def f2_load(g):
                bi = g % 3
                hs_, k_ = divmod(g, NIT)
                if g >= 3:
                    nc.sync.wait_ge(S["f2c"], f2c_base + g - 2)
                nc.sync.dma_start(
                    dw_st[bi][:],
                    d_dwT[k_ * 128:(k_ + 1) * 128,
                          hs_ * 512:(hs_ + 1) * 512]
                ).then_inc(*S.inc(f"f2w{bi}", 16))

            def f2_cast(g):
                bi = g % 3
                nc.vector.wait_ge(S[f"f2w{bi}"], S.cnt[f"f2w{bi}"])
                if g >= 3:
                    nc.vector.wait_ge(S["f2mm"], f2m_base + g - 2)
                nc.vector.tensor_copy(dw_bf[bi][:], dw_st[bi][:]
                                      ).then_inc(*S.inc("f2c"))

            def f2_comp(g):
                bi = g % 3
                hs_, k_ = divmod(g, NIT)
                nc.tensor.wait_ge(S["f2c"], f2c_base + g + 1)
                if k_ == 0 and hs_ > 0:
                    nc.tensor.wait_ge(S["f2ev"], hs_)
                lastm = None
                for tt in range(8):
                    hsrc = hdnT_h0 if tt < 4 else hdnT_h1
                    to = (tt % 4) * 128
                    lastm = nc.tensor.matmul(
                        psD[tt][:],
                        hsrc[:, k_ * 512 + to:k_ * 512 + to + 128],
                        dw_bf[bi][:],
                        start=(k_ == 0), stop=(k_ == NIT - 1))
                lastm.then_inc(*S.inc("f2mm"))

            def f2_evac(hs_):
                nc.vector.wait_ge(S["f2mm"], f2m_base + (hs_ + 1) * NIT)
                laste = None
                for tt in range(8):
                    ti = tt % 2
                    nc.vector.tensor_tensor(dtmp[ti][:], psD[tt][:],
                                            dsb[:, hs_ * 512:(hs_ + 1) * 512],
                                            ALU.mult)
                    nc.vector.drain()
                    seg = res2[:, tt * D + hs_ * 512:tt * D + (hs_ + 1) * 512]
                    laste = nc.vector.tensor_tensor(seg, seg, dtmp[ti][:],
                                                    ALU.add)
                    nc.vector.drain()
                laste.then_inc(*S.inc("f2ev"))

            NG = 4 * NIT
            for g0 in range(3):
                f2_load(g0)
                f2_cast(g0)
            for g in range(NG):
                if g + 3 < NG:
                    f2_load(g + 3)
                    f2_cast(g + 3)
                f2_comp(g)
                if g % NIT == NIT - 1:
                    f2_evac(g // NIT)
            nc.all_engine_barrier()

            qstat = None
            for qt in range(8):
                rseg = res2[:, qt * D:(qt + 1) * D]
                qstat = nc.scalar.activation(dsb[:], rseg, AF.Square,
                                             accum_out=lt2[:, 8 + qt:9 + qt])
                nc.vector.tensor_reduce(lt2[:, qt:qt + 1], rseg,
                                        mybir.AxisListType.X, ALU.add)
            qstat.then_inc(*S.inc("ln2"))
            nc.vector.drain()
            nc.vector.wait_ge(S["ln2"], S.cnt["ln2"])
            nc.vector.tensor_scalar_mul(lt2[:, 16:24], lt2[:, 0:8], -1.0 / D)
            nc.vector.tensor_scalar_mul(lt2[:, 24:32], lt2[:, 8:16], 1.0 / D)
            nc.vector.drain()
            nc.vector.tensor_tensor(lt2[:, 32:40], lt2[:, 16:24],
                                    lt2[:, 16:24], ALU.mult)
            nc.vector.drain()
            nc.vector.tensor_tensor(lt2[:, 40:48], lt2[:, 24:32],
                                    lt2[:, 32:40], ALU.subtract)
            nc.vector.drain()
            nc.vector.tensor_scalar_add(lt2[:, 48:56], lt2[:, 40:48], EPS
                                        ).then_inc(*S.inc("ln2"))
            nc.scalar.wait_ge(S["ln2"], S.cnt["ln2"])
            nc.scalar.activation(lt2[:, 40:48], lt2[:, 48:56], AF.Ln)
            nc.scalar.drain()
            nc.scalar.activation(lt2[:, 48:56], lt2[:, 40:48], AF.Exp,
                                 scale=-0.5).then_inc(*S.inc("ln2"))
            nc.vector.wait_ge(S["ln2"], S.cnt["ln2"])
            nc.vector.tensor_tensor(lt2[:, 56:64], lt2[:, 16:24],
                                    lt2[:, 48:56], ALU.mult
                                    ).then_inc(*S.inc("ln2"))
            nc.scalar.wait_ge(S["ln2"], S.cnt["ln2"])
            for qt in range(8):
                rseg = res2[:, qt * D:(qt + 1) * D]
                nc.scalar.activation(rseg, rseg, AF.Identity,
                                     bias=lt2[:, 56 + qt:57 + qt],
                                     scale=lt2[:, 48 + qt:49 + qt]
                                     ).then_inc(*S.inc("ln2"))
                nc.sync.wait_ge(S["ln2"], S.cnt["ln2"])
                nc.sync.dma_start(d_out[qt * 128:(qt + 1) * 128, :], rseg
                                  ).then_inc(*S.inc("outd", 16))
            nc.sync.wait_ge(S["outd"], S.cnt["outd"])

        ds_stack.close()

    return nc, sorted(dbg_outs)


def _rope_tables(t0, n):
    inv = 10000.0 ** (-(np.arange(32, dtype=np.float64) * 2) / 64.0)
    p = np.arange(128)
    t = np.arange(t0, t0 + n, dtype=np.float64)
    ang = t[None, :] * inv[p % 32][:, None]
    cos = np.cos(ang).astype(np.float32)
    sin = np.sin(ang).astype(np.float32)
    sgn = np.where((p % 64) < 32, -1.0, 1.0).astype(np.float32)
    return cos, np.ascontiguousarray(sin * sgn[:, None]).astype(np.float32)


def _prep_inputs(inp):
    c = np.ascontiguousarray

    def _blk(wT, nk):
        a = wT.reshape(2, 8, 128, nk, 128)
        return c(a.transpose(3, 2, 0, 1, 4).reshape(nk * 128, 2048))

    x = np.asarray(inp["L0_attn_x"], dtype=np.float32)
    cosK, sinK = _rope_tables(0, T)
    base = {
        "wqT": c(np.asarray(inp["L0_attn_Wq"]).T),
        "wkT": c(np.asarray(inp["L0_attn_Wk"]).T),
        "wvT": c(np.asarray(inp["L0_attn_Wv"]).T),
        "woT": c(np.asarray(inp["L0_attn_Wo"]).T),
        "bqt": c(np.asarray(inp["L0_attn_bq"]).reshape(16, 128).T),
        "bkt": c(np.asarray(inp["L0_attn_bk"]).reshape(4, 128).T),
        "bvb": c(np.broadcast_to(np.asarray(inp["L0_attn_bv"])[None, :],
                                 (128, 512))),
        "bob": c(np.broadcast_to(np.asarray(inp["L0_attn_bo"])[None, :],
                                 (128, D))),
        "gwT": c(np.asarray(inp["L0_ffn_gate_w"]).T),
        "uwT": c(np.asarray(inp["L0_ffn_up_w"]).T),
        "dwT": c(np.asarray(inp["L0_ffn_down_w"]).T),
        "gst": c(np.asarray(inp["L0_ffn_gate_s"]).reshape(NIT, 128).T),
        "ust": c(np.asarray(inp["L0_ffn_up_s"]).reshape(NIT, 128).T),
        "dsb": c(np.broadcast_to(np.asarray(inp["L0_ffn_down_s"])[None, :],
                                 (128, D))),
        "cosK": cosK, "sinK": sinK,
        "identf": np.eye(128, dtype=np.float32),
    }
    xT = [c(x[b].T) for b in range(B)]
    maps = []
    for core in range(N_CORES):
        b = core // 4
        q0 = (core % 4) * TOK
        cosQ, sinQ = _rope_tables(q0, TOK)
        m = dict(base)
        m["xT"] = xT[b]
        m["xqT"] = c(xT[b][:, q0:q0 + TOK])
        m["xres"] = c(x[b, q0:q0 + TOK])
        m["cosQ"] = cosQ
        m["sinQ"] = sinQ
        maps.append(m)
    return maps


def kernel(**inputs):
    try:
        from concourse.compiler_utils import (get_compiler_flags,
                                              set_compiler_flags)
        set_compiler_flags([
            f.replace("--enable-ldw-opt=false", "--enable-ldw-opt=true")
            for f in get_compiler_flags()])
    except Exception:
        pass
    maps = _prep_inputs(inputs)
    nc, _ = build_program()
    res = run_bass_kernel_spmd(nc, maps, core_ids=list(range(N_CORES)))
    out = np.empty((B, T, D), np.float32)
    for core in range(N_CORES):
        b = core // 4
        q0 = (core % 4) * TOK
        out[b, q0:q0 + TOK] = np.asarray(res.results[core]["out"])
    return out


# version 18
# speedup vs baseline: 1.0680x; 1.0680x over previous
import contextlib
import numpy as np
import ml_dtypes
import concourse.bass as bass
import concourse.mybir as mybir
from concourse.bass_utils import run_bass_kernel_spmd

F32 = mybir.dt.float32
BF16 = mybir.dt.bfloat16
AF = mybir.ActivationFunctionType
ALU = mybir.AluOpType

B, T, D = 2, 4096, 2048
HQ, HKV, HD = 32, 8, 64
TOK = 1024
INTER = 5632
NIT = INTER // 128
NTC = T // 512
NKT = T // 128
C_SHIFT = 10.0
EPS = 1e-5
N_CORES = 8


class Sems:
    def __init__(self, nc, es):
        self.nc = nc
        self.es = es
        self.cnt = {}
        self.h = {}

    def new(self, *names):
        for n in names:
            self.h[n] = self.es.enter_context(self.nc.semaphore(n))
            self.cnt[n] = 0

    def inc(self, name, n=1):
        self.cnt[name] += n
        return self.h[name], n

    def __getitem__(self, name):
        return self.h[name]


def build_program(debug=None):
    nc = bass.Bass("TRN2", target_bir_lowering=False)

    d_xT = nc.dram_tensor("xT", [D, T], F32, kind="ExternalInput")
    d_xqT = nc.dram_tensor("xqT", [D, TOK], F32, kind="ExternalInput")
    d_xres = nc.dram_tensor("xres", [TOK, D], F32, kind="ExternalInput").ap()
    d_wqT = nc.dram_tensor("wqT", [D, D], F32, kind="ExternalInput")
    d_wkT = nc.dram_tensor("wkT", [D, 512], F32, kind="ExternalInput")
    d_wvT = nc.dram_tensor("wvT", [D, 512], F32, kind="ExternalInput")
    d_woT = nc.dram_tensor("woT", [D, D], F32, kind="ExternalInput").ap()
    d_bqt = nc.dram_tensor("bqt", [128, 16], F32, kind="ExternalInput").ap()
    d_bkt = nc.dram_tensor("bkt", [128, 4], F32, kind="ExternalInput").ap()
    d_bvb = nc.dram_tensor("bvb", [128, 512], F32, kind="ExternalInput").ap()
    d_bob = nc.dram_tensor("bob", [128, D], F32, kind="ExternalInput").ap()
    d_gwT = nc.dram_tensor("gwT", [D, INTER], F32, kind="ExternalInput")
    d_uwT = nc.dram_tensor("uwT", [D, INTER], F32, kind="ExternalInput")
    d_dwT = nc.dram_tensor("dwT", [INTER, D], F32, kind="ExternalInput").ap()
    d_gst = nc.dram_tensor("gst", [128, NIT], F32, kind="ExternalInput").ap()
    d_ust = nc.dram_tensor("ust", [128, NIT], F32, kind="ExternalInput").ap()
    d_dsb = nc.dram_tensor("dsb", [128, D], F32, kind="ExternalInput").ap()
    d_cosK = nc.dram_tensor("cosK", [128, T], F32, kind="ExternalInput").ap()
    d_sinK = nc.dram_tensor("sinK", [128, T], F32, kind="ExternalInput").ap()
    d_cosQ = nc.dram_tensor("cosQ", [128, TOK], F32, kind="ExternalInput").ap()
    d_sinQ = nc.dram_tensor("sinQ", [128, TOK], F32, kind="ExternalInput").ap()
    d_idf = nc.dram_tensor("identf", [128, 128], F32, kind="ExternalInput").ap()
    d_out = nc.dram_tensor("out", [TOK, D], F32, kind="ExternalOutput").ap()
    d_zb = nc.dram_tensor("zb", [64, 1024], F32)
    d_xrb = nc.dram_tensor("xrb", [TOK, D], F32)
    d_ydr = nc.dram_tensor("ydr", [TOK, D], F32)

    dbg_outs = {}

    def dbg(name, shape, dt):
        dbg_outs[name] = nc.dram_tensor(name, shape, dt,
                                        kind="ExternalOutput").ap()
        return dbg_outs[name]

    es = contextlib.ExitStack()
    S = Sems(nc, es)
    S.new("x", "xb", "mmKV", "kev", "vev", "sh", "rope",
          "wld", "wcast", "mmQ", "qev", "qsh", "qrope",
          "psS", "psE", "psA", "zr", "zbd", "bc", "aev",
          "wo_x", "wo_mm", "wo_ev", "ln1", "ysp", "f1c", "f1mm", "f1ev",
          "f1w", "f1wc", "gu_mm", "gu_ev", "hdn",
          "f2w", "f2c", "f2mm", "f2ev", "ln2", "outd", "misc",
          "x0", "x1", "x2", "x3", "wq0", "wq1", "wox0", "wox1", "f1w0", "f1w1",
          "f2w0", "f2w1", "f2w2", "zb0", "zb1", "bc0", "bc1", "ysp0", "ysp1",
          "pre", "tabK", "tabQ", "bobd", "dsbd", "wo0", "wo1",
          "ropev", "qropev", "xq0", "xq1", "xqc", "ropea", "tab0", "tab1",
          "zb", "wx0", "wx1", "tmm", "tev",
          "f1g0", "f1g1", "f1u0", "f1u1", "sdv")

    with es:
        bvb = es.enter_context(nc.sbuf_tensor("s_bvb", [128, 512], F32))
        bkt = es.enter_context(nc.sbuf_tensor("s_bkt", [128, 4], F32))
        bqt = es.enter_context(nc.sbuf_tensor("s_bqt", [128, 16], F32))
        gst = es.enter_context(nc.sbuf_tensor("s_gst", [128, NIT], F32))
        ust = es.enter_context(nc.sbuf_tensor("s_ust", [128, NIT], F32))
        cneg = es.enter_context(nc.sbuf_tensor("cneg", [128, 1], F32))
        epsp = es.enter_context(nc.sbuf_tensor("epsp", [128, 1], F32))
        ident = es.enter_context(nc.sbuf_tensor("ident", [128, 128], F32))
        lt2 = es.enter_context(nc.sbuf_tensor("lt2", [128, 64], F32))
        ngst = es.enter_context(nc.sbuf_tensor("ngst", [128, NIT], F32))

        for ap_d, t_s in ((d_bvb, bvb), (d_bkt, bkt), (d_bqt, bqt),
                          (d_gst, gst), (d_ust, ust), (d_idf, ident)):
            nc.scalar.dma_start(t_s[:], ap_d[:]).then_inc(*S.inc("pre", 16))
        nc.vector.memset(cneg[:], -C_SHIFT)
        nc.vector.memset(epsp[:], EPS)
        nc.vector.wait_ge(S["pre"], 96)
        nc.vector.tensor_scalar_mul(ngst[:], gst[:], -1.0
                                    ).then_inc(*S.inc("misc"))

        kv_stack = contextlib.ExitStack()
        xq_right = contextlib.ExitStack()
        xq_bf = xq_right.enter_context(
            nc.sbuf_tensor("xq_bf", [128, 16 * TOK], BF16, side="right"))
        KT = kv_stack.enter_context(nc.sbuf_tensor("KT", [128, 4 * T], BF16))
        Vp = kv_stack.enter_context(
            nc.sbuf_tensor("Vp", [128, HKV * NKT * 65], BF16))
        nc.vector.memset(
            bass.AP(Vp, 64, [[HKV * NKT * 65, 128], [65, HKV * NKT]]), 1.0)

        with contextlib.ExitStack() as pa1:
            cosb = [pa1.enter_context(
                nc.sbuf_tensor(f"cosb{i}", [128, 512], F32)) for i in range(2)]
            sinb = [pa1.enter_context(
                nc.sbuf_tensor(f"sinb{i}", [128, 512], F32)) for i in range(2)]
            wk_bf = pa1.enter_context(nc.sbuf_tensor("wk_bf", [128, 8192], BF16))
            wv_bf = pa1.enter_context(nc.sbuf_tensor("wv_bf", [128, 8192], BF16))
            Kstg = pa1.enter_context(nc.sbuf_tensor("Kstg", [128, 2048], F32))
            Ksh = pa1.enter_context(nc.sbuf_tensor("Ksh", [128, 2048], F32))
            xbb = [pa1.enter_context(nc.sbuf_tensor(f"xbb{i}", [128, 512], BF16))
                   for i in range(4)]
            kt1 = [pa1.enter_context(nc.sbuf_tensor(f"kt1{i}", [128, 512], F32))
                   for i in range(2)]
            kt2 = [pa1.enter_context(nc.sbuf_tensor(f"kt2{i}", [128, 512], F32))
                   for i in range(2)]
            psK = [pa1.enter_context(nc.psum_tensor(f"psK{i}", [128, 512], F32))
                   for i in range(4)]
            psV = [pa1.enter_context(nc.psum_tensor(f"psV{i}", [128, 512], F32))
                   for i in range(4)]


            for dsrc, bfdst in ((d_wkT, wk_bf), (d_wvT, wv_bf)):
                srcap = bass.AP(dsrc, 0, [[512, 128], [128 * 512, 16], [1, 512]])
                dstap = bass.AP(bfdst, 0, [[8192, 128], [512, 16], [1, 512]])
                nc.gpsimd.dma_start(dstap, srcap).then_inc(*S.inc("pre", 16))

            def kv_load(k):
                bi = k % 4
                tc_, d_ = divmod(k, 16)
                if k >= 4:
                    nc.gpsimd.wait_ge(S["mmKV"], k - 3)
                nc.gpsimd.dma_start(
                    xbb[bi][:],
                    d_xT.ap()[d_ * 128:(d_ + 1) * 128,
                              tc_ * 512:(tc_ + 1) * 512],
                ).then_inc(*S.inc(f"x{bi}", 16))

            def kv_mm(k):
                bi = k % 4
                tc_, d_ = divmod(k, 16)
                nc.tensor.wait_ge(S[f"x{bi}"], 16 * (k // 4 + 1))
                if k == 0:
                    nc.tensor.wait_ge(S["pre"], S.cnt["pre"])
                if tc_ > 0 and d_ == 0:
                    nc.tensor.wait_ge(S["kev"], tc_)
                    nc.tensor.wait_ge(S["vev"], tc_)
                for m in range(4):
                    nc.tensor.matmul(
                        psK[m][:],
                        wk_bf[:, d_ * 512 + m * 128:d_ * 512 + (m + 1) * 128],
                        xbb[bi][:], start=(d_ == 0), stop=(d_ == 15))
                mm = None
                for tt in range(4):
                    mm = nc.tensor.matmul(
                        psV[tt][:], xbb[bi][:, tt * 128:(tt + 1) * 128],
                        wv_bf[:, d_ * 512:(d_ + 1) * 512],
                        start=(d_ == 0), stop=(d_ == 15))
                mm.then_inc(*S.inc("mmKV"))

            def kv_evac_rope(tc_):
                nc.scalar.wait_ge(S["mmKV"], (tc_ + 1) * 16)
                if tc_ == 0:
                    nc.scalar.wait_ge(S["pre"], 96)
                else:
                    nc.scalar.wait_ge(S["ropev"], 4 * tc_)
                    nc.scalar.wait_ge(S["sh"], 64 * tc_)
                act = None
                for m in range(4):
                    act = nc.scalar.activation(
                        Kstg[:, m * 512:(m + 1) * 512], psK[m][:], AF.Identity,
                        bias=bkt[:, m:m + 1], scale=1.0)
                act.then_inc(*S.inc("kev"))
                nc.scalar.wait_ge(S["kev"], tc_ + 1)
                for gg in range(2):
                    for hswap in range(2):
                        so, do_ = (32, 0) if hswap == 0 else (0, 32)
                        nc.scalar.dma_start(
                            Ksh[gg * 64 + do_:gg * 64 + do_ + 32, :],
                            Kstg[gg * 64 + so:gg * 64 + so + 32, :],
                        ).then_inc(*S.inc("sh", 16))
                nc.vector.wait_ge(S["mmKV"], (tc_ + 1) * 16)
                if tc_ == 0:
                    nc.vector.wait_ge(S["pre"], 96)
                tt_last = None
                for tt in range(4):
                    kt_g = tc_ * 4 + tt
                    dst = bass.AP(Vp, kt_g * 65,
                                  [[HKV * NKT * 65, 128], [2080, HKV], [1, 64]])
                    src = bass.AP(psV[tt], 0, [[512, 128], [64, HKV], [1, 64]])
                    tt_last = nc.vector.tensor_tensor(dst, src, bvb[:], ALU.add)
                tt_last.then_inc(*S.inc("vev"))
                nc.vector.wait_ge(S["sh"], 64 * (tc_ + 1))
                nc.vector.wait_ge(S[f"tab{tc_ % 2}"], S.cnt[f"tab{tc_ % 2}"])
                for m in range(4):
                    gm = 4 * tc_ + m
                    if gm >= 2:
                        nc.vector.wait_ge(S["ropea"], gm - 1)
                    nc.vector.tensor_tensor(
                        kt1[m % 2][:], Ksh[:, m * 512:(m + 1) * 512],
                        sinb[tc_ % 2][:], ALU.mult)
                    nc.vector.tensor_tensor(
                        kt2[m % 2][:], Kstg[:, m * 512:(m + 1) * 512],
                        cosb[tc_ % 2][:], ALU.mult
                    ).then_inc(*S.inc("ropev"))
                last = None
                for m in range(4):
                    gm = 4 * tc_ + m
                    nc.gpsimd.wait_ge(S["ropev"], gm + 1)
                    last = nc.gpsimd.tensor_tensor(
                        KT[:, m * T + tc_ * 512:m * T + (tc_ + 1) * 512],
                        kt1[m % 2][:], kt2[m % 2][:], ALU.add
                    ).then_inc(*S.inc("ropea"))

            def tab_load(tc_):
                bi = tc_ % 2
                if tc_ >= 2:
                    nc.gpsimd.wait_ge(S["ropev"], 4 * (tc_ - 1))
                nc.gpsimd.dma_start(cosb[bi][:],
                                    d_cosK[:, tc_ * 512:(tc_ + 1) * 512]
                                    ).then_inc(*S.inc(f"tab{bi}", 16))
                nc.gpsimd.dma_start(sinb[bi][:],
                                    d_sinK[:, tc_ * 512:(tc_ + 1) * 512]
                                    ).then_inc(*S.inc(f"tab{bi}", 16))

            def xq_load(dq):
                nc.gpsimd.dma_start(xq_bf[:, dq * TOK:(dq + 1) * TOK],
                                    d_xqT.ap()[dq * 128:(dq + 1) * 128, :]
                                    ).then_inc(*S.inc("xqc", 16))

            tab_load(0)
            tab_load(1)
            for k in range(4):
                kv_load(k)
            xq_load(0)
            xq_load(1)
            for g in range(NTC * 16):
                tc, d = divmod(g, 16)
                if g + 4 < NTC * 16:
                    kv_load(g + 4)
                kv_mm(g)
                if d == 7 and tc < 8:
                    if 2 * tc + 2 < 16:
                        xq_load(2 * tc + 2)
                if d == 15:
                    kv_evac_rope(tc)
                    if tc + 2 < NTC:
                        tab_load(tc + 2)
                    if 2 * tc + 3 < 16:
                        xq_load(2 * tc + 3)
            nc.all_engine_barrier()
            if debug == "pa1":
                o1 = dbg("dbg_KT", [128, 4 * T], BF16)
                o2 = dbg("dbg_Vp", [128, HKV * NKT * 65], BF16)
                nc.sync.dma_start(o1[:], KT[:]).then_inc(*S.inc("outd", 16))
                nc.sync.dma_start(o2[:], Vp[:]).then_inc(*S.inc("outd", 16))
                nc.sync.wait_ge(S["outd"], S.cnt["outd"])
                nc.all_engine_barrier()

        ps_tiles = contextlib.ExitStack()
        wT = [ps_tiles.enter_context(
            nc.sbuf_tensor(f"wT{i}", [128, 1024], BF16)) for i in range(2)]
        zrec = ps_tiles.enter_context(nc.sbuf_tensor("zrec", [128, 1024], F32))
        bcs = ps_tiles.enter_context(nc.sbuf_tensor("bcs", [128, 1024], F32))
        qb_stack = contextlib.ExitStack()
        QT2b = qb_stack.enter_context(
            nc.sbuf_tensor("QT2b", [128, 16 * 512], BF16))
        qa_stack = contextlib.ExitStack()
        QT2a = qa_stack.enter_context(
            nc.sbuf_tensor("QT2a", [128, 16 * 512], BF16))
        QT2ab = [QT2a, QT2b]
        with contextlib.ExitStack() as pa2:
            cosQ = pa2.enter_context(nc.sbuf_tensor("s_cosQ", [128, TOK], F32))
            sinQ = pa2.enter_context(nc.sbuf_tensor("s_sinQ", [128, TOK], F32))
            wqs_bf = [pa2.enter_context(
                nc.sbuf_tensor(f"wqs_bf{i}", [128, 2048], BF16)) for i in range(2)]
            Qstg = [pa2.enter_context(
                nc.sbuf_tensor(f"Qstg{i}", [128, TOK], F32)) for i in range(2)]
            Qsh = [pa2.enter_context(
                nc.sbuf_tensor(f"Qsh{i}", [128, TOK], F32)) for i in range(2)]
            qt1 = [pa2.enter_context(nc.sbuf_tensor(f"qt1{i}", [128, TOK], F32))
                   for i in range(2)]
            qt2 = [pa2.enter_context(nc.sbuf_tensor(f"qt2{i}", [128, TOK], F32))
                   for i in range(2)]
            psQ = [pa2.enter_context(nc.psum_tensor(f"psQ{i}", [128, 512], F32))
                   for i in range(8)]

            nc.scalar.dma_start(cosQ[:], d_cosQ[:]).then_inc(*S.inc("tabQ", 16))
            nc.scalar.dma_start(sinQ[:], d_sinQ[:]).then_inc(*S.inc("tabQ", 16))
            mq_base = S.cnt["mmQ"]

            def q_load(j):
                bi = j % 2
                if j >= 2:
                    nc.gpsimd.wait_ge(S["mmQ"], mq_base + j - 1)
                for half in range(2):
                    src = bass.AP(d_wqT, half * 8 * 128 * D + j * 128,
                                  [[D, 128], [128 * D, 8], [1, 128]])
                    dst = bass.AP(wqs_bf[bi], half * 1024,
                                  [[2048, 128], [128, 8], [1, 128]])
                    nc.gpsimd.dma_start(dst, src).then_inc(*S.inc(f"wq{bi}", 16))

            def q_comp(j):
                bi = j % 2
                sbi = j % 2
                nc.tensor.wait_ge(S[f"wq{bi}"], 32 * (j // 2 + 1))
                if j == 0:
                    nc.tensor.wait_ge(S["xqc"], 256)
                pA, pB = psQ[(j % 4) * 2], psQ[(j % 4) * 2 + 1]
                if j >= 4:
                    nc.tensor.wait_ge(S["qev"], j - 3)
                mmq = None
                for d in range(16):
                    nc.tensor.matmul(pA[:],
                                     wqs_bf[bi][:, d * 128:(d + 1) * 128],
                                     xq_bf[:, d * TOK:d * TOK + 512],
                                     start=(d == 0), stop=(d == 15))
                    mmq = nc.tensor.matmul(
                        pB[:], wqs_bf[bi][:, d * 128:(d + 1) * 128],
                        xq_bf[:, d * TOK + 512:(d + 1) * TOK],
                        start=(d == 0), stop=(d == 15))
                mmq.then_inc(*S.inc("mmQ"))
                nc.scalar.wait_ge(S["mmQ"], mq_base + j + 1)
                if j == 0:
                    nc.scalar.wait_ge(S["pre"], 96)
                if j >= 2:
                    nc.scalar.wait_ge(S["qrope"], j - 1)
                nc.scalar.activation(Qstg[sbi][:, 0:512], pA[:], AF.Identity,
                                     bias=bqt[:, j:j + 1], scale=1.0)
                nc.scalar.activation(Qstg[sbi][:, 512:1024], pB[:],
                                     AF.Identity, bias=bqt[:, j:j + 1],
                                     scale=1.0).then_inc(*S.inc("qev"))
                nc.scalar.wait_ge(S["qev"], j + 1)
                if j >= 1:
                    nc.scalar.wait_ge(S["qrope"], j)
                for gg in range(2):
                    for hswap in range(2):
                        so, do_ = (32, 0) if hswap == 0 else (0, 32)
                        nc.scalar.dma_start(
                            Qsh[sbi][gg * 64 + do_:gg * 64 + do_ + 32, :],
                            Qstg[sbi][gg * 64 + so:gg * 64 + so + 32, :],
                        ).then_inc(*S.inc("qsh", 16))
                nc.vector.wait_ge(S["qsh"], S.cnt["qsh"])
                if j == 0:
                    nc.vector.wait_ge(S["tabQ"], 32)
                if j >= 2:
                    nc.vector.wait_ge(S["qrope"], j - 1)
                nc.vector.tensor_tensor(qt1[sbi][:], Qsh[sbi][:], sinQ[:],
                                        ALU.mult)
                nc.vector.tensor_tensor(qt2[sbi][:], Qstg[sbi][:], cosQ[:],
                                        ALU.mult).then_inc(*S.inc("qropev"))
                nc.gpsimd.wait_ge(S["qropev"], j + 1)
                last = None
                for H, rsrc in ((2 * j, 0), (2 * j + 1, 64)):
                    blk = 4 * (H // 8) + (H % 4)
                    r0 = 0 if (H % 8) < 4 else 64
                    for qb in range(2):
                        last = nc.gpsimd.tensor_tensor(
                            QT2ab[qb][r0:r0 + 64, blk * 512:(blk + 1) * 512],
                            qt1[sbi][rsrc:rsrc + 64, qb * 512:(qb + 1) * 512],
                            qt2[sbi][rsrc:rsrc + 64, qb * 512:(qb + 1) * 512],
                            ALU.add)
                last.then_inc(*S.inc("qrope"))

            q_load(0)
            q_load(1)
            for j in range(16):
                if j + 2 < 16:
                    q_load(j + 2)
                q_comp(j)
            nc.all_engine_barrier()
            if debug == "pa2":
                o1 = dbg("dbg_QT2", [128, 16 * 512], BF16)
                nc.sync.dma_start(o1[:], QT2a[:]).then_inc(*S.inc("outd", 16))
                nc.sync.wait_ge(S["outd"], S.cnt["outd"])
                nc.all_engine_barrier()

        xq_right.close()
        ds_stack = contextlib.ExitStack()
        attnT_ab = [None, None]
        attnT_ab[1] = ds_stack.enter_context(
            nc.sbuf_tensor("attnTb", [128, 16 * 512], BF16, side="right"))
        attn_a_stack = contextlib.ExitStack()
        attnT_ab[0] = attn_a_stack.enter_context(
            nc.sbuf_tensor("attnTa", [128, 16 * 512], BF16, side="right"))
        psds_stack = contextlib.ExitStack()
        psDS = [psds_stack.enter_context(
            nc.psum_tensor(f"psDS{i}", [128, 512], F32)) for i in range(2)]
        def ds_half(h, attnT, post_wo, half_stack, guard):
            sub1b = contextlib.ExitStack()
            ytile = sub1b.enter_context(
                nc.sbuf_tensor(f"ytile{h}", [128, 4 * 2048], F32))
            sub1a = contextlib.ExitStack()
            wo_bb = [sub1a.enter_context(
                nc.sbuf_tensor(f"wo_bb{h}{i}", [128, 512], BF16))
                for i in range(2)]
            xr_s = [sub1a.enter_context(
                nc.sbuf_tensor(f"xr_s{h}{i}", [128, 512], F32))
                for i in range(4)]
            hdnT_h = yT_h = None
            g_bf = u_bf = None
            se = sr = sg2 = su2 = ptm = None
            b_womm = S.cnt["wo_mm"]
            b_woev = S.cnt["wo_ev"]
            b_ln1 = S.cnt["ln1"]
            b_tmm = S.cnt["tmm"]
            b_tev = S.cnt["tev"]
            b_gumm = S.cnt["gu_mm"]
            b_guev = S.cnt["gu_ev"]
            b_sdv = S.cnt["sdv"]
            b_hdn = S.cnt["hdn"]
            b_aev = S.cnt["aev"]
            b_ysp = S.cnt["ysp0"]
            b_wo = [S.cnt["wo0"], S.cnt["wo1"]]
            b_wx = [S.cnt["wx0"], S.cnt["wx1"]]
            b_f1g = [S.cnt["f1g0"], S.cnt["f1g1"]]
            b_f1u = [S.cnt["f1u0"], S.cnt["f1u1"]]
            b_f1wc = S.cnt["f1wc"]

            def wo_load(u):
                bi = u % 2
                pi, j = divmod(u, 16)
                p, ds_ = divmod(pi, 4)
                if u == 0:
                    nc.gpsimd.wait_ge(S["zbd"], S.cnt["zbd"])
                if u >= 2:
                    nc.gpsimd.wait_ge(S["wo_mm"], b_womm + 2 * (u - 1))
                nc.gpsimd.dma_start(
                    wo_bb[bi][:],
                    d_woT[j * 128:(j + 1) * 128, ds_ * 512:(ds_ + 1) * 512]
                ).then_inc(*S.inc(f"wo{bi}", 16))

            def wo_mms(u):
                bi = u % 2
                pi, j = divmod(u, 16)
                p, ds_ = divmod(pi, 4)
                if u == 0:
                    nc.tensor.wait_ge(S["aev"], b_aev)
                nc.tensor.wait_ge(S[f"wo{bi}"], b_wo[bi] + 16 * (u // 2 + 1))
                if j == 0 and pi >= 1:
                    nc.tensor.wait_ge(S["wo_ev"], b_woev + 2 * pi)
                for iq in range(2):
                    qt = 2 * p + iq
                    nc.tensor.matmul(
                        psDS[iq][:],
                        attnT[:, j * 512 + qt * 128:j * 512 + (qt + 1) * 128],
                        wo_bb[bi][:],
                        start=(j == 0), stop=(j == 15)
                    ).then_inc(*S.inc("wo_mm"))

            def wo_xr(pi):
                p, ds_ = divmod(pi, 4)
                if pi >= 2:
                    nc.sync.wait_ge(S["wo_ev"], b_woev + 2 * (pi - 1))
                for iq in range(2):
                    qt_g = h * 4 + 2 * p + iq
                    nc.sync.dma_start(
                        xr_s[2 * (pi % 2) + iq][:],
                        d_xrb.ap()[qt_g * 128:(qt_g + 1) * 128,
                                   ds_ * 512:(ds_ + 1) * 512]
                    ).then_inc(*S.inc(f"wx{iq}", 16))

            def wo_evac(pi):
                p, ds_ = divmod(pi, 4)
                nc.vector.wait_ge(S["wo_mm"], b_womm + 32 * (pi + 1))
                nc.vector.wait_ge(S["wx0"], b_wx[0] + 16 * (pi + 1))
                nc.vector.wait_ge(S["wx1"], b_wx[1] + 16 * (pi + 1))
                for iq in range(2):
                    qt = 2 * p + iq
                    nc.vector.tensor_tensor(
                        ytile[:, qt * 2048 + ds_ * 512:
                              qt * 2048 + (ds_ + 1) * 512],
                        psDS[iq][:], xr_s[2 * (pi % 2) + iq][:], ALU.add
                    ).then_inc(*S.inc("wo_ev"))

            def ln1_stats():
                junk = hdnT_h[:, 0:2048]
                nc.scalar.wait_ge(S["wo_ev"], b_woev + 16)
                last = None
                for qt in range(4):
                    seg = ytile[:, qt * 2048:(qt + 1) * 2048]
                    last = nc.scalar.activation(
                        junk, seg, AF.Square,
                        accum_out=lt2[:, 8 + qt:9 + qt])
                    nc.vector.tensor_reduce(lt2[:, qt:qt + 1], seg,
                                            mybir.AxisListType.X, ALU.add)
                last.then_inc(*S.inc("ln1"))
                nc.vector.drain().then_inc(*S.inc("ln1"))

            def ln1_chain():
                nc.vector.wait_ge(S["ln1"], b_ln1 + 2)
                nc.vector.tensor_scalar_mul(lt2[:, 16:20], lt2[:, 0:4],
                                            -1.0 / D)
                nc.vector.tensor_scalar_mul(lt2[:, 24:28], lt2[:, 8:12],
                                            1.0 / D)
                nc.vector.drain()
                nc.vector.tensor_tensor(lt2[:, 32:36], lt2[:, 16:20],
                                        lt2[:, 16:20], ALU.mult)
                nc.vector.drain()
                nc.vector.tensor_tensor(lt2[:, 40:44], lt2[:, 24:28],
                                        lt2[:, 32:36], ALU.subtract)
                nc.vector.drain()
                nc.vector.tensor_scalar_add(lt2[:, 48:52], lt2[:, 40:44], EPS
                                            ).then_inc(*S.inc("ln1"))
                nc.scalar.wait_ge(S["ln1"], b_ln1 + 3)
                nc.scalar.activation(lt2[:, 40:44], lt2[:, 48:52], AF.Ln)
                nc.scalar.drain()
                nc.scalar.activation(lt2[:, 48:52], lt2[:, 40:44], AF.Exp,
                                     scale=-0.5).then_inc(*S.inc("ln1"))
                nc.vector.wait_ge(S["ln1"], b_ln1 + 4)
                nc.vector.tensor_tensor(lt2[:, 56:60], lt2[:, 16:20],
                                        lt2[:, 48:52], ALU.mult
                                        ).then_inc(*S.inc("ln1"))

            def ln1_norm(qt):
                seg = ytile[:, qt * 2048:(qt + 1) * 2048]
                if qt == 0:
                    nc.scalar.wait_ge(S["ln1"], b_ln1 + 5)
                nc.scalar.activation(seg, seg, AF.Identity,
                                     bias=lt2[:, 56 + qt:57 + qt],
                                     scale=lt2[:, 48 + qt:49 + qt]
                                     ).then_inc(*S.inc("ln1"))
                qt_g = h * 4 + qt
                nc.sync.wait_ge(S["ln1"], S.cnt["ln1"])
                nc.sync.dma_start(d_ydr.ap()[qt_g * 128:(qt_g + 1) * 128, :],
                                  seg).then_inc(*S.inc("ysp0", 16))

            def transp(qt, f):
                fi = qt * 4 + f
                if fi == 0:
                    nc.tensor.wait_ge(S["wo_ev"], b_woev + 16)
                nc.tensor.wait_ge(S["ln1"], b_ln1 + 6 + qt)
                if fi >= 2:
                    nc.tensor.wait_ge(S["tev"], b_tev + fi - 1)
                last = None
                for c in range(4):
                    hc = 4 * f + c
                    last = nc.tensor.transpose(
                        psDS[fi % 2][:, c * 128:(c + 1) * 128],
                        ytile[:, qt * 2048 + hc * 128:
                              qt * 2048 + (hc + 1) * 128],
                        ident[:])
                last.then_inc(*S.inc("tmm"))
                nc.vector.wait_ge(S["tmm"], b_tmm + fi + 1)
                dst = bass.AP(yT_h, (4 * f) * 512 + qt * 128,
                              [[16 * 512, 128], [512, 4], [1, 128]])
                nc.vector.tensor_copy(dst, psDS[fi % 2][:]
                                      ).then_inc(*S.inc("tev"))

            def f1_load(k):
                bi = k % 2
                if k == 0:
                    nc.gpsimd.wait_ge(S["tmm"], b_tmm + 16)
                    nc.gpsimd.wait_ge(S["ysp0"], b_ysp + 64)
                if k >= 2:
                    nc.gpsimd.wait_ge(S["gu_mm"], b_gumm + 16 * (k - 1))
                for w_, dsrc, bft in (("g", d_gwT, g_bf), ("u", d_uwT, u_bf)):
                    for half in range(2):
                        srcap = bass.AP(dsrc,
                                        half * 8 * 128 * INTER + k * 128,
                                        [[INTER, 128], [128 * INTER, 8],
                                         [1, 128]])
                        dstap = bass.AP(bft[bi], half * 1024,
                                        [[2048, 128], [128, 8], [1, 128]])
                        nc.gpsimd.dma_start(dstap, srcap).then_inc(
                            *S.inc(f"f1{w_}{bi}", 16))

            def f1_load_sync(k):
                bi = k % 2
                if k >= 2:
                    nc.sync.wait_ge(S["f1wc"], b_f1wc + 2 * (k - 1))
                for w_, dsrc, st_ in (("g", d_gwT, g_st), ("u", d_uwT, u_st)):
                    for half in range(2):
                        srcap = bass.AP(dsrc,
                                        half * 8 * 128 * INTER + k * 128,
                                        [[INTER, 128], [128 * INTER, 8],
                                         [1, 128]])
                        dstap = bass.AP(st_[bi], half * 1024,
                                        [[2048, 128], [128, 8], [1, 128]])
                        nc.sync.dma_start(dstap, srcap).then_inc(
                            *S.inc(f"f1{w_}{bi}", 16))

            def f1_cast(k):
                bi = k % 2
                nc.vector.wait_ge(S[f"f1g{bi}"], b_f1g[bi] + 32 * (k // 2 + 1))
                nc.vector.wait_ge(S[f"f1u{bi}"], b_f1u[bi] + 32 * (k // 2 + 1))
                if k >= 2:
                    nc.vector.wait_ge(S["gu_mm"], b_gumm + 16 * (k - 1))
                nc.vector.tensor_copy(g_bf[bi][:], g_st[bi][:])
                nc.vector.tensor_copy(u_bf[bi][:], u_st[bi][:]
                                      ).then_inc(*S.inc("f1wc", 2))

            def f1_mms(k, hc):
                bi = k % 2
                if hc == 0:
                    if h == 0:
                        nc.tensor.wait_ge(S[f"f1g{bi}"],
                                          b_f1g[bi] + 32 * (k // 2 + 1))
                        nc.tensor.wait_ge(S[f"f1u{bi}"],
                                          b_f1u[bi] + 32 * (k // 2 + 1))
                    else:
                        nc.tensor.wait_ge(S["f1wc"], b_f1wc + 2 * (k + 1))
                    if k == 0:
                        nc.tensor.wait_ge(S["tev"], b_tev + 16)
                    if k >= 1:
                        nc.tensor.wait_ge(S["gu_ev"], b_guev + 3 * k)
                mm = None
                for pp, bft in ((psDS[0], g_bf), (psDS[1], u_bf)):
                    mm = nc.tensor.matmul(
                        pp[:], bft[bi][:, hc * 128:(hc + 1) * 128],
                        yT_h[:, hc * 512:(hc + 1) * 512],
                        start=(hc == 0), stop=(hc == 15))
                mm.then_inc(*S.inc("gu_mm"))

            def f1_silu(k):
                nc.scalar.wait_ge(S["gu_mm"], b_gumm + 16 * (k + 1))
                if k == 0:
                    nc.scalar.wait_ge(S["pre"], 96)
                    nc.scalar.wait_ge(S["misc"], 1)
                    nc.scalar.wait_ge(S["ysp0"], b_ysp + 64)
                if k >= 1:
                    nc.scalar.wait_ge(S["hdn"], b_hdn + k)
                nc.scalar.activation(se[:], psDS[0][:], AF.Exp,
                                     scale=ngst[:, k:k + 1]
                                     ).then_inc(*S.inc("gu_ev"))
                nc.scalar.activation(sg2[:], psDS[0][:], AF.Identity,
                                     scale=gst[:, k:k + 1]
                                     ).then_inc(*S.inc("gu_ev"))
                nc.scalar.activation(su2[:], psDS[1][:], AF.Identity,
                                     scale=ust[:, k:k + 1]
                                     ).then_inc(*S.inc("gu_ev"))
                nc.vector.wait_ge(S["gu_ev"], b_guev + 3 * (k + 1))
                nc.vector.tensor_scalar_add(sr[:], se[:], 1.0)
                nc.vector.tensor_tensor(ptm[:], sg2[:], su2[:], ALU.mult)
                nc.vector.drain()
                nc.vector.reciprocal(se[:], sr[:])
                nc.vector.drain()
                nc.vector.tensor_tensor(
                    hdnT_h[:, k * 512:(k + 1) * 512], ptm[:], se[:], ALU.mult
                ).then_inc(*S.inc("hdn"))

            if guard is not None:
                nc.gpsimd.wait_ge(S[guard[0]], guard[1])
                nc.sync.wait_ge(S[guard[0]], guard[1])
            wo_load(0)
            wo_load(1)
            for u in range(128):
                if u + 2 < 128:
                    wo_load(u + 2)
                wo_mms(u)
                pi, j = divmod(u, 16)
                if j == 8:
                    wo_xr(pi)
                if j == 15:
                    wo_evac(pi)
                yield
            sub1a.close()
            hdnT_h, yT_h = post_wo()
            ln1_stats()
            yield
            ln1_chain()
            yield
            for qt in range(4):
                ln1_norm(qt)
                yield
            for qt in range(4):
                for f in range(4):
                    transp(qt, f)
                yield
            sub1b.close()
            g_bf = [half_stack.enter_context(
                nc.sbuf_tensor(f"g_bf{h}{i}", [128, 2048], BF16))
                for i in range(2)]
            u_bf = [half_stack.enter_context(
                nc.sbuf_tensor(f"u_bf{h}{i}", [128, 2048], BF16))
                for i in range(2)]
            if h == 1:
                g_st = [half_stack.enter_context(
                    nc.sbuf_tensor(f"g_st{i}", [128, 2048], F32))
                    for i in range(2)]
                u_st = [half_stack.enter_context(
                    nc.sbuf_tensor(f"u_st{i}", [128, 2048], F32))
                    for i in range(2)]
                ld1, cst1 = f1_load_sync, f1_cast
            else:
                ld1 = f1_load
                cst1 = None
            ld1(0)
            if cst1:
                cst1(0)
            ld1(1)
            if cst1:
                cst1(1)
            se = half_stack.enter_context(
                nc.sbuf_tensor(f"se{h}", [128, 512], F32))
            sr = half_stack.enter_context(
                nc.sbuf_tensor(f"sr{h}", [128, 512], F32))
            sg2 = half_stack.enter_context(
                nc.sbuf_tensor(f"sg2{h}", [128, 512], BF16))
            su2 = half_stack.enter_context(
                nc.sbuf_tensor(f"su2{h}", [128, 512], BF16))
            ptm = half_stack.enter_context(
                nc.sbuf_tensor(f"ptm{h}", [128, 512], BF16))
            for k in range(NIT):
                if k + 2 < NIT:
                    ld1(k + 2)
                    if cst1:
                        cst1(k + 2)
                for hc in range(16):
                    f1_mms(k, hc)
                    if hc % 2 == 1:
                        yield
                f1_silu(k)
                yield

        psx = contextlib.ExitStack()
        if True:
            Sps = [psx.enter_context(nc.psum_tensor(f"Sps{i}", [128, 1024], F32))
                   for i in range(2)]
            Aps = [psx.enter_context(nc.psum_tensor(f"Aps{i}", [65, 512], F32))
                   for i in range(2)]

            seq = [(qb, m, i) for qb in range(2) for m in range(4)
                   for i in range(4)]

            def ps_S(itx, kt):
                qb, m, i = seq[itx]
                blk = 4 * m + i
                QTx = QT2ab[qb]
                sbi = kt % 2
                wbi = kt % 2
                g = itx * NKT + kt
                if g >= 2:
                    nc.tensor.wait_ge(S["psE"], g - 1)
                nc.tensor.matmul(
                    Sps[sbi][:, 0:512],
                    KT[0:64, m * T + kt * 128:m * T + (kt + 1) * 128],
                    QTx[0:64, blk * 512:(blk + 1) * 512],
                    start=True, stop=True, tile_position=(0, 0))
                nc.tensor.matmul(
                    Sps[sbi][:, 512:1024],
                    KT[64:128, m * T + kt * 128:m * T + (kt + 1) * 128],
                    QTx[64:128, blk * 512:(blk + 1) * 512],
                    start=True, stop=True, tile_position=(64, 0)
                ).then_inc(*S.inc("psS"))
                nc.scalar.wait_ge(S["psS"], g + 1)
                if g >= 2:
                    nc.scalar.wait_ge(S["psA"], g - 1)
                nc.scalar.activation(
                    wT[wbi][:], Sps[sbi][:], AF.Exp,
                    bias=cneg[:], scale=0.125).then_inc(*S.inc("psE"))

            def ps_AV(itx, kt):
                qb, m, i = seq[itx]
                kve, kvo = 2 * m, 2 * m + 1
                A0, A1 = Aps[0], Aps[1]
                wbi = kt % 2
                g = itx * NKT + kt
                if kt == 0 and itx >= 1:
                    nc.tensor.wait_ge(S["aev"], itx)
                nc.tensor.wait_ge(S["psE"], g + 1)
                nc.tensor.matmul(
                    A0[:],
                    Vp[:, kve * 2080 + kt * 65:kve * 2080 + kt * 65 + 65],
                    wT[wbi][:, 0:512],
                    start=(kt == 0), stop=(kt == NKT - 1))
                nc.tensor.matmul(
                    A1[:],
                    Vp[:, kvo * 2080 + kt * 65:kvo * 2080 + kt * 65 + 65],
                    wT[wbi][:, 512:1024],
                    start=(kt == 0), stop=(kt == NKT - 1)
                ).then_inc(*S.inc("psA"))

            def ps_epilogue(itx):
                qb, m, i = seq[itx]
                He, Ho = 8 * m + i, 8 * m + 4 + i
                A0, A1 = Aps[0], Aps[1]
                attnT = attnT_ab[qb]
                zrow = itx % 2
                nc.vector.wait_ge(S["psA"], (itx + 1) * NKT)
                if itx >= 2:
                    nc.vector.wait_ge(S["zb"], 16 * (itx - 1))
                zr0 = zrow * 64
                nc.vector.reciprocal(zrec[zr0:zr0 + 1, 0:512], A0[64:65, :])
                nc.vector.reciprocal(zrec[zr0:zr0 + 1, 512:1024],
                                     A1[64:65, :]).then_inc(*S.inc("zr"))
                nc.sync.wait_ge(S["zr"], S.cnt["zr"])
                nc.sync.dma_start(
                    bass.AP(d_zb, zrow * 1024, [[1024, 1], [1, 1024]]),
                    zrec[zr0:zr0 + 1, 0:1024]).then_inc(*S.inc("zb", 16))
                nc.sync.wait_ge(S["zb"], S.cnt["zb"])
                if itx >= 1:
                    nc.sync.wait_ge(S["aev"], itx)
                nc.sync.dma_start(
                    bcs[:],
                    bass.AP(d_zb, zrow * 1024, [[0, 128], [1, 1024]]),
                ).then_inc(*S.inc("bc", 16))
                nc.vector.wait_ge(S["bc"], S.cnt["bc"])
                lastev = None
                for H, A_, co in ((He, A0, 0), (Ho, A1, 512)):
                    bj = H // 2
                    r0 = (H % 2) * 64
                    lastev = nc.vector.tensor_tensor(
                        attnT[r0:r0 + 64, bj * 512:(bj + 1) * 512],
                        A_[0:64, :], bcs[0:64, co:co + 512], ALU.mult)
                lastev.then_inc(*S.inc("aev"))

            GPS = len(seq) * NKT
            H1G = 16 * NKT
            NDSU = 560
            half0_stack = contextlib.ExitStack()
            h0st = {}

            def post_wo_h0():
                attn_a_stack.close()
                h0st["hdnT"] = ds_stack.enter_context(nc.sbuf_tensor(
                    "hdnT0", [128, NIT * 512], BF16, side="right"))
                h0st["yT"] = half0_stack.enter_context(nc.sbuf_tensor(
                    "yT0", [128, 16 * 512], BF16, side="right"))
                return h0st["hdnT"], h0st["yT"]

            with contextlib.ExitStack() as xps:
                bobt = xps.enter_context(
                    nc.sbuf_tensor("bobt", [128, D], F32))
                xst = [xps.enter_context(
                    nc.sbuf_tensor(f"xst{i}", [128, D], F32))
                    for i in range(2)]
                xso = [xps.enter_context(
                    nc.sbuf_tensor(f"xso{i}", [128, D], F32))
                    for i in range(2)]
                nc.scalar.dma_start(bobt[:], d_bob[:]).then_inc(
                    *S.inc("bobd", 16))
                for ti in range(8):
                    bi = ti % 2
                    nc.sync.dma_start(
                        xst[bi][:], d_xres[ti * 128:(ti + 1) * 128, :]
                    ).then_inc(*S.inc(f"wox{bi}", 16))
                    nc.vector.wait_ge(S[f"wox{bi}"], S.cnt[f"wox{bi}"])
                    if ti == 0:
                        nc.vector.wait_ge(S["bobd"], S.cnt["bobd"])
                    if ti >= 2:
                        nc.vector.wait_ge(S["zbd"], 16 * (ti - 1))
                    nc.vector.tensor_tensor(xso[bi][:], xst[bi][:], bobt[:],
                                            ALU.add).then_inc(*S.inc("misc"))
                    nc.sync.wait_ge(S["misc"], S.cnt["misc"])
                    nc.sync.dma_start(
                        d_xrb.ap()[ti * 128:(ti + 1) * 128, :], xso[bi][:]
                    ).then_inc(*S.inc("zbd", 16))

            dsg = None
            n_units = 0
            ps_S(0, 0)
            ps_S(0, 1)
            for g in range(GPS):
                itx, kt = divmod(g, NKT)
                if g == H1G:
                    qa_stack.close()
                    dsg = ds_half(0, attnT_ab[0], post_wo_h0, half0_stack,
                                  ("psS", H1G))
                if dsg is not None:
                    gr = g - H1G
                    want = (NDSU * (gr + 1)) // H1G
                    while n_units < want:
                        if next(dsg, "END") == "END":
                            break
                        n_units += 1
                ps_AV(itx, kt)
                if g + 2 < GPS:
                    ps_S((g + 2) // NKT, (g + 2) % NKT)
                if kt == NKT - 1:
                    ps_epilogue(itx)
            if dsg is not None:
                for _ in dsg:
                    pass
            nc.all_engine_barrier()
            if debug == "ps":
                o1 = dbg("dbg_attnTa", [128, 16 * 512], BF16)
                o2 = dbg("dbg_attnTb", [128, 16 * 512], BF16)
                nc.sync.dma_start(o1[:], attnT_ab[0][:]).then_inc(
                    *S.inc("outd", 16))
                nc.sync.dma_start(o2[:], attnT_ab[1][:]).then_inc(
                    *S.inc("outd", 16))
                nc.sync.wait_ge(S["outd"], S.cnt["outd"])
                nc.all_engine_barrier()

        half0_stack.close()
        psx.close()
        qb_stack.close()
        ps_tiles.close()
        kv_stack.close()

        if True:
            with contextlib.ExitStack() as half1_stack:
                h1st = {}

                def post_wo_h1():
                    h1st["hdnT"] = ds_stack.enter_context(nc.sbuf_tensor(
                        "hdnT1", [128, NIT * 512], BF16, side="right"))
                    h1st["yT"] = half1_stack.enter_context(nc.sbuf_tensor(
                        "yT1", [128, 16 * 512], BF16, side="right"))
                    return h1st["hdnT"], h1st["yT"]

                for _ in ds_half(1, attnT_ab[1], post_wo_h1, half1_stack,
                                 None):
                    pass
                nc.all_engine_barrier()
            if debug == "ffn1":
                o1 = dbg("dbg_hdnT0", [128, NIT * 512], BF16)
                nc.sync.dma_start(o1[:], h0st["hdnT"][:]).then_inc(
                    *S.inc("outd", 16))
                nc.sync.wait_ge(S["outd"], S.cnt["outd"])
                nc.all_engine_barrier()

        hdnT_h0 = h0st["hdnT"]
        hdnT_h1 = h1st["hdnT"]
        psds_stack.close()

        with contextlib.ExitStack() as f2:
            res2 = f2.enter_context(nc.sbuf_tensor("res2", [128, 8 * D], F32))
            dsb = f2.enter_context(nc.sbuf_tensor("s_dsb", [128, D], F32))
            dw_bf = [f2.enter_context(
                nc.sbuf_tensor(f"dw_bf{i}", [128, 512], BF16)) for i in range(3)]
            dtmp = [f2.enter_context(
                nc.sbuf_tensor(f"dtmp{i}", [128, 512], F32)) for i in range(2)]
            psD = [f2.enter_context(nc.psum_tensor(f"psD{i}", [128, 512], F32))
                   for i in range(8)]

            nc.scalar.dma_start(dsb[:], d_dsb[:]).then_inc(*S.inc("dsbd", 16))
            for qt in range(8):
                nc.scalar.dma_start(res2[:, qt * D:(qt + 1) * D],
                                    d_ydr.ap()[qt * 128:(qt + 1) * 128, :]
                                    ).then_inc(*S.inc("dsbd", 16))
            nc.scalar.wait_ge(S["dsbd"], S.cnt["dsbd"])
            nc.all_engine_barrier()

            f2c_base = S.cnt["f2c"]
            f2m_base = S.cnt["f2mm"]

            f2w_base = [S.cnt["f2w0"], S.cnt["f2w1"], S.cnt["f2w2"]]

            def f2_load(g):
                bi = g % 3
                hs_, k_ = divmod(g, NIT)
                if g >= 3:
                    nc.gpsimd.wait_ge(S["f2mm"], f2m_base + g - 2)
                nc.gpsimd.dma_start(
                    dw_bf[bi][:],
                    d_dwT[k_ * 128:(k_ + 1) * 128,
                          hs_ * 512:(hs_ + 1) * 512]
                ).then_inc(*S.inc(f"f2w{bi}", 16))

            def f2_comp(g):
                bi = g % 3
                hs_, k_ = divmod(g, NIT)
                nc.tensor.wait_ge(S[f"f2w{bi}"],
                                  f2w_base[bi] + 16 * (g // 3 + 1))
                if k_ == 0 and hs_ > 0:
                    nc.tensor.wait_ge(S["f2ev"], hs_)
                lastm = None
                for tt in range(8):
                    hsrc = hdnT_h0 if tt < 4 else hdnT_h1
                    to = (tt % 4) * 128
                    lastm = nc.tensor.matmul(
                        psD[tt][:],
                        hsrc[:, k_ * 512 + to:k_ * 512 + to + 128],
                        dw_bf[bi][:],
                        start=(k_ == 0), stop=(k_ == NIT - 1))
                lastm.then_inc(*S.inc("f2mm"))

            def f2_evac(hs_):
                nc.vector.wait_ge(S["f2mm"], f2m_base + (hs_ + 1) * NIT)
                laste = None
                for tt in range(8):
                    ti = tt % 2
                    nc.vector.tensor_tensor(dtmp[ti][:], psD[tt][:],
                                            dsb[:, hs_ * 512:(hs_ + 1) * 512],
                                            ALU.mult)
                    nc.vector.drain()
                    seg = res2[:, tt * D + hs_ * 512:tt * D + (hs_ + 1) * 512]
                    laste = nc.vector.tensor_tensor(seg, seg, dtmp[ti][:],
                                                    ALU.add)
                    nc.vector.drain()
                laste.then_inc(*S.inc("f2ev"))

            NG = 4 * NIT
            for g0 in range(3):
                f2_load(g0)
            for g in range(NG):
                if g + 3 < NG:
                    f2_load(g + 3)
                f2_comp(g)
                if g % NIT == NIT - 1:
                    f2_evac(g // NIT)
            nc.all_engine_barrier()

            qstat = None
            for qt in range(8):
                rseg = res2[:, qt * D:(qt + 1) * D]
                qstat = nc.scalar.activation(dsb[:], rseg, AF.Square,
                                             accum_out=lt2[:, 8 + qt:9 + qt])
                nc.vector.tensor_reduce(lt2[:, qt:qt + 1], rseg,
                                        mybir.AxisListType.X, ALU.add)
            qstat.then_inc(*S.inc("ln2"))
            nc.vector.drain()
            nc.vector.wait_ge(S["ln2"], S.cnt["ln2"])
            nc.vector.tensor_scalar_mul(lt2[:, 16:24], lt2[:, 0:8], -1.0 / D)
            nc.vector.tensor_scalar_mul(lt2[:, 24:32], lt2[:, 8:16], 1.0 / D)
            nc.vector.drain()
            nc.vector.tensor_tensor(lt2[:, 32:40], lt2[:, 16:24],
                                    lt2[:, 16:24], ALU.mult)
            nc.vector.drain()
            nc.vector.tensor_tensor(lt2[:, 40:48], lt2[:, 24:32],
                                    lt2[:, 32:40], ALU.subtract)
            nc.vector.drain()
            nc.vector.tensor_scalar_add(lt2[:, 48:56], lt2[:, 40:48], EPS
                                        ).then_inc(*S.inc("ln2"))
            nc.scalar.wait_ge(S["ln2"], S.cnt["ln2"])
            nc.scalar.activation(lt2[:, 40:48], lt2[:, 48:56], AF.Ln)
            nc.scalar.drain()
            nc.scalar.activation(lt2[:, 48:56], lt2[:, 40:48], AF.Exp,
                                 scale=-0.5).then_inc(*S.inc("ln2"))
            nc.vector.wait_ge(S["ln2"], S.cnt["ln2"])
            nc.vector.tensor_tensor(lt2[:, 56:64], lt2[:, 16:24],
                                    lt2[:, 48:56], ALU.mult
                                    ).then_inc(*S.inc("ln2"))
            nc.scalar.wait_ge(S["ln2"], S.cnt["ln2"])
            for qt in range(8):
                rseg = res2[:, qt * D:(qt + 1) * D]
                nc.scalar.activation(rseg, rseg, AF.Identity,
                                     bias=lt2[:, 56 + qt:57 + qt],
                                     scale=lt2[:, 48 + qt:49 + qt]
                                     ).then_inc(*S.inc("ln2"))
                nc.sync.wait_ge(S["ln2"], S.cnt["ln2"])
                nc.sync.dma_start(d_out[qt * 128:(qt + 1) * 128, :], rseg
                                  ).then_inc(*S.inc("outd", 16))
            nc.sync.wait_ge(S["outd"], S.cnt["outd"])

        ds_stack.close()

    return nc, sorted(dbg_outs)


def _rope_tables(t0, n):
    inv = 10000.0 ** (-(np.arange(32, dtype=np.float64) * 2) / 64.0)
    p = np.arange(128)
    t = np.arange(t0, t0 + n, dtype=np.float64)
    ang = t[None, :] * inv[p % 32][:, None]
    cos = np.cos(ang).astype(np.float32)
    sin = np.sin(ang).astype(np.float32)
    sgn = np.where((p % 64) < 32, -1.0, 1.0).astype(np.float32)
    return cos, np.ascontiguousarray(sin * sgn[:, None]).astype(np.float32)


def _prep_inputs(inp):
    c = np.ascontiguousarray

    def _blk(wT, nk):
        a = wT.reshape(2, 8, 128, nk, 128)
        return c(a.transpose(3, 2, 0, 1, 4).reshape(nk * 128, 2048))

    x = np.asarray(inp["L0_attn_x"], dtype=np.float32)
    cosK, sinK = _rope_tables(0, T)
    base = {
        "wqT": c(np.asarray(inp["L0_attn_Wq"]).T),
        "wkT": c(np.asarray(inp["L0_attn_Wk"]).T),
        "wvT": c(np.asarray(inp["L0_attn_Wv"]).T),
        "woT": c(np.asarray(inp["L0_attn_Wo"]).T),
        "bqt": c(np.asarray(inp["L0_attn_bq"]).reshape(16, 128).T),
        "bkt": c(np.asarray(inp["L0_attn_bk"]).reshape(4, 128).T),
        "bvb": c(np.broadcast_to(np.asarray(inp["L0_attn_bv"])[None, :],
                                 (128, 512))),
        "bob": c(np.broadcast_to(np.asarray(inp["L0_attn_bo"])[None, :],
                                 (128, D))),
        "gwT": c(np.asarray(inp["L0_ffn_gate_w"]).T),
        "uwT": c(np.asarray(inp["L0_ffn_up_w"]).T),
        "dwT": c(np.asarray(inp["L0_ffn_down_w"]).T),
        "gst": c(np.asarray(inp["L0_ffn_gate_s"]).reshape(NIT, 128).T),
        "ust": c(np.asarray(inp["L0_ffn_up_s"]).reshape(NIT, 128).T),
        "dsb": c(np.broadcast_to(np.asarray(inp["L0_ffn_down_s"])[None, :],
                                 (128, D))),
        "cosK": cosK, "sinK": sinK,
        "identf": np.eye(128, dtype=np.float32),
    }
    xT = [c(x[b].T) for b in range(B)]
    maps = []
    for core in range(N_CORES):
        b = core // 4
        q0 = (core % 4) * TOK
        cosQ, sinQ = _rope_tables(q0, TOK)
        m = dict(base)
        m["xT"] = xT[b]
        m["xqT"] = c(xT[b][:, q0:q0 + TOK])
        m["xres"] = c(x[b, q0:q0 + TOK])
        m["cosQ"] = cosQ
        m["sinQ"] = sinQ
        maps.append(m)
    return maps


def kernel(**inputs):
    try:
        from concourse.compiler_utils import (get_compiler_flags,
                                              set_compiler_flags)
        set_compiler_flags([
            f.replace("--enable-ldw-opt=false", "--enable-ldw-opt=true")
            for f in get_compiler_flags()])
    except Exception:
        pass
    maps = _prep_inputs(inputs)
    nc, _ = build_program()
    res = run_bass_kernel_spmd(nc, maps, core_ids=list(range(N_CORES)))
    out = np.empty((B, T, D), np.float32)
    for core in range(N_CORES):
        b = core // 4
        q0 = (core % 4) * TOK
        out[b, q0:q0 + TOK] = np.asarray(res.results[core]["out"])
    return out


# version 19
# speedup vs baseline: 1.1608x; 1.0869x over previous
import contextlib
import numpy as np
import ml_dtypes
import concourse.bass as bass
import concourse.mybir as mybir
from concourse.bass_utils import run_bass_kernel_spmd

F32 = mybir.dt.float32
BF16 = mybir.dt.bfloat16
AF = mybir.ActivationFunctionType
ALU = mybir.AluOpType

B, T, D = 2, 4096, 2048
HQ, HKV, HD = 32, 8, 64
TOK = 1024
INTER = 5632
NIT = INTER // 128
NTC = T // 512
NKT = T // 128
C_SHIFT = 10.0
EPS = 1e-5
N_CORES = 8


class Sems:
    def __init__(self, nc, es):
        self.nc = nc
        self.es = es
        self.cnt = {}
        self.h = {}

    def new(self, *names):
        for n in names:
            self.h[n] = self.es.enter_context(self.nc.semaphore(n))
            self.cnt[n] = 0

    def inc(self, name, n=1):
        self.cnt[name] += n
        return self.h[name], n

    def __getitem__(self, name):
        return self.h[name]


def build_program(debug=None):
    nc = bass.Bass("TRN2", target_bir_lowering=False)

    d_xT = nc.dram_tensor("xT", [D, T], F32, kind="ExternalInput")
    d_xqT = nc.dram_tensor("xqT", [D, TOK], F32, kind="ExternalInput")
    d_xres = nc.dram_tensor("xres", [TOK, D], F32, kind="ExternalInput").ap()
    d_wqT = nc.dram_tensor("wqT", [D, D], F32, kind="ExternalInput")
    d_wkT = nc.dram_tensor("wkT", [D, 512], F32, kind="ExternalInput")
    d_wvT = nc.dram_tensor("wvT", [D, 512], F32, kind="ExternalInput")
    d_woT = nc.dram_tensor("woT", [D, D], F32, kind="ExternalInput").ap()
    d_bqt = nc.dram_tensor("bqt", [128, 16], F32, kind="ExternalInput").ap()
    d_bkt = nc.dram_tensor("bkt", [128, 4], F32, kind="ExternalInput").ap()
    d_bvb = nc.dram_tensor("bvb", [128, 512], F32, kind="ExternalInput").ap()
    d_bob = nc.dram_tensor("bob", [128, D], F32, kind="ExternalInput").ap()
    d_gwT = nc.dram_tensor("gwT", [D, INTER], F32, kind="ExternalInput")
    d_uwT = nc.dram_tensor("uwT", [D, INTER], F32, kind="ExternalInput")
    d_dwT = nc.dram_tensor("dwT", [INTER, D], F32, kind="ExternalInput").ap()
    d_gst = nc.dram_tensor("gst", [128, NIT], F32, kind="ExternalInput").ap()
    d_ust = nc.dram_tensor("ust", [128, NIT], F32, kind="ExternalInput").ap()
    d_dsb = nc.dram_tensor("dsb", [128, D], F32, kind="ExternalInput").ap()
    d_cosK = nc.dram_tensor("cosK", [128, T], F32, kind="ExternalInput").ap()
    d_sinK = nc.dram_tensor("sinK", [128, T], F32, kind="ExternalInput").ap()
    d_cosQ = nc.dram_tensor("cosQ", [128, TOK], F32, kind="ExternalInput").ap()
    d_sinQ = nc.dram_tensor("sinQ", [128, TOK], F32, kind="ExternalInput").ap()
    d_idf = nc.dram_tensor("identf", [128, 128], F32, kind="ExternalInput").ap()
    d_out = nc.dram_tensor("out", [TOK, D], F32, kind="ExternalOutput").ap()
    d_zb = nc.dram_tensor("zb", [64, 1024], F32)
    d_xrb = nc.dram_tensor("xrb", [TOK, D], F32)
    d_ydr = nc.dram_tensor("ydr", [TOK, D], F32)

    dbg_outs = {}

    def dbg(name, shape, dt):
        dbg_outs[name] = nc.dram_tensor(name, shape, dt,
                                        kind="ExternalOutput").ap()
        return dbg_outs[name]

    es = contextlib.ExitStack()
    S = Sems(nc, es)
    S.new("x", "xb", "mmKV", "kev", "vev", "sh", "rope",
          "wld", "wcast", "mmQ", "qev", "qsh", "qrope",
          "psS", "psE", "psA", "zr", "zbd", "bc", "aev",
          "wo_x", "wo_mm", "wo_ev", "ln1", "ysp", "f1c", "f1mm", "f1ev",
          "f1w", "f1wc", "gu_mm", "gu_ev", "hdn",
          "f2w", "f2c", "f2mm", "f2ev", "ln2", "outd", "misc",
          "x0", "x1", "x2", "x3", "wq0", "wq1", "wox0", "wox1", "f1w0", "f1w1",
          "f2w0", "f2w1", "f2w2", "zb0", "zb1", "bc0", "bc1", "ysp0", "ysp1",
          "pre", "tabK", "tabQ", "bobd", "dsbd", "wo0", "wo1",
          "ropev", "qropev", "xq0", "xq1", "xqc", "ropea", "tab0", "tab1",
          "zb", "wx0", "wx1", "tmm", "tev",
          "f1g0", "f1g1", "f1u0", "f1u1", "sdv")

    with es:
        bvb = es.enter_context(nc.sbuf_tensor("s_bvb", [128, 512], F32))
        bkt = es.enter_context(nc.sbuf_tensor("s_bkt", [128, 4], F32))
        bqt = es.enter_context(nc.sbuf_tensor("s_bqt", [128, 16], F32))
        gst = es.enter_context(nc.sbuf_tensor("s_gst", [128, NIT], F32))
        ust = es.enter_context(nc.sbuf_tensor("s_ust", [128, NIT], F32))
        cneg = es.enter_context(nc.sbuf_tensor("cneg", [128, 1], F32))
        epsp = es.enter_context(nc.sbuf_tensor("epsp", [128, 1], F32))
        ident = es.enter_context(nc.sbuf_tensor("ident", [128, 128], F32))
        lt2 = es.enter_context(nc.sbuf_tensor("lt2", [128, 64], F32))
        ngst = es.enter_context(nc.sbuf_tensor("ngst", [128, NIT], F32))

        for ap_d, t_s in ((d_bvb, bvb), (d_bkt, bkt), (d_bqt, bqt),
                          (d_gst, gst), (d_ust, ust), (d_idf, ident)):
            nc.scalar.dma_start(t_s[:], ap_d[:]).then_inc(*S.inc("pre", 16))
        nc.vector.memset(cneg[:], -C_SHIFT)
        nc.vector.memset(epsp[:], EPS)
        nc.vector.wait_ge(S["pre"], 96)
        nc.vector.tensor_scalar_mul(ngst[:], gst[:], -1.0
                                    ).then_inc(*S.inc("misc"))

        kv_stack = contextlib.ExitStack()
        xq_right = contextlib.ExitStack()
        xq_bf = xq_right.enter_context(
            nc.sbuf_tensor("xq_bf", [128, 16 * TOK], BF16, side="right"))
        KT = kv_stack.enter_context(nc.sbuf_tensor("KT", [128, 4 * T], BF16))
        Vp = kv_stack.enter_context(
            nc.sbuf_tensor("Vp", [128, HKV * NKT * 65], BF16))
        nc.vector.memset(
            bass.AP(Vp, 64, [[HKV * NKT * 65, 128], [65, HKV * NKT]]), 1.0)

        with contextlib.ExitStack() as pa1:
            cosb = [pa1.enter_context(
                nc.sbuf_tensor(f"cosb{i}", [128, 512], F32)) for i in range(2)]
            sinb = [pa1.enter_context(
                nc.sbuf_tensor(f"sinb{i}", [128, 512], F32)) for i in range(2)]
            wk_bf = pa1.enter_context(nc.sbuf_tensor("wk_bf", [128, 8192], BF16))
            wv_bf = pa1.enter_context(nc.sbuf_tensor("wv_bf", [128, 8192], BF16))
            Kstg = pa1.enter_context(nc.sbuf_tensor("Kstg", [128, 2048], F32))
            Ksh = pa1.enter_context(nc.sbuf_tensor("Ksh", [128, 2048], F32))
            xbb = [pa1.enter_context(nc.sbuf_tensor(f"xbb{i}", [128, 512], BF16))
                   for i in range(4)]
            kt1 = [pa1.enter_context(nc.sbuf_tensor(f"kt1{i}", [128, 512], F32))
                   for i in range(2)]
            kt2 = [pa1.enter_context(nc.sbuf_tensor(f"kt2{i}", [128, 512], F32))
                   for i in range(2)]
            psK = [pa1.enter_context(nc.psum_tensor(f"psK{i}", [128, 512], F32))
                   for i in range(4)]
            psV = [pa1.enter_context(nc.psum_tensor(f"psV{i}", [128, 512], F32))
                   for i in range(4)]


            for dsrc, bfdst in ((d_wkT, wk_bf), (d_wvT, wv_bf)):
                srcap = bass.AP(dsrc, 0, [[512, 128], [128 * 512, 16], [1, 512]])
                dstap = bass.AP(bfdst, 0, [[8192, 128], [512, 16], [1, 512]])
                nc.gpsimd.dma_start(dstap, srcap).then_inc(*S.inc("pre", 16))

            def kv_load(k):
                bi = k % 4
                tc_, d_ = divmod(k, 16)
                if k >= 4:
                    nc.gpsimd.wait_ge(S["mmKV"], k - 3)
                nc.gpsimd.dma_start(
                    xbb[bi][:],
                    d_xT.ap()[d_ * 128:(d_ + 1) * 128,
                              tc_ * 512:(tc_ + 1) * 512],
                ).then_inc(*S.inc(f"x{bi}", 16))

            def kv_mm(k):
                bi = k % 4
                tc_, d_ = divmod(k, 16)
                nc.tensor.wait_ge(S[f"x{bi}"], 16 * (k // 4 + 1))
                if k == 0:
                    nc.tensor.wait_ge(S["pre"], S.cnt["pre"])
                if tc_ > 0 and d_ == 0:
                    nc.tensor.wait_ge(S["kev"], tc_)
                    nc.tensor.wait_ge(S["vev"], tc_)
                for m in range(4):
                    nc.tensor.matmul(
                        psK[m][:],
                        wk_bf[:, d_ * 512 + m * 128:d_ * 512 + (m + 1) * 128],
                        xbb[bi][:], start=(d_ == 0), stop=(d_ == 15))
                mm = None
                for tt in range(4):
                    mm = nc.tensor.matmul(
                        psV[tt][:], xbb[bi][:, tt * 128:(tt + 1) * 128],
                        wv_bf[:, d_ * 512:(d_ + 1) * 512],
                        start=(d_ == 0), stop=(d_ == 15))
                mm.then_inc(*S.inc("mmKV"))

            def kv_evac_rope(tc_):
                nc.scalar.wait_ge(S["mmKV"], (tc_ + 1) * 16)
                if tc_ == 0:
                    nc.scalar.wait_ge(S["pre"], 96)
                else:
                    nc.scalar.wait_ge(S["ropev"], 4 * tc_)
                    nc.scalar.wait_ge(S["sh"], 64 * tc_)
                act = None
                for m in range(4):
                    act = nc.scalar.activation(
                        Kstg[:, m * 512:(m + 1) * 512], psK[m][:], AF.Identity,
                        bias=bkt[:, m:m + 1], scale=1.0)
                act.then_inc(*S.inc("kev"))
                nc.scalar.wait_ge(S["kev"], tc_ + 1)
                for gg in range(2):
                    for hswap in range(2):
                        so, do_ = (32, 0) if hswap == 0 else (0, 32)
                        nc.scalar.dma_start(
                            Ksh[gg * 64 + do_:gg * 64 + do_ + 32, :],
                            Kstg[gg * 64 + so:gg * 64 + so + 32, :],
                        ).then_inc(*S.inc("sh", 16))
                nc.vector.wait_ge(S["mmKV"], (tc_ + 1) * 16)
                if tc_ == 0:
                    nc.vector.wait_ge(S["pre"], 96)
                tt_last = None
                for tt in range(4):
                    kt_g = tc_ * 4 + tt
                    dst = bass.AP(Vp, kt_g * 65,
                                  [[HKV * NKT * 65, 128], [2080, HKV], [1, 64]])
                    src = bass.AP(psV[tt], 0, [[512, 128], [64, HKV], [1, 64]])
                    tt_last = nc.vector.tensor_tensor(dst, src, bvb[:], ALU.add)
                tt_last.then_inc(*S.inc("vev"))
                nc.vector.wait_ge(S["sh"], 64 * (tc_ + 1))
                nc.vector.wait_ge(S[f"tab{tc_ % 2}"], S.cnt[f"tab{tc_ % 2}"])
                for m in range(4):
                    gm = 4 * tc_ + m
                    if gm >= 2:
                        nc.vector.wait_ge(S["ropea"], gm - 1)
                    nc.vector.tensor_tensor(
                        kt1[m % 2][:], Ksh[:, m * 512:(m + 1) * 512],
                        sinb[tc_ % 2][:], ALU.mult)
                    nc.vector.tensor_tensor(
                        kt2[m % 2][:], Kstg[:, m * 512:(m + 1) * 512],
                        cosb[tc_ % 2][:], ALU.mult
                    ).then_inc(*S.inc("ropev"))
                last = None
                for m in range(4):
                    gm = 4 * tc_ + m
                    nc.gpsimd.wait_ge(S["ropev"], gm + 1)
                    last = nc.gpsimd.tensor_tensor(
                        KT[:, m * T + tc_ * 512:m * T + (tc_ + 1) * 512],
                        kt1[m % 2][:], kt2[m % 2][:], ALU.add
                    ).then_inc(*S.inc("ropea"))

            def tab_load(tc_):
                bi = tc_ % 2
                if tc_ >= 2:
                    nc.gpsimd.wait_ge(S["ropev"], 4 * (tc_ - 1))
                nc.gpsimd.dma_start(cosb[bi][:],
                                    d_cosK[:, tc_ * 512:(tc_ + 1) * 512]
                                    ).then_inc(*S.inc(f"tab{bi}", 16))
                nc.gpsimd.dma_start(sinb[bi][:],
                                    d_sinK[:, tc_ * 512:(tc_ + 1) * 512]
                                    ).then_inc(*S.inc(f"tab{bi}", 16))

            def xq_load(dq):
                nc.gpsimd.dma_start(xq_bf[:, dq * TOK:(dq + 1) * TOK],
                                    d_xqT.ap()[dq * 128:(dq + 1) * 128, :]
                                    ).then_inc(*S.inc("xqc", 16))

            tab_load(0)
            tab_load(1)
            for k in range(4):
                kv_load(k)
            xq_load(0)
            xq_load(1)
            for g in range(NTC * 16):
                tc, d = divmod(g, 16)
                if g + 4 < NTC * 16:
                    kv_load(g + 4)
                kv_mm(g)
                if d == 7 and tc < 8:
                    if 2 * tc + 2 < 16:
                        xq_load(2 * tc + 2)
                if d == 15:
                    kv_evac_rope(tc)
                    if tc + 2 < NTC:
                        tab_load(tc + 2)
                    if 2 * tc + 3 < 16:
                        xq_load(2 * tc + 3)
            nc.all_engine_barrier()
            if debug == "pa1":
                o1 = dbg("dbg_KT", [128, 4 * T], BF16)
                o2 = dbg("dbg_Vp", [128, HKV * NKT * 65], BF16)
                nc.sync.dma_start(o1[:], KT[:]).then_inc(*S.inc("outd", 16))
                nc.sync.dma_start(o2[:], Vp[:]).then_inc(*S.inc("outd", 16))
                nc.sync.wait_ge(S["outd"], S.cnt["outd"])
                nc.all_engine_barrier()

        ps_tiles = contextlib.ExitStack()
        wT = [ps_tiles.enter_context(
            nc.sbuf_tensor(f"wT{i}", [128, 1024], BF16)) for i in range(2)]
        zrec = ps_tiles.enter_context(nc.sbuf_tensor("zrec", [128, 1024], F32))
        bcs = ps_tiles.enter_context(nc.sbuf_tensor("bcs", [128, 1024], F32))
        qb_stack = contextlib.ExitStack()
        QT2b = qb_stack.enter_context(
            nc.sbuf_tensor("QT2b", [128, 16 * 512], BF16))
        qa_stack = contextlib.ExitStack()
        QT2a = qa_stack.enter_context(
            nc.sbuf_tensor("QT2a", [128, 16 * 512], BF16))
        QT2ab = [QT2a, QT2b]
        with contextlib.ExitStack() as pa2:
            cosQ = pa2.enter_context(nc.sbuf_tensor("s_cosQ", [128, TOK], F32))
            sinQ = pa2.enter_context(nc.sbuf_tensor("s_sinQ", [128, TOK], F32))
            wqs_bf = [pa2.enter_context(
                nc.sbuf_tensor(f"wqs_bf{i}", [128, 2048], BF16)) for i in range(2)]
            Qstg = [pa2.enter_context(
                nc.sbuf_tensor(f"Qstg{i}", [128, TOK], F32)) for i in range(2)]
            Qsh = [pa2.enter_context(
                nc.sbuf_tensor(f"Qsh{i}", [128, TOK], F32)) for i in range(2)]
            qt1 = [pa2.enter_context(nc.sbuf_tensor(f"qt1{i}", [128, TOK], F32))
                   for i in range(2)]
            qt2 = [pa2.enter_context(nc.sbuf_tensor(f"qt2{i}", [128, TOK], F32))
                   for i in range(2)]
            psQ = [pa2.enter_context(nc.psum_tensor(f"psQ{i}", [128, 512], F32))
                   for i in range(8)]

            nc.scalar.dma_start(cosQ[:], d_cosQ[:]).then_inc(*S.inc("tabQ", 16))
            nc.scalar.dma_start(sinQ[:], d_sinQ[:]).then_inc(*S.inc("tabQ", 16))
            mq_base = S.cnt["mmQ"]

            def q_load(j):
                bi = j % 2
                if j >= 2:
                    nc.gpsimd.wait_ge(S["mmQ"], mq_base + j - 1)
                for half in range(2):
                    src = bass.AP(d_wqT, half * 8 * 128 * D + j * 128,
                                  [[D, 128], [128 * D, 8], [1, 128]])
                    dst = bass.AP(wqs_bf[bi], half * 1024,
                                  [[2048, 128], [128, 8], [1, 128]])
                    nc.gpsimd.dma_start(dst, src).then_inc(*S.inc(f"wq{bi}", 16))

            def q_comp(j):
                bi = j % 2
                sbi = j % 2
                nc.tensor.wait_ge(S[f"wq{bi}"], 32 * (j // 2 + 1))
                if j == 0:
                    nc.tensor.wait_ge(S["xqc"], 256)
                pA, pB = psQ[(j % 4) * 2], psQ[(j % 4) * 2 + 1]
                if j >= 4:
                    nc.tensor.wait_ge(S["qev"], j - 3)
                mmq = None
                for d in range(16):
                    nc.tensor.matmul(pA[:],
                                     wqs_bf[bi][:, d * 128:(d + 1) * 128],
                                     xq_bf[:, d * TOK:d * TOK + 512],
                                     start=(d == 0), stop=(d == 15))
                    mmq = nc.tensor.matmul(
                        pB[:], wqs_bf[bi][:, d * 128:(d + 1) * 128],
                        xq_bf[:, d * TOK + 512:(d + 1) * TOK],
                        start=(d == 0), stop=(d == 15))
                mmq.then_inc(*S.inc("mmQ"))
                nc.scalar.wait_ge(S["mmQ"], mq_base + j + 1)
                if j == 0:
                    nc.scalar.wait_ge(S["pre"], 96)
                if j >= 2:
                    nc.scalar.wait_ge(S["qrope"], j - 1)
                nc.scalar.activation(Qstg[sbi][:, 0:512], pA[:], AF.Identity,
                                     bias=bqt[:, j:j + 1], scale=1.0)
                nc.scalar.activation(Qstg[sbi][:, 512:1024], pB[:],
                                     AF.Identity, bias=bqt[:, j:j + 1],
                                     scale=1.0).then_inc(*S.inc("qev"))
                nc.scalar.wait_ge(S["qev"], j + 1)
                if j >= 1:
                    nc.scalar.wait_ge(S["qrope"], j)
                for gg in range(2):
                    for hswap in range(2):
                        so, do_ = (32, 0) if hswap == 0 else (0, 32)
                        nc.scalar.dma_start(
                            Qsh[sbi][gg * 64 + do_:gg * 64 + do_ + 32, :],
                            Qstg[sbi][gg * 64 + so:gg * 64 + so + 32, :],
                        ).then_inc(*S.inc("qsh", 16))
                nc.vector.wait_ge(S["qsh"], S.cnt["qsh"])
                if j == 0:
                    nc.vector.wait_ge(S["tabQ"], 32)
                if j >= 2:
                    nc.vector.wait_ge(S["qrope"], j - 1)
                nc.vector.tensor_tensor(qt1[sbi][:], Qsh[sbi][:], sinQ[:],
                                        ALU.mult)
                nc.vector.tensor_tensor(qt2[sbi][:], Qstg[sbi][:], cosQ[:],
                                        ALU.mult).then_inc(*S.inc("qropev"))
                nc.gpsimd.wait_ge(S["qropev"], j + 1)
                last = None
                for H, rsrc in ((2 * j, 0), (2 * j + 1, 64)):
                    blk = 4 * (H // 8) + (H % 4)
                    r0 = 0 if (H % 8) < 4 else 64
                    for qb in range(2):
                        last = nc.gpsimd.tensor_tensor(
                            QT2ab[qb][r0:r0 + 64, blk * 512:(blk + 1) * 512],
                            qt1[sbi][rsrc:rsrc + 64, qb * 512:(qb + 1) * 512],
                            qt2[sbi][rsrc:rsrc + 64, qb * 512:(qb + 1) * 512],
                            ALU.add)
                last.then_inc(*S.inc("qrope"))

            q_load(0)
            q_load(1)
            for j in range(16):
                if j + 2 < 16:
                    q_load(j + 2)
                q_comp(j)
            nc.all_engine_barrier()
            if debug == "pa2":
                o1 = dbg("dbg_QT2", [128, 16 * 512], BF16)
                nc.sync.dma_start(o1[:], QT2a[:]).then_inc(*S.inc("outd", 16))
                nc.sync.wait_ge(S["outd"], S.cnt["outd"])
                nc.all_engine_barrier()

        xq_right.close()
        ds_stack = contextlib.ExitStack()
        attnT_ab = [None, None]
        attnT_ab[1] = ds_stack.enter_context(
            nc.sbuf_tensor("attnTb", [128, 16 * 512], BF16, side="right"))
        attn_a_stack = contextlib.ExitStack()
        attnT_ab[0] = attn_a_stack.enter_context(
            nc.sbuf_tensor("attnTa", [128, 16 * 512], BF16, side="right"))
        psds_stack = contextlib.ExitStack()
        psDS = [None, None]

        def ds_half(h, attnT, post_wo, half_stack, guard):
            sub1b = contextlib.ExitStack()
            ytile = sub1b.enter_context(
                nc.sbuf_tensor(f"ytile{h}", [128, 4 * 2048], F32))
            sub1a = contextlib.ExitStack()
            wo_bb = [sub1a.enter_context(
                nc.sbuf_tensor(f"wo_bb{h}{i}", [128, 512], BF16))
                for i in range(2)]
            xr_s = [sub1a.enter_context(
                nc.sbuf_tensor(f"xr_s{h}{i}", [128, 512], F32))
                for i in range(4)]
            wo_st = None
            if h == 1:
                wo_st = [sub1a.enter_context(
                    nc.sbuf_tensor(f"wo_st{i}", [128, 512], F32))
                    for i in range(2)]
            hdnT_h = yT_h = None
            g_bf = u_bf = None
            se = sr = sg2 = su2 = ptm = None
            b_womm = S.cnt["wo_mm"]
            b_woev = S.cnt["wo_ev"]
            b_ln1 = S.cnt["ln1"]
            b_tmm = S.cnt["tmm"]
            b_tev = S.cnt["tev"]
            b_gumm = S.cnt["gu_mm"]
            b_guev = S.cnt["gu_ev"]
            b_sdv = S.cnt["sdv"]
            b_hdn = S.cnt["hdn"]
            b_aev = S.cnt["aev"]
            b_ysp = S.cnt["ysp0"]
            b_wo = [S.cnt["wo0"], S.cnt["wo1"]]
            b_wx = [S.cnt["wx0"], S.cnt["wx1"]]
            b_f1g = [S.cnt["f1g0"], S.cnt["f1g1"]]
            b_f1u = [S.cnt["f1u0"], S.cnt["f1u1"]]
            b_f1wc = S.cnt["f1wc"]
            b_wcast = S.cnt["wcast"]

            def wo_load(u):
                bi = u % 2
                pi, j = divmod(u, 16)
                p, ds_ = divmod(pi, 4)
                if u == 0:
                    nc.gpsimd.wait_ge(S["zbd"], S.cnt["zbd"])
                if u >= 2:
                    nc.gpsimd.wait_ge(S["wo_mm"], b_womm + 2 * (u - 1))
                nc.gpsimd.dma_start(
                    wo_bb[bi][:],
                    d_woT[j * 128:(j + 1) * 128, ds_ * 512:(ds_ + 1) * 512]
                ).then_inc(*S.inc(f"wo{bi}", 16))

            def wo_load_sync(u):
                bi = u % 2
                pi, j = divmod(u, 16)
                p, ds_ = divmod(pi, 4)
                if u >= 2:
                    nc.sync.wait_ge(S["wcast"], b_wcast + u - 1)
                nc.sync.dma_start(
                    wo_st[bi][:],
                    d_woT[j * 128:(j + 1) * 128, ds_ * 512:(ds_ + 1) * 512]
                ).then_inc(*S.inc(f"wo{bi}", 16))

            def wo_cast(u):
                bi = u % 2
                nc.vector.wait_ge(S[f"wo{bi}"], b_wo[bi] + 16 * (u // 2 + 1))
                if u >= 2:
                    nc.vector.wait_ge(S["wo_mm"], b_womm + 2 * (u - 1))
                nc.vector.tensor_copy(wo_bb[bi][:], wo_st[bi][:]
                                      ).then_inc(*S.inc("wcast"))

            def wo_mms(u):
                bi = u % 2
                pi, j = divmod(u, 16)
                p, ds_ = divmod(pi, 4)
                if u == 0:
                    nc.tensor.wait_ge(S["aev"], b_aev)
                if h == 0:
                    nc.tensor.wait_ge(S[f"wo{bi}"],
                                      b_wo[bi] + 16 * (u // 2 + 1))
                else:
                    nc.tensor.wait_ge(S["wcast"], b_wcast + u + 1)
                if j == 0 and pi >= 1:
                    nc.tensor.wait_ge(S["wo_ev"], b_woev + 2 * pi)
                for iq in range(2):
                    qt = 2 * p + iq
                    nc.tensor.matmul(
                        psDS[iq][:],
                        attnT[:, j * 512 + qt * 128:j * 512 + (qt + 1) * 128],
                        wo_bb[bi][:],
                        start=(j == 0), stop=(j == 15)
                    ).then_inc(*S.inc("wo_mm"))

            def wo_xr(pi):
                p, ds_ = divmod(pi, 4)
                if pi >= 2:
                    nc.sync.wait_ge(S["wo_ev"], b_woev + 2 * (pi - 1))
                for iq in range(2):
                    qt_g = h * 4 + 2 * p + iq
                    nc.sync.dma_start(
                        xr_s[2 * (pi % 2) + iq][:],
                        d_xrb.ap()[qt_g * 128:(qt_g + 1) * 128,
                                   ds_ * 512:(ds_ + 1) * 512]
                    ).then_inc(*S.inc(f"wx{iq}", 16))

            def wo_evac(pi):
                p, ds_ = divmod(pi, 4)
                nc.vector.wait_ge(S["wo_mm"], b_womm + 32 * (pi + 1))
                nc.vector.wait_ge(S["wx0"], b_wx[0] + 16 * (pi + 1))
                nc.vector.wait_ge(S["wx1"], b_wx[1] + 16 * (pi + 1))
                for iq in range(2):
                    qt = 2 * p + iq
                    nc.vector.tensor_tensor(
                        ytile[:, qt * 2048 + ds_ * 512:
                              qt * 2048 + (ds_ + 1) * 512],
                        psDS[iq][:], xr_s[2 * (pi % 2) + iq][:], ALU.add
                    ).then_inc(*S.inc("wo_ev"))

            def ln1_stats():
                junk = hdnT_h[:, 0:2048]
                nc.scalar.wait_ge(S["wo_ev"], b_woev + 16)
                last = None
                for qt in range(4):
                    seg = ytile[:, qt * 2048:(qt + 1) * 2048]
                    last = nc.scalar.activation(
                        junk, seg, AF.Square,
                        accum_out=lt2[:, 8 + qt:9 + qt])
                    nc.vector.tensor_reduce(lt2[:, qt:qt + 1], seg,
                                            mybir.AxisListType.X, ALU.add)
                last.then_inc(*S.inc("ln1"))
                nc.vector.drain().then_inc(*S.inc("ln1"))

            def ln1_chain():
                nc.vector.wait_ge(S["ln1"], b_ln1 + 2)
                nc.vector.tensor_scalar_mul(lt2[:, 16:20], lt2[:, 0:4],
                                            -1.0 / D)
                nc.vector.tensor_scalar_mul(lt2[:, 24:28], lt2[:, 8:12],
                                            1.0 / D)
                nc.vector.drain()
                nc.vector.tensor_tensor(lt2[:, 32:36], lt2[:, 16:20],
                                        lt2[:, 16:20], ALU.mult)
                nc.vector.drain()
                nc.vector.tensor_tensor(lt2[:, 40:44], lt2[:, 24:28],
                                        lt2[:, 32:36], ALU.subtract)
                nc.vector.drain()
                nc.vector.tensor_scalar_add(lt2[:, 48:52], lt2[:, 40:44], EPS
                                            ).then_inc(*S.inc("ln1"))
                nc.scalar.wait_ge(S["ln1"], b_ln1 + 3)
                nc.scalar.activation(lt2[:, 40:44], lt2[:, 48:52], AF.Ln)
                nc.scalar.drain()
                nc.scalar.activation(lt2[:, 48:52], lt2[:, 40:44], AF.Exp,
                                     scale=-0.5).then_inc(*S.inc("ln1"))
                nc.vector.wait_ge(S["ln1"], b_ln1 + 4)
                nc.vector.tensor_tensor(lt2[:, 56:60], lt2[:, 16:20],
                                        lt2[:, 48:52], ALU.mult
                                        ).then_inc(*S.inc("ln1"))

            def ln1_norm(qt):
                seg = ytile[:, qt * 2048:(qt + 1) * 2048]
                if qt == 0:
                    nc.scalar.wait_ge(S["ln1"], b_ln1 + 5)
                nc.scalar.activation(seg, seg, AF.Identity,
                                     bias=lt2[:, 56 + qt:57 + qt],
                                     scale=lt2[:, 48 + qt:49 + qt]
                                     ).then_inc(*S.inc("ln1"))
                qt_g = h * 4 + qt
                nc.sync.wait_ge(S["ln1"], S.cnt["ln1"])
                nc.sync.dma_start(d_ydr.ap()[qt_g * 128:(qt_g + 1) * 128, :],
                                  seg).then_inc(*S.inc("ysp0", 16))

            def transp(qt, f):
                fi = qt * 4 + f
                if fi == 0:
                    nc.tensor.wait_ge(S["wo_ev"], b_woev + 16)
                nc.tensor.wait_ge(S["ln1"], b_ln1 + 6 + qt)
                if fi >= 2:
                    nc.tensor.wait_ge(S["tev"], b_tev + fi - 1)
                last = None
                for c in range(4):
                    hc = 4 * f + c
                    last = nc.tensor.transpose(
                        psDS[fi % 2][:, c * 128:(c + 1) * 128],
                        ytile[:, qt * 2048 + hc * 128:
                              qt * 2048 + (hc + 1) * 128],
                        ident[:])
                last.then_inc(*S.inc("tmm"))
                nc.vector.wait_ge(S["tmm"], b_tmm + fi + 1)
                dst = bass.AP(yT_h, (4 * f) * 512 + qt * 128,
                              [[16 * 512, 128], [512, 4], [1, 128]])
                nc.vector.tensor_copy(dst, psDS[fi % 2][:]
                                      ).then_inc(*S.inc("tev"))

            def f1_load(k):
                bi = k % 2
                if k == 0:
                    nc.gpsimd.wait_ge(S["tmm"], b_tmm + 16)
                    nc.gpsimd.wait_ge(S["ysp0"], b_ysp + 64)
                if k >= 2:
                    nc.gpsimd.wait_ge(S["gu_mm"], b_gumm + 16 * (k - 1))
                for w_, dsrc, bft in (("g", d_gwT, g_bf), ("u", d_uwT, u_bf)):
                    for half in range(2):
                        srcap = bass.AP(dsrc,
                                        half * 8 * 128 * INTER + k * 128,
                                        [[INTER, 128], [128 * INTER, 8],
                                         [1, 128]])
                        dstap = bass.AP(bft[bi], half * 1024,
                                        [[2048, 128], [128, 8], [1, 128]])
                        nc.gpsimd.dma_start(dstap, srcap).then_inc(
                            *S.inc(f"f1{w_}{bi}", 16))

            def f1_load_sync(k):
                bi = k % 2
                if k >= 2:
                    nc.sync.wait_ge(S["f1wc"], b_f1wc + 2 * (k - 1))
                for w_, dsrc, st_ in (("g", d_gwT, g_st), ("u", d_uwT, u_st)):
                    for half in range(2):
                        srcap = bass.AP(dsrc,
                                        half * 8 * 128 * INTER + k * 128,
                                        [[INTER, 128], [128 * INTER, 8],
                                         [1, 128]])
                        dstap = bass.AP(st_[bi], half * 1024,
                                        [[2048, 128], [128, 8], [1, 128]])
                        nc.sync.dma_start(dstap, srcap).then_inc(
                            *S.inc(f"f1{w_}{bi}", 16))

            def f1_cast(k):
                bi = k % 2
                nc.vector.wait_ge(S[f"f1g{bi}"], b_f1g[bi] + 32 * (k // 2 + 1))
                nc.vector.wait_ge(S[f"f1u{bi}"], b_f1u[bi] + 32 * (k // 2 + 1))
                if k >= 2:
                    nc.vector.wait_ge(S["gu_mm"], b_gumm + 16 * (k - 1))
                nc.vector.tensor_copy(g_bf[bi][:], g_st[bi][:])
                nc.vector.tensor_copy(u_bf[bi][:], u_st[bi][:]
                                      ).then_inc(*S.inc("f1wc", 2))

            def f1_mms(k, hc):
                bi = k % 2
                if hc == 0:
                    if h == 0:
                        nc.tensor.wait_ge(S[f"f1g{bi}"],
                                          b_f1g[bi] + 32 * (k // 2 + 1))
                        nc.tensor.wait_ge(S[f"f1u{bi}"],
                                          b_f1u[bi] + 32 * (k // 2 + 1))
                    else:
                        nc.tensor.wait_ge(S["f1wc"], b_f1wc + 2 * (k + 1))
                    if k == 0:
                        nc.tensor.wait_ge(S["tev"], b_tev + 16)
                    if k >= 1:
                        nc.tensor.wait_ge(S["gu_ev"], b_guev + 3 * k)
                mm = None
                for pp, bft in ((psDS[0], g_bf), (psDS[1], u_bf)):
                    mm = nc.tensor.matmul(
                        pp[:], bft[bi][:, hc * 128:(hc + 1) * 128],
                        yT_h[:, hc * 512:(hc + 1) * 512],
                        start=(hc == 0), stop=(hc == 15))
                mm.then_inc(*S.inc("gu_mm"))

            def f1_silu(k):
                nc.scalar.wait_ge(S["gu_mm"], b_gumm + 16 * (k + 1))
                if k == 0:
                    nc.scalar.wait_ge(S["pre"], 96)
                    nc.scalar.wait_ge(S["misc"], 1)
                    nc.scalar.wait_ge(S["ysp0"], b_ysp + 64)
                if k >= 1:
                    nc.scalar.wait_ge(S["hdn"], b_hdn + k)
                nc.scalar.activation(se[:], psDS[0][:], AF.Exp,
                                     scale=ngst[:, k:k + 1]
                                     ).then_inc(*S.inc("gu_ev"))
                nc.scalar.activation(sg2[:], psDS[0][:], AF.Identity,
                                     scale=gst[:, k:k + 1]
                                     ).then_inc(*S.inc("gu_ev"))
                nc.scalar.activation(su2[:], psDS[1][:], AF.Identity,
                                     scale=ust[:, k:k + 1]
                                     ).then_inc(*S.inc("gu_ev"))
                nc.vector.wait_ge(S["gu_ev"], b_guev + 3 * (k + 1))
                nc.vector.tensor_scalar_add(sr[:], se[:], 1.0)
                nc.vector.tensor_tensor(ptm[:], sg2[:], su2[:], ALU.mult)
                nc.vector.drain()
                nc.vector.reciprocal(se[:], sr[:])
                nc.vector.drain()
                nc.vector.tensor_tensor(
                    hdnT_h[:, k * 512:(k + 1) * 512], ptm[:], se[:], ALU.mult
                ).then_inc(*S.inc("hdn"))

            if guard is not None:
                nc.gpsimd.wait_ge(S[guard[0]], guard[1])
                nc.sync.wait_ge(S[guard[0]], guard[1])
            if h == 0:
                ldw, cw = wo_load, None
            else:
                ldw, cw = wo_load_sync, wo_cast
            ldw(0)
            if cw:
                cw(0)
            ldw(1)
            if cw:
                cw(1)
            for u in range(128):
                if u + 2 < 128:
                    ldw(u + 2)
                    if cw:
                        cw(u + 2)
                wo_mms(u)
                pi, j = divmod(u, 16)
                if j == 8:
                    wo_xr(pi)
                if j == 15:
                    wo_evac(pi)
                yield
            sub1a.close()
            hdnT_h, yT_h = post_wo()
            ln1_stats()
            yield
            ln1_chain()
            yield
            for qt in range(4):
                ln1_norm(qt)
                yield
            for qt in range(4):
                for f in range(4):
                    transp(qt, f)
                yield
            sub1b.close()
            g_bf = [half_stack.enter_context(
                nc.sbuf_tensor(f"g_bf{h}{i}", [128, 2048], BF16))
                for i in range(2)]
            u_bf = [half_stack.enter_context(
                nc.sbuf_tensor(f"u_bf{h}{i}", [128, 2048], BF16))
                for i in range(2)]
            if h == 1:
                g_st = [half_stack.enter_context(
                    nc.sbuf_tensor(f"g_st{i}", [128, 2048], F32))
                    for i in range(2)]
                u_st = [half_stack.enter_context(
                    nc.sbuf_tensor(f"u_st{i}", [128, 2048], F32))
                    for i in range(2)]
                ld1, cst1 = f1_load_sync, f1_cast
            else:
                ld1 = f1_load
                cst1 = None
            ld1(0)
            if cst1:
                cst1(0)
            ld1(1)
            if cst1:
                cst1(1)
            se = half_stack.enter_context(
                nc.sbuf_tensor(f"se{h}", [128, 512], F32))
            sr = half_stack.enter_context(
                nc.sbuf_tensor(f"sr{h}", [128, 512], F32))
            sg2 = half_stack.enter_context(
                nc.sbuf_tensor(f"sg2{h}", [128, 512], BF16))
            su2 = half_stack.enter_context(
                nc.sbuf_tensor(f"su2{h}", [128, 512], BF16))
            ptm = half_stack.enter_context(
                nc.sbuf_tensor(f"ptm{h}", [128, 512], BF16))
            for k in range(NIT):
                if k + 2 < NIT:
                    ld1(k + 2)
                    if cst1:
                        cst1(k + 2)
                for hc in range(16):
                    f1_mms(k, hc)
                    if hc % 2 == 1:
                        yield
                f1_silu(k)
                yield

        psx = contextlib.ExitStack()
        if True:
            Sps = [psx.enter_context(nc.psum_tensor(f"Sps{i}", [128, 1024], F32))
                   for i in range(2)]
            Aps = [psx.enter_context(nc.psum_tensor(f"Aps{i}", [65, 512], F32))
                   for i in range(2)]
            apsx = contextlib.ExitStack()
            Aps2 = [apsx.enter_context(
                nc.psum_tensor(f"Aps2{i}", [65, 512], F32)) for i in range(2)]

            seq = [(qb, m, i) for qb in range(2) for m in range(4)
                   for i in range(4)]

            def ps_S(itx, kt):
                qb, m, i = seq[itx]
                blk = 4 * m + i
                QTx = QT2ab[qb]
                sbi = kt % 2
                wbi = kt % 2
                g = itx * NKT + kt
                if g >= 2:
                    nc.tensor.wait_ge(S["psE"], g - 1)
                nc.tensor.matmul(
                    Sps[sbi][:, 0:512],
                    KT[0:64, m * T + kt * 128:m * T + (kt + 1) * 128],
                    QTx[0:64, blk * 512:(blk + 1) * 512],
                    start=True, stop=True, tile_position=(0, 0))
                nc.tensor.matmul(
                    Sps[sbi][:, 512:1024],
                    KT[64:128, m * T + kt * 128:m * T + (kt + 1) * 128],
                    QTx[64:128, blk * 512:(blk + 1) * 512],
                    start=True, stop=True, tile_position=(64, 0)
                ).then_inc(*S.inc("psS"))
                nc.scalar.wait_ge(S["psS"], g + 1)
                if g >= 2:
                    nc.scalar.wait_ge(S["psA"], g - 1)
                nc.scalar.activation(
                    wT[wbi][:], Sps[sbi][:], AF.Exp,
                    bias=cneg[:], scale=0.125).then_inc(*S.inc("psE"))

            def ps_AV(itx, kt):
                qb, m, i = seq[itx]
                kve, kvo = 2 * m, 2 * m + 1
                As = Aps if (itx >= 16 or itx % 2 == 0) else Aps2
                A0, A1 = As[0], As[1]
                wbi = kt % 2
                g = itx * NKT + kt
                if kt == 0:
                    if itx >= 16:
                        nc.tensor.wait_ge(S["aev"], itx)
                    elif itx >= 2:
                        nc.tensor.wait_ge(S["aev"], itx - 1)
                nc.tensor.wait_ge(S["psE"], g + 1)
                nc.tensor.matmul(
                    A0[:],
                    Vp[:, kve * 2080 + kt * 65:kve * 2080 + kt * 65 + 65],
                    wT[wbi][:, 0:512],
                    start=(kt == 0), stop=(kt == NKT - 1))
                nc.tensor.matmul(
                    A1[:],
                    Vp[:, kvo * 2080 + kt * 65:kvo * 2080 + kt * 65 + 65],
                    wT[wbi][:, 512:1024],
                    start=(kt == 0), stop=(kt == NKT - 1)
                ).then_inc(*S.inc("psA"))

            def ps_epilogue(itx):
                qb, m, i = seq[itx]
                He, Ho = 8 * m + i, 8 * m + 4 + i
                As = Aps if (itx >= 16 or itx % 2 == 0) else Aps2
                A0, A1 = As[0], As[1]
                attnT = attnT_ab[qb]
                zrow = itx % 2
                nc.vector.wait_ge(S["psA"], (itx + 1) * NKT)
                if itx >= 2:
                    nc.vector.wait_ge(S["zb"], 16 * (itx - 1))
                zr0 = zrow * 64
                nc.vector.reciprocal(zrec[zr0:zr0 + 1, 0:512], A0[64:65, :])
                nc.vector.reciprocal(zrec[zr0:zr0 + 1, 512:1024],
                                     A1[64:65, :]).then_inc(*S.inc("zr"))
                nc.sync.wait_ge(S["zr"], S.cnt["zr"])
                nc.sync.dma_start(
                    bass.AP(d_zb, zrow * 1024, [[1024, 1], [1, 1024]]),
                    zrec[zr0:zr0 + 1, 0:1024]).then_inc(*S.inc("zb", 16))
                nc.sync.wait_ge(S["zb"], S.cnt["zb"])
                if itx >= 1:
                    nc.sync.wait_ge(S["aev"], itx)
                nc.sync.dma_start(
                    bcs[:],
                    bass.AP(d_zb, zrow * 1024, [[0, 128], [1, 1024]]),
                ).then_inc(*S.inc("bc", 16))
                nc.vector.wait_ge(S["bc"], S.cnt["bc"])
                lastev = None
                for H, A_, co in ((He, A0, 0), (Ho, A1, 512)):
                    bj = H // 2
                    r0 = (H % 2) * 64
                    lastev = nc.vector.tensor_tensor(
                        attnT[r0:r0 + 64, bj * 512:(bj + 1) * 512],
                        A_[0:64, :], bcs[0:64, co:co + 512], ALU.mult)
                lastev.then_inc(*S.inc("aev"))

            GPS = len(seq) * NKT
            H1G = 16 * NKT
            NDSU = 560
            half0_stack = contextlib.ExitStack()
            h0st = {}

            def post_wo_h0():
                attn_a_stack.close()
                h0st["hdnT"] = ds_stack.enter_context(nc.sbuf_tensor(
                    "hdnT0", [128, NIT * 512], BF16, side="right"))
                h0st["yT"] = half0_stack.enter_context(nc.sbuf_tensor(
                    "yT0", [128, 16 * 512], BF16, side="right"))
                return h0st["hdnT"], h0st["yT"]

            with contextlib.ExitStack() as xps:
                bobt = xps.enter_context(
                    nc.sbuf_tensor("bobt", [128, D], F32))
                xst = [xps.enter_context(
                    nc.sbuf_tensor(f"xst{i}", [128, D], F32))
                    for i in range(2)]
                xso = [xps.enter_context(
                    nc.sbuf_tensor(f"xso{i}", [128, D], F32))
                    for i in range(2)]
                nc.scalar.dma_start(bobt[:], d_bob[:]).then_inc(
                    *S.inc("bobd", 16))
                for ti in range(8):
                    bi = ti % 2
                    nc.sync.dma_start(
                        xst[bi][:], d_xres[ti * 128:(ti + 1) * 128, :]
                    ).then_inc(*S.inc(f"wox{bi}", 16))
                    nc.vector.wait_ge(S[f"wox{bi}"], S.cnt[f"wox{bi}"])
                    if ti == 0:
                        nc.vector.wait_ge(S["bobd"], S.cnt["bobd"])
                    if ti >= 2:
                        nc.vector.wait_ge(S["zbd"], 16 * (ti - 1))
                    nc.vector.tensor_tensor(xso[bi][:], xst[bi][:], bobt[:],
                                            ALU.add).then_inc(*S.inc("misc"))
                    nc.sync.wait_ge(S["misc"], S.cnt["misc"])
                    nc.sync.dma_start(
                        d_xrb.ap()[ti * 128:(ti + 1) * 128, :], xso[bi][:]
                    ).then_inc(*S.inc("zbd", 16))

            dsg = None
            n_units = 0
            ps_S(0, 0)
            ps_S(0, 1)
            for g in range(GPS):
                itx, kt = divmod(g, NKT)
                if g == H1G:
                    qa_stack.close()
                    apsx.close()
                    psDS[0] = psds_stack.enter_context(
                        nc.psum_tensor("psDS0", [128, 512], F32))
                    psDS[1] = psds_stack.enter_context(
                        nc.psum_tensor("psDS1", [128, 512], F32))
                    dsg = ds_half(0, attnT_ab[0], post_wo_h0, half0_stack,
                                  ("psS", H1G))
                if dsg is not None:
                    gr = g - H1G
                    want = (NDSU * (gr + 1)) // H1G
                    while n_units < want:
                        if next(dsg, "END") == "END":
                            break
                        n_units += 1
                ps_AV(itx, kt)
                if g + 2 < GPS:
                    ps_S((g + 2) // NKT, (g + 2) % NKT)
                if kt == NKT - 1:
                    ps_epilogue(itx)
            if dsg is not None:
                for _ in dsg:
                    pass
            nc.all_engine_barrier()
            if debug == "ps":
                o1 = dbg("dbg_attnTa", [128, 16 * 512], BF16)
                o2 = dbg("dbg_attnTb", [128, 16 * 512], BF16)
                nc.sync.dma_start(o1[:], attnT_ab[0][:]).then_inc(
                    *S.inc("outd", 16))
                nc.sync.dma_start(o2[:], attnT_ab[1][:]).then_inc(
                    *S.inc("outd", 16))
                nc.sync.wait_ge(S["outd"], S.cnt["outd"])
                nc.all_engine_barrier()

        half0_stack.close()
        psds_stack.close()
        psx.close()
        qb_stack.close()
        ps_tiles.close()
        kv_stack.close()

        if True:
            psds2 = contextlib.ExitStack()
            psDS[0] = psds2.enter_context(
                nc.psum_tensor("psDS0b", [128, 512], F32))
            psDS[1] = psds2.enter_context(
                nc.psum_tensor("psDS1b", [128, 512], F32))
            with contextlib.ExitStack() as half1_stack:
                h1st = {}

                def post_wo_h1():
                    h1st["hdnT"] = ds_stack.enter_context(nc.sbuf_tensor(
                        "hdnT1", [128, NIT * 512], BF16, side="right"))
                    h1st["yT"] = half1_stack.enter_context(nc.sbuf_tensor(
                        "yT1", [128, 16 * 512], BF16, side="right"))
                    return h1st["hdnT"], h1st["yT"]

                for _ in ds_half(1, attnT_ab[1], post_wo_h1, half1_stack,
                                 None):
                    pass
                nc.all_engine_barrier()
            if debug == "ffn1":
                o1 = dbg("dbg_hdnT0", [128, NIT * 512], BF16)
                nc.sync.dma_start(o1[:], h0st["hdnT"][:]).then_inc(
                    *S.inc("outd", 16))
                nc.sync.wait_ge(S["outd"], S.cnt["outd"])
                nc.all_engine_barrier()

        hdnT_h0 = h0st["hdnT"]
        hdnT_h1 = h1st["hdnT"]
        psds2.close()

        with contextlib.ExitStack() as f2:
            res2 = f2.enter_context(nc.sbuf_tensor("res2", [128, 8 * D], F32))
            dsb = f2.enter_context(nc.sbuf_tensor("s_dsb", [128, D], F32))
            dw_bf = [f2.enter_context(
                nc.sbuf_tensor(f"dw_bf{i}", [128, 512], BF16)) for i in range(3)]
            dtmp = [f2.enter_context(
                nc.sbuf_tensor(f"dtmp{i}", [128, 512], F32)) for i in range(2)]
            psD = [f2.enter_context(nc.psum_tensor(f"psD{i}", [128, 512], F32))
                   for i in range(8)]

            nc.scalar.dma_start(dsb[:], d_dsb[:]).then_inc(*S.inc("dsbd", 16))
            for qt in range(8):
                nc.scalar.dma_start(res2[:, qt * D:(qt + 1) * D],
                                    d_ydr.ap()[qt * 128:(qt + 1) * 128, :]
                                    ).then_inc(*S.inc("dsbd", 16))
            nc.scalar.wait_ge(S["dsbd"], S.cnt["dsbd"])
            nc.all_engine_barrier()

            f2c_base = S.cnt["f2c"]
            f2m_base = S.cnt["f2mm"]

            f2w_base = [S.cnt["f2w0"], S.cnt["f2w1"], S.cnt["f2w2"]]

            def f2_load(g):
                bi = g % 3
                hs_, k_ = divmod(g, NIT)
                if g >= 3:
                    nc.gpsimd.wait_ge(S["f2mm"], f2m_base + g - 2)
                nc.gpsimd.dma_start(
                    dw_bf[bi][:],
                    d_dwT[k_ * 128:(k_ + 1) * 128,
                          hs_ * 512:(hs_ + 1) * 512]
                ).then_inc(*S.inc(f"f2w{bi}", 16))

            def f2_comp(g):
                bi = g % 3
                hs_, k_ = divmod(g, NIT)
                nc.tensor.wait_ge(S[f"f2w{bi}"],
                                  f2w_base[bi] + 16 * (g // 3 + 1))
                if k_ == 0 and hs_ > 0:
                    nc.tensor.wait_ge(S["f2ev"], hs_)
                lastm = None
                for tt in range(8):
                    hsrc = hdnT_h0 if tt < 4 else hdnT_h1
                    to = (tt % 4) * 128
                    lastm = nc.tensor.matmul(
                        psD[tt][:],
                        hsrc[:, k_ * 512 + to:k_ * 512 + to + 128],
                        dw_bf[bi][:],
                        start=(k_ == 0), stop=(k_ == NIT - 1))
                lastm.then_inc(*S.inc("f2mm"))

            def f2_evac(hs_):
                nc.vector.wait_ge(S["f2mm"], f2m_base + (hs_ + 1) * NIT)
                laste = None
                for tt in range(8):
                    ti = tt % 2
                    nc.vector.tensor_tensor(dtmp[ti][:], psD[tt][:],
                                            dsb[:, hs_ * 512:(hs_ + 1) * 512],
                                            ALU.mult)
                    nc.vector.drain()
                    seg = res2[:, tt * D + hs_ * 512:tt * D + (hs_ + 1) * 512]
                    laste = nc.vector.tensor_tensor(seg, seg, dtmp[ti][:],
                                                    ALU.add)
                    nc.vector.drain()
                laste.then_inc(*S.inc("f2ev"))

            NG = 4 * NIT
            for g0 in range(3):
                f2_load(g0)
            for g in range(NG):
                if g + 3 < NG:
                    f2_load(g + 3)
                f2_comp(g)
                if g % NIT == NIT - 1:
                    f2_evac(g // NIT)
            nc.all_engine_barrier()

            qstat = None
            for qt in range(8):
                rseg = res2[:, qt * D:(qt + 1) * D]
                qstat = nc.scalar.activation(dsb[:], rseg, AF.Square,
                                             accum_out=lt2[:, 8 + qt:9 + qt])
                nc.vector.tensor_reduce(lt2[:, qt:qt + 1], rseg,
                                        mybir.AxisListType.X, ALU.add)
            qstat.then_inc(*S.inc("ln2"))
            nc.vector.drain()
            nc.vector.wait_ge(S["ln2"], S.cnt["ln2"])
            nc.vector.tensor_scalar_mul(lt2[:, 16:24], lt2[:, 0:8], -1.0 / D)
            nc.vector.tensor_scalar_mul(lt2[:, 24:32], lt2[:, 8:16], 1.0 / D)
            nc.vector.drain()
            nc.vector.tensor_tensor(lt2[:, 32:40], lt2[:, 16:24],
                                    lt2[:, 16:24], ALU.mult)
            nc.vector.drain()
            nc.vector.tensor_tensor(lt2[:, 40:48], lt2[:, 24:32],
                                    lt2[:, 32:40], ALU.subtract)
            nc.vector.drain()
            nc.vector.tensor_scalar_add(lt2[:, 48:56], lt2[:, 40:48], EPS
                                        ).then_inc(*S.inc("ln2"))
            nc.scalar.wait_ge(S["ln2"], S.cnt["ln2"])
            nc.scalar.activation(lt2[:, 40:48], lt2[:, 48:56], AF.Ln)
            nc.scalar.drain()
            nc.scalar.activation(lt2[:, 48:56], lt2[:, 40:48], AF.Exp,
                                 scale=-0.5).then_inc(*S.inc("ln2"))
            nc.vector.wait_ge(S["ln2"], S.cnt["ln2"])
            nc.vector.tensor_tensor(lt2[:, 56:64], lt2[:, 16:24],
                                    lt2[:, 48:56], ALU.mult
                                    ).then_inc(*S.inc("ln2"))
            nc.scalar.wait_ge(S["ln2"], S.cnt["ln2"])
            for qt in range(8):
                rseg = res2[:, qt * D:(qt + 1) * D]
                nc.scalar.activation(rseg, rseg, AF.Identity,
                                     bias=lt2[:, 56 + qt:57 + qt],
                                     scale=lt2[:, 48 + qt:49 + qt]
                                     ).then_inc(*S.inc("ln2"))
                nc.sync.wait_ge(S["ln2"], S.cnt["ln2"])
                nc.sync.dma_start(d_out[qt * 128:(qt + 1) * 128, :], rseg
                                  ).then_inc(*S.inc("outd", 16))
            nc.sync.wait_ge(S["outd"], S.cnt["outd"])

        ds_stack.close()

    return nc, sorted(dbg_outs)


def _rope_tables(t0, n):
    inv = 10000.0 ** (-(np.arange(32, dtype=np.float64) * 2) / 64.0)
    p = np.arange(128)
    t = np.arange(t0, t0 + n, dtype=np.float64)
    ang = t[None, :] * inv[p % 32][:, None]
    cos = np.cos(ang).astype(np.float32)
    sin = np.sin(ang).astype(np.float32)
    sgn = np.where((p % 64) < 32, -1.0, 1.0).astype(np.float32)
    return cos, np.ascontiguousarray(sin * sgn[:, None]).astype(np.float32)


def _prep_inputs(inp):
    c = np.ascontiguousarray

    def _blk(wT, nk):
        a = wT.reshape(2, 8, 128, nk, 128)
        return c(a.transpose(3, 2, 0, 1, 4).reshape(nk * 128, 2048))

    x = np.asarray(inp["L0_attn_x"], dtype=np.float32)
    cosK, sinK = _rope_tables(0, T)
    base = {
        "wqT": c(np.asarray(inp["L0_attn_Wq"]).T),
        "wkT": c(np.asarray(inp["L0_attn_Wk"]).T),
        "wvT": c(np.asarray(inp["L0_attn_Wv"]).T),
        "woT": c(np.asarray(inp["L0_attn_Wo"]).T),
        "bqt": c(np.asarray(inp["L0_attn_bq"]).reshape(16, 128).T),
        "bkt": c(np.asarray(inp["L0_attn_bk"]).reshape(4, 128).T),
        "bvb": c(np.broadcast_to(np.asarray(inp["L0_attn_bv"])[None, :],
                                 (128, 512))),
        "bob": c(np.broadcast_to(np.asarray(inp["L0_attn_bo"])[None, :],
                                 (128, D))),
        "gwT": c(np.asarray(inp["L0_ffn_gate_w"]).T),
        "uwT": c(np.asarray(inp["L0_ffn_up_w"]).T),
        "dwT": c(np.asarray(inp["L0_ffn_down_w"]).T),
        "gst": c(np.asarray(inp["L0_ffn_gate_s"]).reshape(NIT, 128).T),
        "ust": c(np.asarray(inp["L0_ffn_up_s"]).reshape(NIT, 128).T),
        "dsb": c(np.broadcast_to(np.asarray(inp["L0_ffn_down_s"])[None, :],
                                 (128, D))),
        "cosK": cosK, "sinK": sinK,
        "identf": np.eye(128, dtype=np.float32),
    }
    xT = [c(x[b].T) for b in range(B)]
    maps = []
    for core in range(N_CORES):
        b = core // 4
        q0 = (core % 4) * TOK
        cosQ, sinQ = _rope_tables(q0, TOK)
        m = dict(base)
        m["xT"] = xT[b]
        m["xqT"] = c(xT[b][:, q0:q0 + TOK])
        m["xres"] = c(x[b, q0:q0 + TOK])
        m["cosQ"] = cosQ
        m["sinQ"] = sinQ
        maps.append(m)
    return maps


def kernel(**inputs):
    try:
        from concourse.compiler_utils import (get_compiler_flags,
                                              set_compiler_flags)
        set_compiler_flags([
            f.replace("--enable-ldw-opt=false", "--enable-ldw-opt=true")
            for f in get_compiler_flags()])
    except Exception:
        pass
    maps = _prep_inputs(inputs)
    nc, _ = build_program()
    res = run_bass_kernel_spmd(nc, maps, core_ids=list(range(N_CORES)))
    out = np.empty((B, T, D), np.float32)
    for core in range(N_CORES):
        b = core // 4
        q0 = (core % 4) * TOK
        out[b, q0:q0 + TOK] = np.asarray(res.results[core]["out"])
    return out


# version 20
# speedup vs baseline: 1.4723x; 1.2684x over previous
import contextlib
import numpy as np
import ml_dtypes
import concourse.bass as bass
import concourse.mybir as mybir
from concourse.bass_utils import run_bass_kernel_spmd

F32 = mybir.dt.float32
BF16 = mybir.dt.bfloat16
AF = mybir.ActivationFunctionType
ALU = mybir.AluOpType

B, T, D = 2, 4096, 2048
HQ, HKV, HD = 32, 8, 64
TOK = 1024
INTER = 5632
NIT = INTER // 128
NTC = T // 512
NKT = T // 128
C_SHIFT = 10.0
EPS = 1e-5
N_CORES = 8


class Sems:
    def __init__(self, nc, es):
        self.nc = nc
        self.es = es
        self.cnt = {}
        self.h = {}

    def new(self, *names):
        for n in names:
            self.h[n] = self.es.enter_context(self.nc.semaphore(n))
            self.cnt[n] = 0

    def inc(self, name, n=1):
        self.cnt[name] += n
        return self.h[name], n

    def __getitem__(self, name):
        return self.h[name]


def build_program(debug=None):
    nc = bass.Bass("TRN2", target_bir_lowering=False)

    d_xT = nc.dram_tensor("xT", [D, T], F32, kind="ExternalInput")
    d_xqT = nc.dram_tensor("xqT", [D, TOK], F32, kind="ExternalInput")
    d_xres = nc.dram_tensor("xres", [TOK, D], F32, kind="ExternalInput").ap()
    d_wqT = nc.dram_tensor("wqT", [D, D], F32, kind="ExternalInput")
    d_wkT = nc.dram_tensor("wkT", [D, 512], F32, kind="ExternalInput")
    d_wvT = nc.dram_tensor("wvT", [D, 512], F32, kind="ExternalInput")
    d_woT = nc.dram_tensor("woT", [D, D], F32, kind="ExternalInput").ap()
    d_bqt = nc.dram_tensor("bqt", [128, 16], F32, kind="ExternalInput").ap()
    d_bkt = nc.dram_tensor("bkt", [128, 4], F32, kind="ExternalInput").ap()
    d_bvb = nc.dram_tensor("bvb", [128, 512], F32, kind="ExternalInput").ap()
    d_bob = nc.dram_tensor("bob", [128, D], F32, kind="ExternalInput").ap()
    d_gwT = nc.dram_tensor("gwT", [D, INTER], F32, kind="ExternalInput")
    d_uwT = nc.dram_tensor("uwT", [D, INTER], F32, kind="ExternalInput")
    d_dwT = nc.dram_tensor("dwT", [INTER, D], F32, kind="ExternalInput").ap()
    d_gst = nc.dram_tensor("gst", [128, NIT], F32, kind="ExternalInput").ap()
    d_ust = nc.dram_tensor("ust", [128, NIT], F32, kind="ExternalInput").ap()
    d_dsb = nc.dram_tensor("dsb", [128, D], F32, kind="ExternalInput").ap()
    d_cosK = nc.dram_tensor("cosK", [128, T], F32, kind="ExternalInput").ap()
    d_sinK = nc.dram_tensor("sinK", [128, T], F32, kind="ExternalInput").ap()
    d_cosQ = nc.dram_tensor("cosQ", [128, TOK], F32, kind="ExternalInput").ap()
    d_sinQ = nc.dram_tensor("sinQ", [128, TOK], F32, kind="ExternalInput").ap()
    d_idb = nc.dram_tensor("identb", [128, 128], BF16, kind="ExternalInput").ap()
    d_out = nc.dram_tensor("out", [TOK, D], F32, kind="ExternalOutput").ap()
    d_zb = nc.dram_tensor("zb", [64, 1024], F32)
    d_ydr = nc.dram_tensor("ydr", [TOK, D], F32)

    dbg_outs = {}

    def dbg(name, shape, dt):
        dbg_outs[name] = nc.dram_tensor(name, shape, dt,
                                        kind="ExternalOutput").ap()
        return dbg_outs[name]

    es = contextlib.ExitStack()
    S = Sems(nc, es)
    S.new("x", "xb", "mmKV", "kev", "vev", "sh", "rope",
          "wld", "wcast", "mmQ", "qev", "qsh", "qrope",
          "psS", "psE", "psA", "zr", "zbd", "bc", "aev",
          "wo_x", "wo_mm", "wo_ev", "ln1", "ysp", "f1c", "f1mm", "f1ev",
          "f1w", "f1wc", "gu_mm", "gu_ev", "hdn",
          "f2w", "f2c", "f2mm", "f2ev", "ln2", "outd", "misc",
          "x0", "x1", "x2", "x3", "wq0", "wq1", "wox0", "wox1", "f1w0", "f1w1",
          "f2w0", "f2w1", "f2w2", "zb0", "zb1", "bc0", "bc1", "ysp0", "ysp1",
          "pre", "tabK", "tabQ", "bobd", "dsbd", "wo0", "wo1",
          "ropev", "qropev", "xq0", "xq1", "xqc", "ropea", "tab0", "tab1")

    with es:
        bvb = es.enter_context(nc.sbuf_tensor("s_bvb", [128, 512], F32))
        bkt = es.enter_context(nc.sbuf_tensor("s_bkt", [128, 4], F32))
        bqt = es.enter_context(nc.sbuf_tensor("s_bqt", [128, 16], F32))
        gst = es.enter_context(nc.sbuf_tensor("s_gst", [128, NIT], F32))
        ust = es.enter_context(nc.sbuf_tensor("s_ust", [128, NIT], F32))
        cneg = es.enter_context(nc.sbuf_tensor("cneg", [128, 1], F32))
        epsp = es.enter_context(nc.sbuf_tensor("epsp", [128, 1], F32))
        ident = es.enter_context(nc.sbuf_tensor("ident", [128, 128], BF16))
        lt = es.enter_context(nc.sbuf_tensor("lt", [128, 8], F32))

        for ap_d, t_s in ((d_bvb, bvb), (d_bkt, bkt), (d_bqt, bqt),
                          (d_gst, gst), (d_ust, ust), (d_idb, ident)):
            nc.scalar.dma_start(t_s[:], ap_d[:]).then_inc(*S.inc("pre", 16))
        nc.vector.memset(cneg[:], -C_SHIFT)
        nc.vector.memset(epsp[:], EPS)

        kv_stack = contextlib.ExitStack()
        xq_right = contextlib.ExitStack()
        xq_bf = xq_right.enter_context(
            nc.sbuf_tensor("xq_bf", [128, 16 * TOK], BF16, side="right"))
        KT = kv_stack.enter_context(nc.sbuf_tensor("KT", [128, 4 * T], BF16))
        Vp = kv_stack.enter_context(
            nc.sbuf_tensor("Vp", [128, HKV * NKT * 65], BF16))
        nc.vector.memset(
            bass.AP(Vp, 64, [[HKV * NKT * 65, 128], [65, HKV * NKT]]), 1.0)

        with contextlib.ExitStack() as pa1:
            cosb = [pa1.enter_context(
                nc.sbuf_tensor(f"cosb{i}", [128, 512], F32)) for i in range(2)]
            sinb = [pa1.enter_context(
                nc.sbuf_tensor(f"sinb{i}", [128, 512], F32)) for i in range(2)]
            wk_bf = pa1.enter_context(nc.sbuf_tensor("wk_bf", [128, 8192], BF16))
            wv_bf = pa1.enter_context(nc.sbuf_tensor("wv_bf", [128, 8192], BF16))
            Kstg = pa1.enter_context(nc.sbuf_tensor("Kstg", [128, 2048], F32))
            Ksh = pa1.enter_context(nc.sbuf_tensor("Ksh", [128, 2048], F32))
            xq_st = [pa1.enter_context(
                nc.sbuf_tensor(f"xq_st{i}", [128, TOK], F32)) for i in range(2)]
            xbf = [pa1.enter_context(nc.sbuf_tensor(f"xbf{i}", [128, 512], F32))
                   for i in range(4)]
            xbb = [pa1.enter_context(nc.sbuf_tensor(f"xbb{i}", [128, 512], BF16))
                   for i in range(4)]
            kt1 = [pa1.enter_context(nc.sbuf_tensor(f"kt1{i}", [128, 512], F32))
                   for i in range(2)]
            kt2 = [pa1.enter_context(nc.sbuf_tensor(f"kt2{i}", [128, 512], F32))
                   for i in range(2)]
            psK = [pa1.enter_context(nc.psum_tensor(f"psK{i}", [128, 512], F32))
                   for i in range(4)]
            psV = [pa1.enter_context(nc.psum_tensor(f"psV{i}", [128, 512], F32))
                   for i in range(4)]


            with nc.sbuf_tensor("wst", [128, 4096], F32) as wst:
                for dsrc, bfdst in ((d_wkT, wk_bf), (d_wvT, wv_bf)):
                    for half in range(2):
                        src = bass.AP(dsrc, half * 8 * 128 * 512,
                                      [[512, 128], [128 * 512, 8], [1, 512]])
                        dst = bass.AP(wst, 0, [[4096, 128], [512, 8], [1, 512]])
                        if S.cnt["wcast"] > 0:
                            nc.sync.wait_ge(S["wcast"], S.cnt["wcast"])
                        h_, _ = S.inc("wld", 16); c_ = S.cnt["wld"]
                        nc.sync.dma_start(dst, src).then_inc(h_, 16)
                        nc.vector.wait_ge(S["wld"], c_)
                        nc.vector.tensor_copy(
                            bfdst[:, half * 4096:(half + 1) * 4096], wst[:]
                        ).then_inc(*S.inc("wcast"))
                nc.all_engine_barrier()

            xb_b1 = S.cnt["xb"]

            def kv_load(k):
                bi = k % 4
                tc_, d_ = divmod(k, 16)
                if k >= 4:
                    nc.sync.wait_ge(S["xb"], xb_b1 + k - 3)
                nc.sync.dma_start(
                    xbf[bi][:],
                    d_xT.ap()[d_ * 128:(d_ + 1) * 128,
                              tc_ * 512:(tc_ + 1) * 512],
                ).then_inc(*S.inc(f"x{bi}", 16))

            def kv_cast(k):
                bi = k % 4
                nc.vector.wait_ge(S[f"x{bi}"], S.cnt[f"x{bi}"])
                if k >= 4:
                    nc.vector.wait_ge(S["mmKV"], k - 3)
                nc.vector.tensor_copy(xbb[bi][:], xbf[bi][:]).then_inc(
                    *S.inc("xb"))

            def kv_mm(k):
                bi = k % 4
                tc_, d_ = divmod(k, 16)
                nc.tensor.wait_ge(S["xb"], xb_b1 + k + 1)
                if tc_ > 0 and d_ == 0:
                    nc.tensor.wait_ge(S["kev"], tc_)
                    nc.tensor.wait_ge(S["vev"], tc_)
                for m in range(4):
                    nc.tensor.matmul(
                        psK[m][:],
                        wk_bf[:, d_ * 512 + m * 128:d_ * 512 + (m + 1) * 128],
                        xbb[bi][:], start=(d_ == 0), stop=(d_ == 15))
                mm = None
                for tt in range(4):
                    mm = nc.tensor.matmul(
                        psV[tt][:], xbb[bi][:, tt * 128:(tt + 1) * 128],
                        wv_bf[:, d_ * 512:(d_ + 1) * 512],
                        start=(d_ == 0), stop=(d_ == 15))
                mm.then_inc(*S.inc("mmKV"))

            def kv_evac_rope(tc_):
                nc.scalar.wait_ge(S["mmKV"], (tc_ + 1) * 16)
                if tc_ == 0:
                    nc.scalar.wait_ge(S["pre"], 96)
                else:
                    nc.scalar.wait_ge(S["ropev"], 4 * tc_)
                    nc.scalar.wait_ge(S["sh"], 64 * tc_)
                act = None
                for m in range(4):
                    act = nc.scalar.activation(
                        Kstg[:, m * 512:(m + 1) * 512], psK[m][:], AF.Identity,
                        bias=bkt[:, m:m + 1], scale=1.0)
                act.then_inc(*S.inc("kev"))
                nc.scalar.wait_ge(S["kev"], tc_ + 1)
                for gg in range(2):
                    for hswap in range(2):
                        so, do_ = (32, 0) if hswap == 0 else (0, 32)
                        nc.scalar.dma_start(
                            Ksh[gg * 64 + do_:gg * 64 + do_ + 32, :],
                            Kstg[gg * 64 + so:gg * 64 + so + 32, :],
                        ).then_inc(*S.inc("sh", 16))
                nc.vector.wait_ge(S["mmKV"], (tc_ + 1) * 16)
                if tc_ == 0:
                    nc.vector.wait_ge(S["pre"], 96)
                tt_last = None
                for tt in range(4):
                    kt_g = tc_ * 4 + tt
                    dst = bass.AP(Vp, kt_g * 65,
                                  [[HKV * NKT * 65, 128], [2080, HKV], [1, 64]])
                    src = bass.AP(psV[tt], 0, [[512, 128], [64, HKV], [1, 64]])
                    tt_last = nc.vector.tensor_tensor(dst, src, bvb[:], ALU.add)
                tt_last.then_inc(*S.inc("vev"))
                nc.vector.wait_ge(S["sh"], 64 * (tc_ + 1))
                nc.vector.wait_ge(S[f"tab{tc_ % 2}"], S.cnt[f"tab{tc_ % 2}"])
                for m in range(4):
                    gm = 4 * tc_ + m
                    if gm >= 2:
                        nc.vector.wait_ge(S["ropea"], gm - 1)
                    nc.vector.tensor_tensor(
                        kt1[m % 2][:], Ksh[:, m * 512:(m + 1) * 512],
                        sinb[tc_ % 2][:], ALU.mult)
                    nc.vector.tensor_tensor(
                        kt2[m % 2][:], Kstg[:, m * 512:(m + 1) * 512],
                        cosb[tc_ % 2][:], ALU.mult
                    ).then_inc(*S.inc("ropev"))
                last = None
                for m in range(4):
                    gm = 4 * tc_ + m
                    nc.gpsimd.wait_ge(S["ropev"], gm + 1)
                    last = nc.gpsimd.tensor_tensor(
                        KT[:, m * T + tc_ * 512:m * T + (tc_ + 1) * 512],
                        kt1[m % 2][:], kt2[m % 2][:], ALU.add
                    ).then_inc(*S.inc("ropea"))

            def tab_load(tc_):
                bi = tc_ % 2
                if tc_ >= 2:
                    nc.gpsimd.wait_ge(S["ropev"], 4 * (tc_ - 1))
                nc.gpsimd.dma_start(cosb[bi][:],
                                    d_cosK[:, tc_ * 512:(tc_ + 1) * 512]
                                    ).then_inc(*S.inc(f"tab{bi}", 16))
                nc.gpsimd.dma_start(sinb[bi][:],
                                    d_sinK[:, tc_ * 512:(tc_ + 1) * 512]
                                    ).then_inc(*S.inc(f"tab{bi}", 16))

            def xq_load(dq):
                bi = dq % 2
                if dq >= 2:
                    nc.gpsimd.wait_ge(S["xqc"], dq - 1)
                nc.gpsimd.dma_start(xq_st[bi][:],
                                    d_xqT.ap()[dq * 128:(dq + 1) * 128, :]
                                    ).then_inc(*S.inc(f"xq{bi}", 16))

            def xq_cast(dq):
                bi = dq % 2
                nc.vector.wait_ge(S[f"xq{bi}"], S.cnt[f"xq{bi}"])
                nc.vector.tensor_copy(xq_bf[:, dq * TOK:(dq + 1) * TOK],
                                      xq_st[bi][:]).then_inc(*S.inc("xqc"))

            tab_load(0)
            tab_load(1)
            for k in range(4):
                kv_load(k)
                kv_cast(k)
            xq_load(0)
            xq_load(1)
            for g in range(NTC * 16):
                tc, d = divmod(g, 16)
                if g + 4 < NTC * 16:
                    kv_load(g + 4)
                    kv_cast(g + 4)
                kv_mm(g)
                if d == 7 and tc < 8:
                    xq_cast(2 * tc)
                    if 2 * tc + 2 < 16:
                        xq_load(2 * tc + 2)
                if d == 15:
                    kv_evac_rope(tc)
                    if tc + 2 < NTC:
                        tab_load(tc + 2)
                    xq_cast(2 * tc + 1)
                    if 2 * tc + 3 < 16:
                        xq_load(2 * tc + 3)
            nc.all_engine_barrier()
            if debug == "pa1":
                o1 = dbg("dbg_KT", [128, 4 * T], BF16)
                o2 = dbg("dbg_Vp", [128, HKV * NKT * 65], BF16)
                nc.sync.dma_start(o1[:], KT[:]).then_inc(*S.inc("outd", 16))
                nc.sync.dma_start(o2[:], Vp[:]).then_inc(*S.inc("outd", 16))
                nc.sync.wait_ge(S["outd"], S.cnt["outd"])
                nc.all_engine_barrier()

        q_stack = contextlib.ExitStack()
        QT2 = q_stack.enter_context(nc.sbuf_tensor("QT2", [128, 16 * TOK], BF16))
        with contextlib.ExitStack() as pa2:
            cosQ = pa2.enter_context(nc.sbuf_tensor("s_cosQ", [128, TOK], F32))
            sinQ = pa2.enter_context(nc.sbuf_tensor("s_sinQ", [128, TOK], F32))
            wqs_st = [pa2.enter_context(
                nc.sbuf_tensor(f"wqs_st{i}", [128, 2048], F32)) for i in range(2)]
            wqs_bf = [pa2.enter_context(
                nc.sbuf_tensor(f"wqs_bf{i}", [128, 2048], BF16)) for i in range(2)]
            Qstg = [pa2.enter_context(
                nc.sbuf_tensor(f"Qstg{i}", [128, TOK], F32)) for i in range(2)]
            Qsh = [pa2.enter_context(
                nc.sbuf_tensor(f"Qsh{i}", [128, TOK], F32)) for i in range(2)]
            qt1 = [pa2.enter_context(nc.sbuf_tensor(f"qt1{i}", [128, TOK], F32))
                   for i in range(2)]
            qt2 = [pa2.enter_context(nc.sbuf_tensor(f"qt2{i}", [128, TOK], F32))
                   for i in range(2)]
            psQ = [pa2.enter_context(nc.psum_tensor(f"psQ{i}", [128, 512], F32))
                   for i in range(8)]

            nc.scalar.dma_start(cosQ[:], d_cosQ[:]).then_inc(*S.inc("tabQ", 16))
            nc.scalar.dma_start(sinQ[:], d_sinQ[:]).then_inc(*S.inc("tabQ", 16))
            wc_base = S.cnt["wcast"]
            mq_base = S.cnt["mmQ"]

            def q_load(j):
                bi = j % 2
                if j >= 2:
                    nc.sync.wait_ge(S["wcast"], wc_base + j - 1)
                for half in range(2):
                    src = bass.AP(d_wqT, half * 8 * 128 * D + j * 128,
                                  [[D, 128], [128 * D, 8], [1, 128]])
                    dst = bass.AP(wqs_st[bi], half * 1024,
                                  [[2048, 128], [128, 8], [1, 128]])
                    nc.sync.dma_start(dst, src).then_inc(*S.inc(f"wq{bi}", 16))

            def q_cast(j):
                bi = j % 2
                nc.vector.wait_ge(S[f"wq{bi}"], S.cnt[f"wq{bi}"])
                if j >= 2:
                    nc.vector.wait_ge(S["mmQ"], mq_base + j - 1)
                nc.vector.tensor_copy(wqs_bf[bi][:], wqs_st[bi][:]
                                      ).then_inc(*S.inc("wcast"))

            def q_comp(j):
                bi = j % 2
                sbi = j % 2
                nc.tensor.wait_ge(S["wcast"], wc_base + j + 1)
                if j == 0:
                    nc.tensor.wait_ge(S["xqc"], 16)
                pA, pB = psQ[(j % 4) * 2], psQ[(j % 4) * 2 + 1]
                if j >= 4:
                    nc.tensor.wait_ge(S["qev"], j - 3)
                mmq = None
                for d in range(16):
                    nc.tensor.matmul(pA[:],
                                     wqs_bf[bi][:, d * 128:(d + 1) * 128],
                                     xq_bf[:, d * TOK:d * TOK + 512],
                                     start=(d == 0), stop=(d == 15))
                    mmq = nc.tensor.matmul(
                        pB[:], wqs_bf[bi][:, d * 128:(d + 1) * 128],
                        xq_bf[:, d * TOK + 512:(d + 1) * TOK],
                        start=(d == 0), stop=(d == 15))
                mmq.then_inc(*S.inc("mmQ"))
                nc.scalar.wait_ge(S["mmQ"], mq_base + j + 1)
                if j == 0:
                    nc.scalar.wait_ge(S["pre"], 96)
                if j >= 2:
                    nc.scalar.wait_ge(S["qrope"], j - 1)
                nc.scalar.activation(Qstg[sbi][:, 0:512], pA[:], AF.Identity,
                                     bias=bqt[:, j:j + 1], scale=1.0)
                nc.scalar.activation(Qstg[sbi][:, 512:1024], pB[:],
                                     AF.Identity, bias=bqt[:, j:j + 1],
                                     scale=1.0).then_inc(*S.inc("qev"))
                nc.scalar.wait_ge(S["qev"], j + 1)
                if j >= 1:
                    nc.scalar.wait_ge(S["qrope"], j)
                for gg in range(2):
                    for hswap in range(2):
                        so, do_ = (32, 0) if hswap == 0 else (0, 32)
                        nc.scalar.dma_start(
                            Qsh[sbi][gg * 64 + do_:gg * 64 + do_ + 32, :],
                            Qstg[sbi][gg * 64 + so:gg * 64 + so + 32, :],
                        ).then_inc(*S.inc("qsh", 16))
                nc.vector.wait_ge(S["qsh"], S.cnt["qsh"])
                if j == 0:
                    nc.vector.wait_ge(S["tabQ"], 32)
                if j >= 2:
                    nc.vector.wait_ge(S["qrope"], j - 1)
                nc.vector.tensor_tensor(qt1[sbi][:], Qsh[sbi][:], sinQ[:],
                                        ALU.mult)
                nc.vector.tensor_tensor(qt2[sbi][:], Qstg[sbi][:], cosQ[:],
                                        ALU.mult).then_inc(*S.inc("qropev"))
                nc.gpsimd.wait_ge(S["qropev"], j + 1)
                last = None
                for H, rsrc in ((2 * j, 0), (2 * j + 1, 64)):
                    blk = 4 * (H // 8) + (H % 4)
                    r0 = 0 if (H % 8) < 4 else 64
                    last = nc.gpsimd.tensor_tensor(
                        QT2[r0:r0 + 64, blk * TOK:(blk + 1) * TOK],
                        qt1[sbi][rsrc:rsrc + 64, :], qt2[sbi][rsrc:rsrc + 64, :],
                        ALU.add)
                last.then_inc(*S.inc("qrope"))

            q_load(0)
            q_cast(0)
            q_load(1)
            q_cast(1)
            for j in range(16):
                if j + 2 < 16:
                    q_load(j + 2)
                    q_cast(j + 2)
                q_comp(j)
            nc.all_engine_barrier()
            if debug == "pa2":
                o1 = dbg("dbg_QT2", [128, 16 * TOK], BF16)
                nc.sync.dma_start(o1[:], QT2[:]).then_inc(*S.inc("outd", 16))
                nc.sync.wait_ge(S["outd"], S.cnt["outd"])
                nc.all_engine_barrier()

        xq_right.close()
        attn_right = contextlib.ExitStack()
        attnT = attn_right.enter_context(
            nc.sbuf_tensor("attnT", [128, 16 * TOK], BF16, side="right"))
        woA = attn_right.enter_context(
            nc.sbuf_tensor("woA", [128, 10 * D], BF16, side="right"))
        with contextlib.ExitStack() as psx:
            wo_st = [psx.enter_context(
                nc.sbuf_tensor(f"wo_st{i}", [128, 1024], F32)) for i in range(2)]
            Sps = [psx.enter_context(nc.psum_tensor(f"Sps{i}", [128, 1024], F32))
                   for i in range(2)]
            Aps = [psx.enter_context(nc.psum_tensor(f"Aps{i}", [65, 512], F32))
                   for i in range(4)]
            wT = [psx.enter_context(nc.sbuf_tensor(f"wT{i}", [128, 1024], BF16))
                  for i in range(4)]
            zrec = [psx.enter_context(nc.sbuf_tensor(f"zrec{i}", [1, 1024], F32))
                    for i in range(2)]
            bcs = [psx.enter_context(nc.sbuf_tensor(f"bcs{i}", [128, 1024], F32))
                   for i in range(2)]

            seq = [(m, i, qb) for m in range(4) for i in range(4)
                   for qb in range(2)]

            def ps_S(itx, kt):
                m, i, qb = seq[itx]
                blk = 4 * m + i
                sbi = kt % 2
                wbi = kt % 4
                g = itx * NKT + kt
                if g >= 2:
                    nc.tensor.wait_ge(S["psE"], g - 1)
                nc.tensor.matmul(
                    Sps[sbi][:, 0:512],
                    KT[0:64, m * T + kt * 128:m * T + (kt + 1) * 128],
                    QT2[0:64, blk * TOK + qb * 512:blk * TOK + (qb + 1) * 512],
                    start=True, stop=True, tile_position=(0, 0))
                nc.tensor.matmul(
                    Sps[sbi][:, 512:1024],
                    KT[64:128, m * T + kt * 128:m * T + (kt + 1) * 128],
                    QT2[64:128, blk * TOK + qb * 512:blk * TOK + (qb + 1) * 512],
                    start=True, stop=True, tile_position=(64, 0)
                ).then_inc(*S.inc("psS"))
                nc.scalar.wait_ge(S["psS"], g + 1)
                if g >= 4:
                    nc.scalar.wait_ge(S["psA"], g - 3)
                nc.scalar.activation(
                    wT[wbi][:], Sps[sbi][:], AF.Exp,
                    bias=cneg[:], scale=0.125).then_inc(*S.inc("psE"))

            def ps_AV(itx, kt):
                m, i, qb = seq[itx]
                kve, kvo = 2 * m, 2 * m + 1
                aset = itx % 2
                A0, A1 = Aps[aset * 2], Aps[aset * 2 + 1]
                wbi = kt % 4
                g = itx * NKT + kt
                if kt == 0 and itx >= 2:
                    nc.tensor.wait_ge(S["aev"], itx - 1)
                nc.tensor.wait_ge(S["psE"], g + 1)
                nc.tensor.matmul(
                    A0[:],
                    Vp[:, kve * 2080 + kt * 65:kve * 2080 + kt * 65 + 65],
                    wT[wbi][:, 0:512],
                    start=(kt == 0), stop=(kt == NKT - 1))
                nc.tensor.matmul(
                    A1[:],
                    Vp[:, kvo * 2080 + kt * 65:kvo * 2080 + kt * 65 + 65],
                    wT[wbi][:, 512:1024],
                    start=(kt == 0), stop=(kt == NKT - 1)
                ).then_inc(*S.inc("psA"))

            def ps_epilogue(itx):
                m, i, qb = seq[itx]
                He, Ho = 8 * m + i, 8 * m + 4 + i
                aset = itx % 2
                A0, A1 = Aps[aset * 2], Aps[aset * 2 + 1]
                bset = itx % 2
                nc.vector.wait_ge(S["psA"], (itx + 1) * NKT)
                if itx >= 2:
                    nc.vector.wait_ge(S[f"zb{bset}"], S.cnt[f"zb{bset}"])
                nc.vector.reciprocal(zrec[bset][0:1, 0:512], A0[64:65, :])
                nc.vector.reciprocal(zrec[bset][0:1, 512:1024],
                                     A1[64:65, :]).then_inc(*S.inc("zr"))
                nc.gpsimd.wait_ge(S["zr"], S.cnt["zr"])
                h_, _ = S.inc(f"zb{bset}", 16); c_ = S.cnt[f"zb{bset}"]
                nc.gpsimd.dma_start(
                    bass.AP(d_zb, itx * 1024, [[1024, 1], [1, 1024]]),
                    zrec[bset][0:1, :]).then_inc(h_, 16)
                nc.gpsimd.wait_ge(S[f"zb{bset}"], c_)
                if itx >= 2:
                    nc.gpsimd.wait_ge(S["aev"], itx - 1)
                h_, _ = S.inc(f"bc{bset}", 16); c_ = S.cnt[f"bc{bset}"]
                nc.gpsimd.dma_start(
                    bcs[bset][:],
                    bass.AP(d_zb, itx * 1024, [[0, 128], [1, 1024]]),
                ).then_inc(h_, 16)
                nc.vector.wait_ge(S[f"bc{bset}"], c_)
                lastev = None
                for H, A_, co in ((He, A0, 0), (Ho, A1, 512)):
                    bj = H // 2
                    r0 = (H % 2) * 64
                    lastev = nc.vector.tensor_tensor(
                        attnT[r0:r0 + 64,
                              bj * TOK + qb * 512:bj * TOK + (qb + 1) * 512],
                        A_[0:64, :], bcs[bset][0:64, co:co + 512], ALU.mult)
                lastev.then_inc(*S.inc("aev"))

            woc_base = S.cnt["wcast"]

            def wo_preload(k):
                bi = k % 2
                j, half = divmod(k, 2)
                if k >= 2:
                    nc.sync.wait_ge(S["wcast"], woc_base + k - 1)
                nc.sync.dma_start(
                    wo_st[bi][:],
                    d_woT[j * 128:(j + 1) * 128,
                          half * 1024:(half + 1) * 1024]
                ).then_inc(*S.inc(f"wo{bi}", 16))
                nc.vector.wait_ge(S[f"wo{bi}"], S.cnt[f"wo{bi}"])
                nc.vector.tensor_copy(
                    woA[:, j * D + half * 1024:j * D + (half + 1) * 1024],
                    wo_st[bi][:]).then_inc(*S.inc("wcast"))

            GPS = len(seq) * NKT
            ps_S(0, 0)
            ps_S(0, 1)
            for g in range(GPS):
                itx, kt = divmod(g, NKT)
                ps_AV(itx, kt)
                if g + 2 < GPS:
                    ps_S((g + 2) // NKT, (g + 2) % NKT)
                if kt == NKT - 1:
                    ps_epilogue(itx)
                    if itx < 10:
                        wo_preload(2 * itx)
                        wo_preload(2 * itx + 1)
            nc.all_engine_barrier()
            if debug == "ps":
                o1 = dbg("dbg_attnT", [128, 16 * TOK], BF16)
                nc.sync.dma_start(o1[:], attnT[:]).then_inc(*S.inc("outd", 16))
                nc.sync.wait_ge(S["outd"], S.cnt["outd"])
                nc.all_engine_barrier()

        q_stack.close()
        kv_stack.close()

        yt_stack = contextlib.ExitStack()
        yT = yt_stack.enter_context(nc.sbuf_tensor("yT", [128, 16 * TOK], BF16))
        with contextlib.ExitStack() as wo:
            woB = wo.enter_context(nc.sbuf_tensor("woB", [128, 6 * D], BF16))
            with contextlib.ExitStack() as pre:
                wo_st2 = [pre.enter_context(
                    nc.sbuf_tensor(f"wo_st2{i}", [128, 1024], F32))
                    for i in range(2)]
                wc_base = S.cnt["wcast"]
                for k in range(12):
                    bi = k % 2
                    j, half = divmod(k, 2)
                    if k >= 2:
                        nc.sync.wait_ge(S["wcast"], wc_base + k - 1)
                    nc.sync.dma_start(
                        wo_st2[bi][:],
                        d_woT[(10 + j) * 128:(11 + j) * 128,
                              half * 1024:(half + 1) * 1024]
                    ).then_inc(*S.inc(f"wo{bi}", 16))
                    nc.vector.wait_ge(S[f"wo{bi}"], S.cnt[f"wo{bi}"])
                    nc.vector.tensor_copy(
                        woB[:, j * D + half * 1024:j * D + (half + 1) * 1024],
                        wo_st2[bi][:]).then_inc(*S.inc("wcast"))
                nc.vector.drain().then_inc(*S.inc("wcast"))
                woB_target = S.cnt["wcast"]
            bob = wo.enter_context(nc.sbuf_tensor("s_bob", [128, D], F32))
            xr = [wo.enter_context(nc.sbuf_tensor(f"xr{i}", [128, D], F32))
                  for i in range(2)]
            wtmp = [wo.enter_context(nc.sbuf_tensor(f"wtmp{i}", [128, 512], F32))
                    for i in range(4)]
            ytile = [wo.enter_context(nc.sbuf_tensor(f"ytile{i}", [128, D], F32))
                     for i in range(2)]
            ybf = [wo.enter_context(nc.sbuf_tensor(f"ybf{i}", [128, D], BF16))
                   for i in range(2)]
            junk = wo.enter_context(nc.sbuf_tensor("junk", [128, D], F32))
            psW = [wo.enter_context(nc.psum_tensor(f"psW{i}", [128, 512], F32))
                   for i in range(8)]

            nc.scalar.wait_ge(S["wcast"], woB_target)
            nc.scalar.dma_start(bob[:], d_bob[:]).then_inc(*S.inc("bobd", 16))
            nc.scalar.wait_ge(S["bobd"], 16)
            nc.all_engine_barrier()

            def emit_transposes(qt):
                par = qt % 2
                ws = par * 4
                if qt == 0:
                    nc.tensor.wait_ge(S["pre"], 96)
                nc.tensor.wait_ge(S["f1c"], qt + 1)
                lastt = None
                for k in range(16):
                    kk = k % 8
                    pv = bass.AP(psW[ws + k // 8], kk * 64,
                                 [[512, 128], [1, 64]]).bitcast(BF16)
                    lastt = nc.tensor.transpose(
                        pv, ybf[par][:, k * 128:(k + 1) * 128], ident[:])
                lastt.then_inc(*S.inc("f1mm"))
                nc.vector.wait_ge(S["f1mm"], S.cnt["f1mm"])
                for half in range(2):
                    pv = bass.AP(psW[ws + half], 0,
                                 [[512, 128], [1, 512]]).bitcast(BF16)
                    dst = bass.AP(yT, (half * 8) * TOK + qt * 128,
                                  [[16 * TOK, 128], [TOK, 8], [1, 128]])
                    ev = nc.vector.tensor_copy(dst, pv)
                ev.then_inc(*S.inc("f1ev"))

            for qt in range(8):
                par = qt % 2
                ws = par * 4
                if qt >= 2:
                    nc.sync.wait_ge(S["wo_ev"], qt - 1)
                h_, _ = S.inc(f"wox{par}", 16); c_ = S.cnt[f"wox{par}"]
                nc.sync.dma_start(xr[par][:],
                                  d_xres[qt * 128:(qt + 1) * 128, :]
                                  ).then_inc(h_, 16)
                if qt >= 2:
                    nc.tensor.wait_ge(S["wo_ev"], qt - 1)
                    nc.tensor.wait_ge(S["f1ev"], qt - 1)
                mmwo = None
                for j in range(16):
                    if qt == 0 and j >= 10:
                        nc.tensor.wait_ge(S["wcast"],
                                          woB_target - (15 - j) * 2)
                    wsrc = (woA[:, j * D:(j + 1) * D] if j < 10
                            else woB[:, (j - 10) * D:(j - 9) * D])
                    for ds in range(4):
                        mmwo = nc.tensor.matmul(
                            psW[ws + ds][:],
                            attnT[:, j * TOK + qt * 128:j * TOK + (qt + 1) * 128],
                            wsrc[:, ds * 512:(ds + 1) * 512],
                            start=(j == 0), stop=(j == 15))
                mmwo.then_inc(*S.inc("wo_mm"))
                nc.vector.wait_ge(S["wo_mm"], qt + 1)
                nc.vector.wait_ge(S[f"wox{par}"], c_)
                if qt >= 2:
                    nc.vector.wait_ge(S[f"ysp{par}"],
                                      S.cnt[f"ysp{par}"])
                for ds in range(4):
                    nc.vector.tensor_tensor(
                        wtmp[ds][:], psW[ws + ds][:],
                        xr[par][:, ds * 512:(ds + 1) * 512], ALU.add)
                nc.vector.drain()
                lastv = None
                for ds in range(4):
                    lastv = nc.vector.tensor_tensor(
                        ytile[par][:, ds * 512:(ds + 1) * 512],
                        wtmp[ds][:], bob[:, ds * 512:(ds + 1) * 512], ALU.add)
                lastv.then_inc(*S.inc("wo_ev"))
                rseg = ytile[par][:]
                nc.scalar.wait_ge(S["wo_ev"], qt + 1)
                nc.scalar.activation(junk[:], rseg, AF.Square,
                                     accum_out=lt[:, 1:2])
                nc.scalar.drain()
                nc.scalar.activation(junk[:], rseg, AF.Copy,
                                     accum_out=lt[:, 0:1])
                nc.scalar.drain()
                nc.scalar.activation(lt[:, 2:3], lt[:, 0:1], AF.Copy,
                                     scale=-1.0 / D)
                nc.scalar.drain()
                nc.scalar.activation(lt[:, 3:4], lt[:, 2:3], AF.Square)
                nc.scalar.drain()
                nc.scalar.activation(lt[:, 4:5], lt[:, 1:2], AF.Copy,
                                     scale=1.0 / D).then_inc(*S.inc("ln1"))
                nc.vector.wait_ge(S["ln1"], S.cnt["ln1"])
                nc.vector.tensor_tensor(lt[:, 5:6], lt[:, 4:5], lt[:, 3:4],
                                        ALU.subtract)
                nc.vector.drain()
                nc.vector.tensor_tensor(lt[:, 6:7], lt[:, 5:6], epsp[:],
                                        ALU.add).then_inc(*S.inc("ln1"))
                nc.scalar.wait_ge(S["ln1"], S.cnt["ln1"])
                nc.scalar.activation(lt[:, 7:8], lt[:, 6:7], AF.Ln)
                nc.scalar.drain()
                nc.scalar.activation(lt[:, 6:7], lt[:, 7:8], AF.Exp,
                                     scale=-0.5).then_inc(*S.inc("ln1"))
                nc.vector.wait_ge(S["ln1"], S.cnt["ln1"])
                nc.vector.tensor_scalar_mul(lt[:, 7:8], lt[:, 2:3],
                                            lt[:, 6:7]).then_inc(*S.inc("ln1"))
                nc.scalar.wait_ge(S["ln1"], S.cnt["ln1"])
                nc.scalar.activation(rseg, rseg, AF.Identity, bias=lt[:, 7:8],
                                     scale=lt[:, 6:7]).then_inc(*S.inc("ln1"))
                nc.sync.wait_ge(S["ln1"], S.cnt["ln1"])
                nc.sync.dma_start(d_ydr.ap()[qt * 128:(qt + 1) * 128, :], rseg
                                  ).then_inc(*S.inc(f"ysp{par}", 16))
                nc.vector.wait_ge(S["ln1"], S.cnt["ln1"])
                if qt >= 2:
                    nc.vector.wait_ge(S["f1mm"], qt - 1)
                nc.vector.tensor_copy(ybf[par][:], rseg).then_inc(*S.inc("f1c"))
                emit_transposes(qt)
            nc.sync.wait_ge(S["ysp0"], S.cnt["ysp0"])
            nc.sync.wait_ge(S["ysp1"], S.cnt["ysp1"])
            nc.all_engine_barrier()
            if debug == "wo":
                o1 = dbg("dbg_yT", [128, 16 * TOK], BF16)
                nc.sync.dma_start(o1[:], yT[:]).then_inc(*S.inc("outd", 16))
                nc.sync.wait_ge(S["outd"], S.cnt["outd"])
                nc.all_engine_barrier()

        attn_right.close()
        f1_right = contextlib.ExitStack()
        hdnT = f1_right.enter_context(
            nc.sbuf_tensor("hdnT", [128, NIT * TOK], BF16, side="right"))
        with contextlib.ExitStack() as f1:
            g_st = [f1.enter_context(nc.sbuf_tensor(f"g_st{i}", [128, D], F32))
                    for i in range(2)]
            g_bf = [f1.enter_context(nc.sbuf_tensor(f"g_bf{i}", [128, D], BF16))
                    for i in range(2)]
            u_st = [f1.enter_context(nc.sbuf_tensor(f"u_st{i}", [128, D], F32))
                    for i in range(2)]
            u_bf = [f1.enter_context(nc.sbuf_tensor(f"u_bf{i}", [128, D], BF16))
                    for i in range(2)]
            sg = [f1.enter_context(nc.sbuf_tensor(f"sg{i}", [128, 1024], BF16))
                  for i in range(2)]
            su = [f1.enter_context(nc.sbuf_tensor(f"su{i}", [128, 1024], BF16))
                  for i in range(2)]
            psG = [f1.enter_context(nc.psum_tensor(f"psG{i}", [128, 1024], F32))
                   for i in range(4)]

            f1w_base = S.cnt["f1wc"]
            gu_base = S.cnt["gu_mm"]
            hdn_base = S.cnt["hdn"]
            ev_base = S.cnt["gu_ev"]

            def f1_load(k):
                bi = k % 2
                if k >= 2:
                    nc.sync.wait_ge(S["f1wc"], f1w_base + k - 1)
                for dsrc, st_ in ((d_gwT, g_st[bi]), (d_uwT, u_st[bi])):
                    for half in range(2):
                        src = bass.AP(dsrc, half * 8 * 128 * INTER + k * 128,
                                      [[INTER, 128], [128 * INTER, 8], [1, 128]])
                        dst = bass.AP(st_, half * 1024,
                                      [[D, 128], [128, 8], [1, 128]])
                        nc.sync.dma_start(dst, src).then_inc(
                            *S.inc(f"f1w{bi}", 16))

            def f1_cast(k):
                bi = k % 2
                nc.vector.wait_ge(S[f"f1w{bi}"], S.cnt[f"f1w{bi}"])
                if k >= 2:
                    nc.vector.wait_ge(S["gu_mm"], gu_base + k - 1)
                nc.vector.tensor_copy(g_bf[bi][:], g_st[bi][:])
                nc.vector.tensor_copy(u_bf[bi][:], u_st[bi][:]
                                      ).then_inc(*S.inc("f1wc"))

            def f1_comp(k):
                bi = k % 2
                pg, pu = psG[bi * 2], psG[bi * 2 + 1]
                nc.tensor.wait_ge(S["f1wc"], f1w_base + k + 1)
                if k >= 2:
                    nc.tensor.wait_ge(S["hdn"], hdn_base + k - 1)
                lastm = None
                for hc in range(16):
                    for pp, wbf in ((pg, g_bf[bi]), (pu, u_bf[bi])):
                        for th in range(2):
                            lastm = nc.tensor.matmul(
                                pp[:, th * 512:(th + 1) * 512],
                                wbf[:, hc * 128:(hc + 1) * 128],
                                yT[:, hc * TOK + th * 512:
                                   hc * TOK + (th + 1) * 512],
                                start=(hc == 0), stop=(hc == 15))
                lastm.then_inc(*S.inc("gu_mm"))
                nc.scalar.wait_ge(S["gu_mm"], gu_base + k + 1)
                if k == 0:
                    nc.scalar.wait_ge(S["pre"], 96)
                if k >= 2:
                    nc.scalar.wait_ge(S["hdn"], hdn_base + k - 1)
                nc.scalar.activation(sg[bi][:], pg[:], AF.Silu,
                                     scale=gst[:, k:k + 1])
                nc.scalar.activation(su[bi][:], pu[:], AF.Copy,
                                     scale=ust[:, k:k + 1]
                                     ).then_inc(*S.inc("gu_ev"))
                nc.gpsimd.wait_ge(S["gu_ev"], ev_base + k + 1)
                nc.gpsimd.tensor_tensor(hdnT[:, k * TOK:(k + 1) * TOK],
                                        sg[bi][:], su[bi][:], ALU.mult
                                        ).then_inc(*S.inc("hdn"))

            f1_load(0)
            f1_cast(0)
            f1_load(1)
            f1_cast(1)
            for k in range(NIT):
                if k + 2 < NIT:
                    f1_load(k + 2)
                    f1_cast(k + 2)
                f1_comp(k)
            nc.all_engine_barrier()
            if debug == "ffn1":
                o1 = dbg("dbg_hdnT", [128, NIT * TOK], BF16)
                nc.sync.dma_start(o1[:], hdnT[:]).then_inc(*S.inc("outd", 16))
                nc.sync.wait_ge(S["outd"], S.cnt["outd"])
                nc.all_engine_barrier()

        yt_stack.close()
        with contextlib.ExitStack() as f2:
            res2 = f2.enter_context(nc.sbuf_tensor("res2", [128, 8 * D], F32))
            dsb = f2.enter_context(nc.sbuf_tensor("s_dsb", [128, D], F32))
            lt2 = f2.enter_context(nc.sbuf_tensor("lt2", [128, 64], F32))
            dw_st = [f2.enter_context(
                nc.sbuf_tensor(f"dw_st{i}", [128, 512], F32)) for i in range(3)]
            dw_bf = [f2.enter_context(
                nc.sbuf_tensor(f"dw_bf{i}", [128, 512], BF16)) for i in range(3)]
            dtmp = [f2.enter_context(
                nc.sbuf_tensor(f"dtmp{i}", [128, 512], F32)) for i in range(8)]
            psD = [f2.enter_context(nc.psum_tensor(f"psD{i}", [128, 512], F32))
                   for i in range(8)]

            nc.scalar.wait_ge(S["ysp0"], S.cnt["ysp0"])
            nc.scalar.wait_ge(S["ysp1"], S.cnt["ysp1"])
            nc.scalar.dma_start(dsb[:], d_dsb[:]).then_inc(*S.inc("dsbd", 16))
            for qt in range(8):
                nc.scalar.dma_start(res2[:, qt * D:(qt + 1) * D],
                                    d_ydr.ap()[qt * 128:(qt + 1) * 128, :]
                                    ).then_inc(*S.inc("dsbd", 16))
            nc.scalar.wait_ge(S["dsbd"], 144)
            nc.all_engine_barrier()

            f2c_base = S.cnt["f2c"]
            f2m_base = S.cnt["f2mm"]

            def f2_load(g):
                bi = g % 3
                hs_, k_ = divmod(g, NIT)
                if g >= 3:
                    nc.sync.wait_ge(S["f2c"], f2c_base + g - 2)
                nc.sync.dma_start(
                    dw_st[bi][:],
                    d_dwT[k_ * 128:(k_ + 1) * 128,
                          hs_ * 512:(hs_ + 1) * 512]
                ).then_inc(*S.inc(f"f2w{bi}", 16))

            def f2_cast(g):
                bi = g % 3
                nc.vector.wait_ge(S[f"f2w{bi}"], S.cnt[f"f2w{bi}"])
                if g >= 3:
                    nc.vector.wait_ge(S["f2mm"], f2m_base + g - 2)
                nc.vector.tensor_copy(dw_bf[bi][:], dw_st[bi][:]
                                      ).then_inc(*S.inc("f2c"))

            def f2_comp(g):
                bi = g % 3
                hs_, k_ = divmod(g, NIT)
                nc.tensor.wait_ge(S["f2c"], f2c_base + g + 1)
                if k_ == 0 and hs_ > 0:
                    nc.tensor.wait_ge(S["f2ev"], hs_)
                lastm = None
                for tt in range(8):
                    lastm = nc.tensor.matmul(
                        psD[tt][:],
                        hdnT[:, k_ * TOK + tt * 128:k_ * TOK + (tt + 1) * 128],
                        dw_bf[bi][:],
                        start=(k_ == 0), stop=(k_ == NIT - 1))
                lastm.then_inc(*S.inc("f2mm"))

            def f2_evac(hs_):
                nc.vector.wait_ge(S["f2mm"], f2m_base + (hs_ + 1) * NIT)
                for tt in range(8):
                    nc.vector.tensor_tensor(dtmp[tt][:], psD[tt][:],
                                            dsb[:, hs_ * 512:(hs_ + 1) * 512],
                                            ALU.mult)
                nc.vector.drain()
                laste = None
                for tt in range(8):
                    seg = res2[:, tt * D + hs_ * 512:tt * D + (hs_ + 1) * 512]
                    laste = nc.vector.tensor_tensor(seg, seg, dtmp[tt][:],
                                                    ALU.add)
                laste.then_inc(*S.inc("f2ev"))

            NG = 4 * NIT
            for g0 in range(3):
                f2_load(g0)
                f2_cast(g0)
            for g in range(NG):
                if g + 3 < NG:
                    f2_load(g + 3)
                    f2_cast(g + 3)
                f2_comp(g)
                if g % NIT == NIT - 1:
                    f2_evac(g // NIT)
            nc.all_engine_barrier()

            qstat = None
            for qt in range(8):
                rseg = res2[:, qt * D:(qt + 1) * D]
                if qt > 0:
                    nc.scalar.drain()
                qstat = nc.scalar.activation(dsb[:], rseg, AF.Square,
                                             accum_out=lt2[:, 8 + qt:9 + qt])
                nc.vector.tensor_reduce(lt2[:, qt:qt + 1], rseg,
                                        mybir.AxisListType.X, ALU.add)
            qstat.then_inc(*S.inc("ln2"))
            nc.vector.drain()
            nc.vector.wait_ge(S["ln2"], S.cnt["ln2"])
            nc.vector.tensor_scalar_mul(lt2[:, 16:24], lt2[:, 0:8], -1.0 / D)
            nc.vector.tensor_scalar_mul(lt2[:, 24:32], lt2[:, 8:16], 1.0 / D)
            nc.vector.drain()
            nc.vector.tensor_tensor(lt2[:, 32:40], lt2[:, 16:24],
                                    lt2[:, 16:24], ALU.mult)
            nc.vector.drain()
            nc.vector.tensor_tensor(lt2[:, 40:48], lt2[:, 24:32],
                                    lt2[:, 32:40], ALU.subtract)
            nc.vector.drain()
            nc.vector.tensor_scalar_add(lt2[:, 48:56], lt2[:, 40:48], EPS
                                        ).then_inc(*S.inc("ln2"))
            nc.scalar.wait_ge(S["ln2"], S.cnt["ln2"])
            nc.scalar.activation(lt2[:, 40:48], lt2[:, 48:56], AF.Ln)
            nc.scalar.drain()
            nc.scalar.activation(lt2[:, 48:56], lt2[:, 40:48], AF.Exp,
                                 scale=-0.5).then_inc(*S.inc("ln2"))
            nc.vector.wait_ge(S["ln2"], S.cnt["ln2"])
            nc.vector.tensor_tensor(lt2[:, 56:64], lt2[:, 16:24],
                                    lt2[:, 48:56], ALU.mult
                                    ).then_inc(*S.inc("ln2"))
            nc.scalar.wait_ge(S["ln2"], S.cnt["ln2"])
            for qt in range(8):
                rseg = res2[:, qt * D:(qt + 1) * D]
                nc.scalar.activation(rseg, rseg, AF.Identity,
                                     bias=lt2[:, 56 + qt:57 + qt],
                                     scale=lt2[:, 48 + qt:49 + qt]
                                     ).then_inc(*S.inc("ln2"))
                nc.sync.wait_ge(S["ln2"], S.cnt["ln2"])
                nc.sync.dma_start(d_out[qt * 128:(qt + 1) * 128, :], rseg
                                  ).then_inc(*S.inc("outd", 16))
            nc.sync.wait_ge(S["outd"], S.cnt["outd"])

        f1_right.close()

    return nc, sorted(dbg_outs)


def _rope_tables(t0, n):
    inv = 10000.0 ** (-(np.arange(32, dtype=np.float64) * 2) / 64.0)
    p = np.arange(128)
    t = np.arange(t0, t0 + n, dtype=np.float64)
    ang = t[None, :] * inv[p % 32][:, None]
    cos = np.cos(ang).astype(np.float32)
    sin = np.sin(ang).astype(np.float32)
    sgn = np.where((p % 64) < 32, -1.0, 1.0).astype(np.float32)
    return cos, np.ascontiguousarray(sin * sgn[:, None]).astype(np.float32)


def _prep_inputs(inp):
    c = np.ascontiguousarray

    def _blk(wT, nk):
        a = wT.reshape(2, 8, 128, nk, 128)
        return c(a.transpose(3, 2, 0, 1, 4).reshape(nk * 128, 2048))

    x = np.asarray(inp["L0_attn_x"], dtype=np.float32)
    cosK, sinK = _rope_tables(0, T)
    base = {
        "wqT": c(np.asarray(inp["L0_attn_Wq"]).T),
        "wkT": c(np.asarray(inp["L0_attn_Wk"]).T),
        "wvT": c(np.asarray(inp["L0_attn_Wv"]).T),
        "woT": c(np.asarray(inp["L0_attn_Wo"]).T),
        "bqt": c(np.asarray(inp["L0_attn_bq"]).reshape(16, 128).T),
        "bkt": c(np.asarray(inp["L0_attn_bk"]).reshape(4, 128).T),
        "bvb": c(np.broadcast_to(np.asarray(inp["L0_attn_bv"])[None, :],
                                 (128, 512))),
        "bob": c(np.broadcast_to(np.asarray(inp["L0_attn_bo"])[None, :],
                                 (128, D))),
        "gwT": c(np.asarray(inp["L0_ffn_gate_w"]).T),
        "uwT": c(np.asarray(inp["L0_ffn_up_w"]).T),
        "dwT": c(np.asarray(inp["L0_ffn_down_w"]).T),
        "gst": c(np.asarray(inp["L0_ffn_gate_s"]).reshape(NIT, 128).T),
        "ust": c(np.asarray(inp["L0_ffn_up_s"]).reshape(NIT, 128).T),
        "dsb": c(np.broadcast_to(np.asarray(inp["L0_ffn_down_s"])[None, :],
                                 (128, D))),
        "cosK": cosK, "sinK": sinK,
        "identb": np.eye(128).astype(ml_dtypes.bfloat16),
    }
    xT = [c(x[b].T) for b in range(B)]
    maps = []
    for core in range(N_CORES):
        b = core // 4
        q0 = (core % 4) * TOK
        cosQ, sinQ = _rope_tables(q0, TOK)
        m = dict(base)
        m["xT"] = xT[b]
        m["xqT"] = c(xT[b][:, q0:q0 + TOK])
        m["xres"] = c(x[b, q0:q0 + TOK])
        m["cosQ"] = cosQ
        m["sinQ"] = sinQ
        maps.append(m)
    return maps


def kernel(**inputs):
    try:
        from concourse.compiler_utils import (get_compiler_flags,
                                              set_compiler_flags)
        set_compiler_flags([
            f.replace("--enable-ldw-opt=false", "--enable-ldw-opt=true")
            for f in get_compiler_flags()])
    except Exception:
        pass
    maps = _prep_inputs(inputs)
    nc, _ = build_program()
    res = run_bass_kernel_spmd(nc, maps, core_ids=list(range(N_CORES)))
    out = np.empty((B, T, D), np.float32)
    for core in range(N_CORES):
        b = core // 4
        q0 = (core % 4) * TOK
        out[b, q0:q0 + TOK] = np.asarray(res.results[core]["out"])
    return out


# version 21 (final)
# speedup vs baseline: 1.4910x; 1.0127x over previous
import contextlib
import numpy as np
import ml_dtypes
import concourse.bass as bass
import concourse.mybir as mybir
from concourse.bass_utils import run_bass_kernel_spmd

F32 = mybir.dt.float32
BF16 = mybir.dt.bfloat16
AF = mybir.ActivationFunctionType
ALU = mybir.AluOpType

B, T, D = 2, 4096, 2048
HQ, HKV, HD = 32, 8, 64
TOK = 1024
INTER = 5632
NIT = INTER // 128
NTC = T // 512
NKT = T // 128
C_SHIFT = 10.0
EPS = 1e-5
N_CORES = 8


class Sems:
    def __init__(self, nc, es):
        self.nc = nc
        self.es = es
        self.cnt = {}
        self.h = {}

    def new(self, *names):
        for n in names:
            self.h[n] = self.es.enter_context(self.nc.semaphore(n))
            self.cnt[n] = 0

    def inc(self, name, n=1):
        self.cnt[name] += n
        return self.h[name], n

    def __getitem__(self, name):
        return self.h[name]


def build_program(debug=None):
    nc = bass.Bass("TRN2", target_bir_lowering=False)

    d_xT = nc.dram_tensor("xT", [D, T], F32, kind="ExternalInput")
    d_xqT = nc.dram_tensor("xqT", [D, TOK], F32, kind="ExternalInput")
    d_xres = nc.dram_tensor("xres", [TOK, D], F32, kind="ExternalInput").ap()
    d_wqT = nc.dram_tensor("wqT", [D, D], F32, kind="ExternalInput")
    d_wkT = nc.dram_tensor("wkT", [D, 512], F32, kind="ExternalInput")
    d_wvT = nc.dram_tensor("wvT", [D, 512], F32, kind="ExternalInput")
    d_woT = nc.dram_tensor("woT", [D, D], F32, kind="ExternalInput").ap()
    d_bqt = nc.dram_tensor("bqt", [128, 16], F32, kind="ExternalInput").ap()
    d_bkt = nc.dram_tensor("bkt", [128, 4], F32, kind="ExternalInput").ap()
    d_bvb = nc.dram_tensor("bvb", [128, 512], F32, kind="ExternalInput").ap()
    d_bob = nc.dram_tensor("bob", [128, D], F32, kind="ExternalInput").ap()
    d_gwT = nc.dram_tensor("gwT", [D, INTER], F32, kind="ExternalInput")
    d_uwT = nc.dram_tensor("uwT", [D, INTER], F32, kind="ExternalInput")
    d_dwT = nc.dram_tensor("dwT", [INTER, D], F32, kind="ExternalInput").ap()
    d_gst = nc.dram_tensor("gst", [128, NIT], F32, kind="ExternalInput").ap()
    d_ust = nc.dram_tensor("ust", [128, NIT], F32, kind="ExternalInput").ap()
    d_dsb = nc.dram_tensor("dsb", [128, D], F32, kind="ExternalInput").ap()
    d_cosK = nc.dram_tensor("cosK", [128, T], F32, kind="ExternalInput").ap()
    d_sinK = nc.dram_tensor("sinK", [128, T], F32, kind="ExternalInput").ap()
    d_cosQ = nc.dram_tensor("cosQ", [128, TOK], F32, kind="ExternalInput").ap()
    d_sinQ = nc.dram_tensor("sinQ", [128, TOK], F32, kind="ExternalInput").ap()
    d_idb = nc.dram_tensor("identb", [128, 128], BF16, kind="ExternalInput").ap()
    d_out = nc.dram_tensor("out", [TOK, D], F32, kind="ExternalOutput").ap()
    d_zb = nc.dram_tensor("zb", [64, 1024], F32)
    d_ydr = nc.dram_tensor("ydr", [TOK, D], F32)

    dbg_outs = {}

    def dbg(name, shape, dt):
        dbg_outs[name] = nc.dram_tensor(name, shape, dt,
                                        kind="ExternalOutput").ap()
        return dbg_outs[name]

    es = contextlib.ExitStack()
    S = Sems(nc, es)
    S.new("x", "xb", "mmKV", "kev", "vev", "sh", "rope",
          "wld", "wcast", "mmQ", "qev", "qsh", "qrope",
          "psS", "psE", "psA", "zr", "zbd", "bc", "aev",
          "wo_x", "wo_mm", "wo_ev", "ln1", "ysp", "f1c", "f1mm", "f1ev",
          "f1w", "f1wc", "gu_mm", "gu_ev", "hdn",
          "f2w", "f2c", "f2mm", "f2ev", "ln2", "outd", "misc",
          "x0", "x1", "x2", "x3", "wq0", "wq1", "wox0", "wox1", "f1w0", "f1w1",
          "f2w0", "f2w1", "f2w2", "zb0", "zb1", "bc0", "bc1", "ysp0", "ysp1",
          "pre", "tabK", "tabQ", "bobd", "dsbd", "wo0", "wo1",
          "ropev", "qropev", "xq0", "xq1", "xqc", "ropea", "tab0", "tab1")

    with es:
        bvb = es.enter_context(nc.sbuf_tensor("s_bvb", [128, 512], F32))
        bkt = es.enter_context(nc.sbuf_tensor("s_bkt", [128, 4], F32))
        bqt = es.enter_context(nc.sbuf_tensor("s_bqt", [128, 16], F32))
        gst = es.enter_context(nc.sbuf_tensor("s_gst", [128, NIT], F32))
        ust = es.enter_context(nc.sbuf_tensor("s_ust", [128, NIT], F32))
        cneg = es.enter_context(nc.sbuf_tensor("cneg", [128, 1], F32))
        epsp = es.enter_context(nc.sbuf_tensor("epsp", [128, 1], F32))
        ident = es.enter_context(nc.sbuf_tensor("ident", [128, 128], BF16))
        lt = es.enter_context(nc.sbuf_tensor("lt", [128, 8], F32))

        for ap_d, t_s in ((d_bvb, bvb), (d_bkt, bkt), (d_bqt, bqt),
                          (d_gst, gst), (d_ust, ust), (d_idb, ident)):
            nc.scalar.dma_start(t_s[:], ap_d[:]).then_inc(*S.inc("pre", 16))
        nc.vector.memset(cneg[:], -C_SHIFT)
        nc.vector.memset(epsp[:], EPS)

        kv_stack = contextlib.ExitStack()
        xq_right = contextlib.ExitStack()
        xq_bf = xq_right.enter_context(
            nc.sbuf_tensor("xq_bf", [128, 16 * TOK], BF16, side="right"))
        KT = kv_stack.enter_context(nc.sbuf_tensor("KT", [128, 4 * T], BF16))
        Vp = kv_stack.enter_context(
            nc.sbuf_tensor("Vp", [128, HKV * NKT * 65], BF16))
        nc.vector.memset(
            bass.AP(Vp, 64, [[HKV * NKT * 65, 128], [65, HKV * NKT]]), 1.0)

        with contextlib.ExitStack() as pa1:
            cosb = [pa1.enter_context(
                nc.sbuf_tensor(f"cosb{i}", [128, 512], F32)) for i in range(2)]
            sinb = [pa1.enter_context(
                nc.sbuf_tensor(f"sinb{i}", [128, 512], F32)) for i in range(2)]
            wk_bf = pa1.enter_context(nc.sbuf_tensor("wk_bf", [128, 8192], BF16))
            wv_bf = pa1.enter_context(nc.sbuf_tensor("wv_bf", [128, 8192], BF16))
            Kstg = pa1.enter_context(nc.sbuf_tensor("Kstg", [128, 2048], F32))
            Ksh = pa1.enter_context(nc.sbuf_tensor("Ksh", [128, 2048], F32))
            xq_st = [pa1.enter_context(
                nc.sbuf_tensor(f"xq_st{i}", [128, TOK], F32)) for i in range(2)]
            xbf = [pa1.enter_context(nc.sbuf_tensor(f"xbf{i}", [128, 512], F32))
                   for i in range(4)]
            xbb = [pa1.enter_context(nc.sbuf_tensor(f"xbb{i}", [128, 512], BF16))
                   for i in range(4)]
            kt1 = [pa1.enter_context(nc.sbuf_tensor(f"kt1{i}", [128, 512], F32))
                   for i in range(2)]
            kt2 = [pa1.enter_context(nc.sbuf_tensor(f"kt2{i}", [128, 512], F32))
                   for i in range(2)]
            psK = [pa1.enter_context(nc.psum_tensor(f"psK{i}", [128, 512], F32))
                   for i in range(4)]
            psV = [pa1.enter_context(nc.psum_tensor(f"psV{i}", [128, 512], F32))
                   for i in range(4)]


            with nc.sbuf_tensor("wst", [128, 4096], F32) as wst:
                for dsrc, bfdst in ((d_wkT, wk_bf), (d_wvT, wv_bf)):
                    for half in range(2):
                        src = bass.AP(dsrc, half * 8 * 128 * 512,
                                      [[512, 128], [128 * 512, 8], [1, 512]])
                        dst = bass.AP(wst, 0, [[4096, 128], [512, 8], [1, 512]])
                        if S.cnt["wcast"] > 0:
                            nc.sync.wait_ge(S["wcast"], S.cnt["wcast"])
                        h_, _ = S.inc("wld", 16); c_ = S.cnt["wld"]
                        nc.sync.dma_start(dst, src).then_inc(h_, 16)
                        nc.vector.wait_ge(S["wld"], c_)
                        nc.vector.tensor_copy(
                            bfdst[:, half * 4096:(half + 1) * 4096], wst[:]
                        ).then_inc(*S.inc("wcast"))
                nc.all_engine_barrier()

            xb_b1 = S.cnt["xb"]

            def kv_load(k):
                bi = k % 4
                tc_, d_ = divmod(k, 16)
                if k >= 4:
                    nc.sync.wait_ge(S["xb"], xb_b1 + k - 3)
                nc.sync.dma_start(
                    xbf[bi][:],
                    d_xT.ap()[d_ * 128:(d_ + 1) * 128,
                              tc_ * 512:(tc_ + 1) * 512],
                ).then_inc(*S.inc(f"x{bi}", 16))

            def kv_cast(k):
                bi = k % 4
                nc.vector.wait_ge(S[f"x{bi}"], S.cnt[f"x{bi}"])
                if k >= 4:
                    nc.vector.wait_ge(S["mmKV"], k - 3)
                nc.vector.tensor_copy(xbb[bi][:], xbf[bi][:]).then_inc(
                    *S.inc("xb"))

            def kv_mm(k):
                bi = k % 4
                tc_, d_ = divmod(k, 16)
                nc.tensor.wait_ge(S["xb"], xb_b1 + k + 1)
                if tc_ > 0 and d_ == 0:
                    nc.tensor.wait_ge(S["kev"], tc_)
                    nc.tensor.wait_ge(S["vev"], tc_)
                for m in range(4):
                    nc.tensor.matmul(
                        psK[m][:],
                        wk_bf[:, d_ * 512 + m * 128:d_ * 512 + (m + 1) * 128],
                        xbb[bi][:], start=(d_ == 0), stop=(d_ == 15))
                mm = None
                for tt in range(4):
                    mm = nc.tensor.matmul(
                        psV[tt][:], xbb[bi][:, tt * 128:(tt + 1) * 128],
                        wv_bf[:, d_ * 512:(d_ + 1) * 512],
                        start=(d_ == 0), stop=(d_ == 15))
                mm.then_inc(*S.inc("mmKV"))

            def kv_evac_rope(tc_):
                nc.scalar.wait_ge(S["mmKV"], (tc_ + 1) * 16)
                if tc_ == 0:
                    nc.scalar.wait_ge(S["pre"], 96)
                else:
                    nc.scalar.wait_ge(S["ropev"], 4 * tc_)
                    nc.scalar.wait_ge(S["sh"], 64 * tc_)
                act = None
                for m in range(4):
                    act = nc.scalar.activation(
                        Kstg[:, m * 512:(m + 1) * 512], psK[m][:], AF.Identity,
                        bias=bkt[:, m:m + 1], scale=1.0)
                act.then_inc(*S.inc("kev"))
                nc.scalar.wait_ge(S["kev"], tc_ + 1)
                for gg in range(2):
                    for hswap in range(2):
                        so, do_ = (32, 0) if hswap == 0 else (0, 32)
                        nc.scalar.dma_start(
                            Ksh[gg * 64 + do_:gg * 64 + do_ + 32, :],
                            Kstg[gg * 64 + so:gg * 64 + so + 32, :],
                        ).then_inc(*S.inc("sh", 16))
                nc.vector.wait_ge(S["mmKV"], (tc_ + 1) * 16)
                if tc_ == 0:
                    nc.vector.wait_ge(S["pre"], 96)
                tt_last = None
                for tt in range(4):
                    kt_g = tc_ * 4 + tt
                    dst = bass.AP(Vp, kt_g * 65,
                                  [[HKV * NKT * 65, 128], [2080, HKV], [1, 64]])
                    src = bass.AP(psV[tt], 0, [[512, 128], [64, HKV], [1, 64]])
                    tt_last = nc.vector.tensor_tensor(dst, src, bvb[:], ALU.add)
                tt_last.then_inc(*S.inc("vev"))
                nc.vector.wait_ge(S["sh"], 64 * (tc_ + 1))
                nc.vector.wait_ge(S[f"tab{tc_ % 2}"], S.cnt[f"tab{tc_ % 2}"])
                for m in range(4):
                    gm = 4 * tc_ + m
                    if gm >= 2:
                        nc.vector.wait_ge(S["ropea"], gm - 1)
                    nc.vector.tensor_tensor(
                        kt1[m % 2][:], Ksh[:, m * 512:(m + 1) * 512],
                        sinb[tc_ % 2][:], ALU.mult)
                    nc.vector.tensor_tensor(
                        kt2[m % 2][:], Kstg[:, m * 512:(m + 1) * 512],
                        cosb[tc_ % 2][:], ALU.mult
                    ).then_inc(*S.inc("ropev"))
                last = None
                for m in range(4):
                    gm = 4 * tc_ + m
                    nc.gpsimd.wait_ge(S["ropev"], gm + 1)
                    last = nc.gpsimd.tensor_tensor(
                        KT[:, m * T + tc_ * 512:m * T + (tc_ + 1) * 512],
                        kt1[m % 2][:], kt2[m % 2][:], ALU.add
                    ).then_inc(*S.inc("ropea"))

            def tab_load(tc_):
                bi = tc_ % 2
                if tc_ >= 2:
                    nc.gpsimd.wait_ge(S["ropev"], 4 * (tc_ - 1))
                nc.gpsimd.dma_start(cosb[bi][:],
                                    d_cosK[:, tc_ * 512:(tc_ + 1) * 512]
                                    ).then_inc(*S.inc(f"tab{bi}", 16))
                nc.gpsimd.dma_start(sinb[bi][:],
                                    d_sinK[:, tc_ * 512:(tc_ + 1) * 512]
                                    ).then_inc(*S.inc(f"tab{bi}", 16))

            def xq_load(dq):
                bi = dq % 2
                if dq >= 2:
                    nc.gpsimd.wait_ge(S["xqc"], dq - 1)
                nc.gpsimd.dma_start(xq_st[bi][:],
                                    d_xqT.ap()[dq * 128:(dq + 1) * 128, :]
                                    ).then_inc(*S.inc(f"xq{bi}", 16))

            def xq_cast(dq):
                bi = dq % 2
                nc.vector.wait_ge(S[f"xq{bi}"], S.cnt[f"xq{bi}"])
                nc.vector.tensor_copy(xq_bf[:, dq * TOK:(dq + 1) * TOK],
                                      xq_st[bi][:]).then_inc(*S.inc("xqc"))

            tab_load(0)
            tab_load(1)
            for k in range(4):
                kv_load(k)
                kv_cast(k)
            xq_load(0)
            xq_load(1)
            for g in range(NTC * 16):
                tc, d = divmod(g, 16)
                if g + 4 < NTC * 16:
                    kv_load(g + 4)
                    kv_cast(g + 4)
                kv_mm(g)
                if d == 7 and tc < 8:
                    xq_cast(2 * tc)
                    if 2 * tc + 2 < 16:
                        xq_load(2 * tc + 2)
                if d == 15:
                    kv_evac_rope(tc)
                    if tc + 2 < NTC:
                        tab_load(tc + 2)
                    xq_cast(2 * tc + 1)
                    if 2 * tc + 3 < 16:
                        xq_load(2 * tc + 3)
            nc.all_engine_barrier()
            if debug == "pa1":
                o1 = dbg("dbg_KT", [128, 4 * T], BF16)
                o2 = dbg("dbg_Vp", [128, HKV * NKT * 65], BF16)
                nc.sync.dma_start(o1[:], KT[:]).then_inc(*S.inc("outd", 16))
                nc.sync.dma_start(o2[:], Vp[:]).then_inc(*S.inc("outd", 16))
                nc.sync.wait_ge(S["outd"], S.cnt["outd"])
                nc.all_engine_barrier()

        q_stack = contextlib.ExitStack()
        QT2 = q_stack.enter_context(nc.sbuf_tensor("QT2", [128, 16 * TOK], BF16))
        with contextlib.ExitStack() as pa2:
            cosQ = pa2.enter_context(nc.sbuf_tensor("s_cosQ", [128, TOK], F32))
            sinQ = pa2.enter_context(nc.sbuf_tensor("s_sinQ", [128, TOK], F32))
            wqs_st = [pa2.enter_context(
                nc.sbuf_tensor(f"wqs_st{i}", [128, 2048], F32)) for i in range(2)]
            wqs_bf = [pa2.enter_context(
                nc.sbuf_tensor(f"wqs_bf{i}", [128, 2048], BF16)) for i in range(2)]
            Qstg = [pa2.enter_context(
                nc.sbuf_tensor(f"Qstg{i}", [128, TOK], F32)) for i in range(2)]
            Qsh = [pa2.enter_context(
                nc.sbuf_tensor(f"Qsh{i}", [128, TOK], F32)) for i in range(2)]
            qt1 = [pa2.enter_context(nc.sbuf_tensor(f"qt1{i}", [128, TOK], F32))
                   for i in range(2)]
            qt2 = [pa2.enter_context(nc.sbuf_tensor(f"qt2{i}", [128, TOK], F32))
                   for i in range(2)]
            psQ = [pa2.enter_context(nc.psum_tensor(f"psQ{i}", [128, 512], F32))
                   for i in range(8)]

            nc.scalar.dma_start(cosQ[:], d_cosQ[:]).then_inc(*S.inc("tabQ", 16))
            nc.scalar.dma_start(sinQ[:], d_sinQ[:]).then_inc(*S.inc("tabQ", 16))
            wc_base = S.cnt["wcast"]
            mq_base = S.cnt["mmQ"]

            def q_load(j):
                bi = j % 2
                if j >= 2:
                    nc.sync.wait_ge(S["wcast"], wc_base + j - 1)
                for half in range(2):
                    src = bass.AP(d_wqT, half * 8 * 128 * D + j * 128,
                                  [[D, 128], [128 * D, 8], [1, 128]])
                    dst = bass.AP(wqs_st[bi], half * 1024,
                                  [[2048, 128], [128, 8], [1, 128]])
                    nc.sync.dma_start(dst, src).then_inc(*S.inc(f"wq{bi}", 16))

            def q_cast(j):
                bi = j % 2
                nc.vector.wait_ge(S[f"wq{bi}"], S.cnt[f"wq{bi}"])
                if j >= 2:
                    nc.vector.wait_ge(S["mmQ"], mq_base + j - 1)
                nc.vector.tensor_copy(wqs_bf[bi][:], wqs_st[bi][:]
                                      ).then_inc(*S.inc("wcast"))

            def q_comp(j):
                bi = j % 2
                sbi = j % 2
                nc.tensor.wait_ge(S["wcast"], wc_base + j + 1)
                if j == 0:
                    nc.tensor.wait_ge(S["xqc"], 16)
                pA, pB = psQ[(j % 4) * 2], psQ[(j % 4) * 2 + 1]
                if j >= 4:
                    nc.tensor.wait_ge(S["qev"], j - 3)
                mmq = None
                for d in range(16):
                    nc.tensor.matmul(pA[:],
                                     wqs_bf[bi][:, d * 128:(d + 1) * 128],
                                     xq_bf[:, d * TOK:d * TOK + 512],
                                     start=(d == 0), stop=(d == 15))
                    mmq = nc.tensor.matmul(
                        pB[:], wqs_bf[bi][:, d * 128:(d + 1) * 128],
                        xq_bf[:, d * TOK + 512:(d + 1) * TOK],
                        start=(d == 0), stop=(d == 15))
                mmq.then_inc(*S.inc("mmQ"))
                nc.scalar.wait_ge(S["mmQ"], mq_base + j + 1)
                if j == 0:
                    nc.scalar.wait_ge(S["pre"], 96)
                if j >= 2:
                    nc.scalar.wait_ge(S["qrope"], j - 1)
                nc.scalar.activation(Qstg[sbi][:, 0:512], pA[:], AF.Identity,
                                     bias=bqt[:, j:j + 1], scale=1.0)
                nc.scalar.activation(Qstg[sbi][:, 512:1024], pB[:],
                                     AF.Identity, bias=bqt[:, j:j + 1],
                                     scale=1.0).then_inc(*S.inc("qev"))
                nc.scalar.wait_ge(S["qev"], j + 1)
                if j >= 1:
                    nc.scalar.wait_ge(S["qrope"], j)
                for gg in range(2):
                    for hswap in range(2):
                        so, do_ = (32, 0) if hswap == 0 else (0, 32)
                        nc.scalar.dma_start(
                            Qsh[sbi][gg * 64 + do_:gg * 64 + do_ + 32, :],
                            Qstg[sbi][gg * 64 + so:gg * 64 + so + 32, :],
                        ).then_inc(*S.inc("qsh", 16))
                nc.vector.wait_ge(S["qsh"], S.cnt["qsh"])
                if j == 0:
                    nc.vector.wait_ge(S["tabQ"], 32)
                if j >= 2:
                    nc.vector.wait_ge(S["qrope"], j - 1)
                nc.vector.tensor_tensor(qt1[sbi][:], Qsh[sbi][:], sinQ[:],
                                        ALU.mult)
                nc.vector.tensor_tensor(qt2[sbi][:], Qstg[sbi][:], cosQ[:],
                                        ALU.mult).then_inc(*S.inc("qropev"))
                nc.gpsimd.wait_ge(S["qropev"], j + 1)
                last = None
                for H, rsrc in ((2 * j, 0), (2 * j + 1, 64)):
                    blk = 4 * (H // 8) + (H % 4)
                    r0 = 0 if (H % 8) < 4 else 64
                    last = nc.gpsimd.tensor_tensor(
                        QT2[r0:r0 + 64, blk * TOK:(blk + 1) * TOK],
                        qt1[sbi][rsrc:rsrc + 64, :], qt2[sbi][rsrc:rsrc + 64, :],
                        ALU.add)
                last.then_inc(*S.inc("qrope"))

            q_load(0)
            q_cast(0)
            q_load(1)
            q_cast(1)
            for j in range(16):
                if j + 2 < 16:
                    q_load(j + 2)
                    q_cast(j + 2)
                q_comp(j)
            nc.all_engine_barrier()
            if debug == "pa2":
                o1 = dbg("dbg_QT2", [128, 16 * TOK], BF16)
                nc.sync.dma_start(o1[:], QT2[:]).then_inc(*S.inc("outd", 16))
                nc.sync.wait_ge(S["outd"], S.cnt["outd"])
                nc.all_engine_barrier()

        xq_right.close()
        attn_right = contextlib.ExitStack()
        attnT = attn_right.enter_context(
            nc.sbuf_tensor("attnT", [128, 16 * TOK], BF16, side="right"))
        woA = attn_right.enter_context(
            nc.sbuf_tensor("woA", [128, 10 * D], BF16, side="right"))
        with contextlib.ExitStack() as psx:
            wo_st = [psx.enter_context(
                nc.sbuf_tensor(f"wo_st{i}", [128, 1024], F32)) for i in range(2)]
            Sps = [psx.enter_context(nc.psum_tensor(f"Sps{i}", [128, 1024], F32))
                   for i in range(2)]
            Aps = [psx.enter_context(nc.psum_tensor(f"Aps{i}", [65, 512], F32))
                   for i in range(4)]
            wT = [psx.enter_context(nc.sbuf_tensor(f"wT{i}", [128, 1024], BF16))
                  for i in range(4)]
            zrec = [psx.enter_context(nc.sbuf_tensor(f"zrec{i}", [1, 1024], F32))
                    for i in range(2)]
            bcs = [psx.enter_context(nc.sbuf_tensor(f"bcs{i}", [128, 1024], F32))
                   for i in range(2)]

            seq = [(m, i, qb) for m in range(4) for i in range(4)
                   for qb in range(2)]

            def ps_S(itx, kt):
                m, i, qb = seq[itx]
                blk = 4 * m + i
                sbi = kt % 2
                wbi = kt % 4
                g = itx * NKT + kt
                if g >= 2:
                    nc.tensor.wait_ge(S["psE"], g - 1)
                nc.tensor.matmul(
                    Sps[sbi][:, 0:512],
                    KT[0:64, m * T + kt * 128:m * T + (kt + 1) * 128],
                    QT2[0:64, blk * TOK + qb * 512:blk * TOK + (qb + 1) * 512],
                    start=True, stop=True, tile_position=(0, 0))
                nc.tensor.matmul(
                    Sps[sbi][:, 512:1024],
                    KT[64:128, m * T + kt * 128:m * T + (kt + 1) * 128],
                    QT2[64:128, blk * TOK + qb * 512:blk * TOK + (qb + 1) * 512],
                    start=True, stop=True, tile_position=(64, 0)
                ).then_inc(*S.inc("psS"))
                nc.scalar.wait_ge(S["psS"], g + 1)
                if g >= 4:
                    nc.scalar.wait_ge(S["psA"], g - 3)
                nc.scalar.activation(
                    wT[wbi][:], Sps[sbi][:], AF.Exp,
                    bias=cneg[:], scale=0.125).then_inc(*S.inc("psE"))

            def ps_AV(itx, kt):
                m, i, qb = seq[itx]
                kve, kvo = 2 * m, 2 * m + 1
                aset = itx % 2
                A0, A1 = Aps[aset * 2], Aps[aset * 2 + 1]
                wbi = kt % 4
                g = itx * NKT + kt
                if kt == 0 and itx >= 2:
                    nc.tensor.wait_ge(S["aev"], itx - 1)
                nc.tensor.wait_ge(S["psE"], g + 1)
                nc.tensor.matmul(
                    A0[:],
                    Vp[:, kve * 2080 + kt * 65:kve * 2080 + kt * 65 + 65],
                    wT[wbi][:, 0:512],
                    start=(kt == 0), stop=(kt == NKT - 1))
                nc.tensor.matmul(
                    A1[:],
                    Vp[:, kvo * 2080 + kt * 65:kvo * 2080 + kt * 65 + 65],
                    wT[wbi][:, 512:1024],
                    start=(kt == 0), stop=(kt == NKT - 1)
                ).then_inc(*S.inc("psA"))

            def ps_epilogue(itx):
                m, i, qb = seq[itx]
                He, Ho = 8 * m + i, 8 * m + 4 + i
                aset = itx % 2
                A0, A1 = Aps[aset * 2], Aps[aset * 2 + 1]
                bset = itx % 2
                nc.vector.wait_ge(S["psA"], (itx + 1) * NKT)
                if itx >= 2:
                    nc.vector.wait_ge(S[f"zb{bset}"], S.cnt[f"zb{bset}"])
                nc.vector.reciprocal(zrec[bset][0:1, 0:512], A0[64:65, :])
                nc.vector.reciprocal(zrec[bset][0:1, 512:1024],
                                     A1[64:65, :]).then_inc(*S.inc("zr"))
                nc.gpsimd.wait_ge(S["zr"], S.cnt["zr"])
                h_, _ = S.inc(f"zb{bset}", 16); c_ = S.cnt[f"zb{bset}"]
                nc.gpsimd.dma_start(
                    bass.AP(d_zb, itx * 1024, [[1024, 1], [1, 1024]]),
                    zrec[bset][0:1, :]).then_inc(h_, 16)
                nc.gpsimd.wait_ge(S[f"zb{bset}"], c_)
                if itx >= 2:
                    nc.gpsimd.wait_ge(S["aev"], itx - 1)
                h_, _ = S.inc(f"bc{bset}", 16); c_ = S.cnt[f"bc{bset}"]
                nc.gpsimd.dma_start(
                    bcs[bset][:],
                    bass.AP(d_zb, itx * 1024, [[0, 128], [1, 1024]]),
                ).then_inc(h_, 16)
                nc.vector.wait_ge(S[f"bc{bset}"], c_)
                lastev = None
                for H, A_, co in ((He, A0, 0), (Ho, A1, 512)):
                    bj = H // 2
                    r0 = (H % 2) * 64
                    lastev = nc.vector.tensor_tensor(
                        attnT[r0:r0 + 64,
                              bj * TOK + qb * 512:bj * TOK + (qb + 1) * 512],
                        A_[0:64, :], bcs[bset][0:64, co:co + 512], ALU.mult)
                lastev.then_inc(*S.inc("aev"))

            woc_base = S.cnt["wcast"]

            def wo_preload(k):
                bi = k % 2
                j, half = divmod(k, 2)
                if k >= 2:
                    nc.sync.wait_ge(S["wcast"], woc_base + k - 1)
                nc.sync.dma_start(
                    wo_st[bi][:],
                    d_woT[j * 128:(j + 1) * 128,
                          half * 1024:(half + 1) * 1024]
                ).then_inc(*S.inc(f"wo{bi}", 16))
                nc.vector.wait_ge(S[f"wo{bi}"], S.cnt[f"wo{bi}"])
                nc.vector.tensor_copy(
                    woA[:, j * D + half * 1024:j * D + (half + 1) * 1024],
                    wo_st[bi][:]).then_inc(*S.inc("wcast"))

            GPS = len(seq) * NKT
            ps_S(0, 0)
            ps_S(0, 1)
            for g in range(GPS):
                itx, kt = divmod(g, NKT)
                ps_AV(itx, kt)
                if g + 2 < GPS:
                    ps_S((g + 2) // NKT, (g + 2) % NKT)
                if kt == NKT - 1:
                    ps_epilogue(itx)
                    if itx < 10:
                        wo_preload(2 * itx)
                        wo_preload(2 * itx + 1)
            nc.all_engine_barrier()
            if debug == "ps":
                o1 = dbg("dbg_attnT", [128, 16 * TOK], BF16)
                nc.sync.dma_start(o1[:], attnT[:]).then_inc(*S.inc("outd", 16))
                nc.sync.wait_ge(S["outd"], S.cnt["outd"])
                nc.all_engine_barrier()

        q_stack.close()
        kv_stack.close()

        yt_stack = contextlib.ExitStack()
        yT = yt_stack.enter_context(nc.sbuf_tensor("yT", [128, 16 * TOK], BF16))
        with contextlib.ExitStack() as wo:
            woB = wo.enter_context(nc.sbuf_tensor("woB", [128, 6 * D], BF16))
            with contextlib.ExitStack() as pre:
                wo_st2 = [pre.enter_context(
                    nc.sbuf_tensor(f"wo_st2{i}", [128, 1024], F32))
                    for i in range(2)]
                wc_base = S.cnt["wcast"]
                for k in range(12):
                    bi = k % 2
                    j, half = divmod(k, 2)
                    if k >= 2:
                        nc.sync.wait_ge(S["wcast"], wc_base + k - 1)
                    nc.sync.dma_start(
                        wo_st2[bi][:],
                        d_woT[(10 + j) * 128:(11 + j) * 128,
                              half * 1024:(half + 1) * 1024]
                    ).then_inc(*S.inc(f"wo{bi}", 16))
                    nc.vector.wait_ge(S[f"wo{bi}"], S.cnt[f"wo{bi}"])
                    nc.vector.tensor_copy(
                        woB[:, j * D + half * 1024:j * D + (half + 1) * 1024],
                        wo_st2[bi][:]).then_inc(*S.inc("wcast"))
                nc.vector.drain().then_inc(*S.inc("wcast"))
                woB_target = S.cnt["wcast"]
            bob = wo.enter_context(nc.sbuf_tensor("s_bob", [128, D], F32))
            xr = [wo.enter_context(nc.sbuf_tensor(f"xr{i}", [128, D], F32))
                  for i in range(2)]
            wtmp = [wo.enter_context(nc.sbuf_tensor(f"wtmp{i}", [128, 512], F32))
                    for i in range(4)]
            ytile = [wo.enter_context(nc.sbuf_tensor(f"ytile{i}", [128, D], F32))
                     for i in range(2)]
            ybf = [wo.enter_context(nc.sbuf_tensor(f"ybf{i}", [128, D], BF16))
                   for i in range(2)]
            junk = wo.enter_context(nc.sbuf_tensor("junk", [128, D], F32))
            psW = [wo.enter_context(nc.psum_tensor(f"psW{i}", [128, 512], F32))
                   for i in range(8)]

            nc.scalar.wait_ge(S["wcast"], woB_target)
            nc.scalar.dma_start(bob[:], d_bob[:]).then_inc(*S.inc("bobd", 16))
            nc.scalar.wait_ge(S["bobd"], 16)
            nc.all_engine_barrier()

            def emit_transposes(qt):
                par = qt % 2
                ws = par * 4
                if qt == 0:
                    nc.tensor.wait_ge(S["pre"], 96)
                nc.tensor.wait_ge(S["f1c"], qt + 1)
                lastt = None
                for k in range(16):
                    kk = k % 8
                    pv = bass.AP(psW[ws + k // 8], kk * 64,
                                 [[512, 128], [1, 64]]).bitcast(BF16)
                    lastt = nc.tensor.transpose(
                        pv, ybf[par][:, k * 128:(k + 1) * 128], ident[:])
                lastt.then_inc(*S.inc("f1mm"))
                nc.vector.wait_ge(S["f1mm"], S.cnt["f1mm"])
                for half in range(2):
                    pv = bass.AP(psW[ws + half], 0,
                                 [[512, 128], [1, 512]]).bitcast(BF16)
                    dst = bass.AP(yT, (half * 8) * TOK + qt * 128,
                                  [[16 * TOK, 128], [TOK, 8], [1, 128]])
                    ev = nc.vector.tensor_copy(dst, pv)
                ev.then_inc(*S.inc("f1ev"))

            for qt in range(8):
                par = qt % 2
                ws = par * 4
                if qt >= 2:
                    nc.sync.wait_ge(S["wo_ev"], qt - 1)
                h_, _ = S.inc(f"wox{par}", 16); c_ = S.cnt[f"wox{par}"]
                nc.sync.dma_start(xr[par][:],
                                  d_xres[qt * 128:(qt + 1) * 128, :]
                                  ).then_inc(h_, 16)
                if qt >= 2:
                    nc.tensor.wait_ge(S["wo_ev"], qt - 1)
                    nc.tensor.wait_ge(S["f1ev"], qt - 1)
                mmwo = None
                for j in range(16):
                    if qt == 0 and j >= 10:
                        nc.tensor.wait_ge(S["wcast"],
                                          woB_target - (15 - j) * 2)
                    wsrc = (woA[:, j * D:(j + 1) * D] if j < 10
                            else woB[:, (j - 10) * D:(j - 9) * D])
                    for ds in range(4):
                        mmwo = nc.tensor.matmul(
                            psW[ws + ds][:],
                            attnT[:, j * TOK + qt * 128:j * TOK + (qt + 1) * 128],
                            wsrc[:, ds * 512:(ds + 1) * 512],
                            start=(j == 0), stop=(j == 15))
                mmwo.then_inc(*S.inc("wo_mm"))
                nc.vector.wait_ge(S["wo_mm"], qt + 1)
                nc.vector.wait_ge(S[f"wox{par}"], c_)
                if qt >= 2:
                    nc.vector.wait_ge(S[f"ysp{par}"],
                                      S.cnt[f"ysp{par}"])
                for ds in range(4):
                    nc.vector.tensor_tensor(
                        wtmp[ds][:], psW[ws + ds][:],
                        xr[par][:, ds * 512:(ds + 1) * 512], ALU.add)
                nc.vector.drain()
                lastv = None
                for ds in range(4):
                    lastv = nc.vector.tensor_tensor(
                        ytile[par][:, ds * 512:(ds + 1) * 512],
                        wtmp[ds][:], bob[:, ds * 512:(ds + 1) * 512], ALU.add)
                lastv.then_inc(*S.inc("wo_ev"))
                rseg = ytile[par][:]
                nc.scalar.wait_ge(S["wo_ev"], qt + 1)
                nc.scalar.activation(junk[:], rseg, AF.Square,
                                     accum_out=lt[:, 1:2])
                nc.vector.drain()
                nc.vector.tensor_reduce(lt[:, 0:1], rseg,
                                        mybir.AxisListType.X, ALU.add
                                        ).then_inc(*S.inc("ln1"))
                nc.scalar.drain()
                nc.scalar.wait_ge(S["ln1"], S.cnt["ln1"])
                nc.scalar.activation(lt[:, 2:3], lt[:, 0:1], AF.Copy,
                                     scale=-1.0 / D)
                nc.scalar.drain()
                nc.scalar.activation(lt[:, 3:4], lt[:, 2:3], AF.Square)
                nc.scalar.drain()
                nc.scalar.activation(lt[:, 4:5], lt[:, 1:2], AF.Copy,
                                     scale=1.0 / D).then_inc(*S.inc("ln1"))
                nc.vector.wait_ge(S["ln1"], S.cnt["ln1"])
                nc.vector.tensor_tensor(lt[:, 5:6], lt[:, 4:5], lt[:, 3:4],
                                        ALU.subtract)
                nc.vector.drain()
                nc.vector.tensor_tensor(lt[:, 6:7], lt[:, 5:6], epsp[:],
                                        ALU.add).then_inc(*S.inc("ln1"))
                nc.scalar.wait_ge(S["ln1"], S.cnt["ln1"])
                nc.scalar.activation(lt[:, 7:8], lt[:, 6:7], AF.Ln)
                nc.scalar.drain()
                nc.scalar.activation(lt[:, 6:7], lt[:, 7:8], AF.Exp,
                                     scale=-0.5).then_inc(*S.inc("ln1"))
                nc.vector.wait_ge(S["ln1"], S.cnt["ln1"])
                nc.vector.tensor_scalar_mul(lt[:, 7:8], lt[:, 2:3],
                                            lt[:, 6:7]).then_inc(*S.inc("ln1"))
                nc.scalar.wait_ge(S["ln1"], S.cnt["ln1"])
                nc.scalar.activation(rseg, rseg, AF.Identity, bias=lt[:, 7:8],
                                     scale=lt[:, 6:7]).then_inc(*S.inc("ln1"))
                nc.sync.wait_ge(S["ln1"], S.cnt["ln1"])
                nc.sync.dma_start(d_ydr.ap()[qt * 128:(qt + 1) * 128, :], rseg
                                  ).then_inc(*S.inc(f"ysp{par}", 16))
                nc.vector.wait_ge(S["ln1"], S.cnt["ln1"])
                if qt >= 2:
                    nc.vector.wait_ge(S["f1mm"], qt - 1)
                nc.vector.tensor_copy(ybf[par][:], rseg).then_inc(*S.inc("f1c"))
                emit_transposes(qt)
            nc.sync.wait_ge(S["ysp0"], S.cnt["ysp0"])
            nc.sync.wait_ge(S["ysp1"], S.cnt["ysp1"])
            nc.all_engine_barrier()
            if debug == "wo":
                o1 = dbg("dbg_yT", [128, 16 * TOK], BF16)
                nc.sync.dma_start(o1[:], yT[:]).then_inc(*S.inc("outd", 16))
                nc.sync.wait_ge(S["outd"], S.cnt["outd"])
                nc.all_engine_barrier()

        attn_right.close()
        f1_right = contextlib.ExitStack()
        hdnT = f1_right.enter_context(
            nc.sbuf_tensor("hdnT", [128, NIT * TOK], BF16, side="right"))
        with contextlib.ExitStack() as f1:
            g_st = [f1.enter_context(nc.sbuf_tensor(f"g_st{i}", [128, D], F32))
                    for i in range(2)]
            g_bf = [f1.enter_context(nc.sbuf_tensor(f"g_bf{i}", [128, D], BF16))
                    for i in range(2)]
            u_st = [f1.enter_context(nc.sbuf_tensor(f"u_st{i}", [128, D], F32))
                    for i in range(2)]
            u_bf = [f1.enter_context(nc.sbuf_tensor(f"u_bf{i}", [128, D], BF16))
                    for i in range(2)]
            sg = [f1.enter_context(nc.sbuf_tensor(f"sg{i}", [128, 1024], BF16))
                  for i in range(2)]
            su = [f1.enter_context(nc.sbuf_tensor(f"su{i}", [128, 1024], BF16))
                  for i in range(2)]
            psG = [f1.enter_context(nc.psum_tensor(f"psG{i}", [128, 1024], F32))
                   for i in range(4)]

            f1w_base = S.cnt["f1wc"]
            gu_base = S.cnt["gu_mm"]
            hdn_base = S.cnt["hdn"]
            ev_base = S.cnt["gu_ev"]

            def f1_load(k):
                bi = k % 2
                if k >= 2:
                    nc.sync.wait_ge(S["f1wc"], f1w_base + k - 1)
                for dsrc, st_ in ((d_gwT, g_st[bi]), (d_uwT, u_st[bi])):
                    for half in range(2):
                        src = bass.AP(dsrc, half * 8 * 128 * INTER + k * 128,
                                      [[INTER, 128], [128 * INTER, 8], [1, 128]])
                        dst = bass.AP(st_, half * 1024,
                                      [[D, 128], [128, 8], [1, 128]])
                        nc.sync.dma_start(dst, src).then_inc(
                            *S.inc(f"f1w{bi}", 16))

            def f1_cast(k):
                bi = k % 2
                nc.vector.wait_ge(S[f"f1w{bi}"], S.cnt[f"f1w{bi}"])
                if k >= 2:
                    nc.vector.wait_ge(S["gu_mm"], gu_base + k - 1)
                nc.vector.tensor_copy(g_bf[bi][:], g_st[bi][:])
                nc.vector.tensor_copy(u_bf[bi][:], u_st[bi][:]
                                      ).then_inc(*S.inc("f1wc"))

            def f1_comp(k):
                bi = k % 2
                pg, pu = psG[bi * 2], psG[bi * 2 + 1]
                nc.tensor.wait_ge(S["f1wc"], f1w_base + k + 1)
                if k >= 2:
                    nc.tensor.wait_ge(S["hdn"], hdn_base + k - 1)
                lastm = None
                for hc in range(16):
                    for pp, wbf in ((pg, g_bf[bi]), (pu, u_bf[bi])):
                        for th in range(2):
                            lastm = nc.tensor.matmul(
                                pp[:, th * 512:(th + 1) * 512],
                                wbf[:, hc * 128:(hc + 1) * 128],
                                yT[:, hc * TOK + th * 512:
                                   hc * TOK + (th + 1) * 512],
                                start=(hc == 0), stop=(hc == 15))
                lastm.then_inc(*S.inc("gu_mm"))
                nc.scalar.wait_ge(S["gu_mm"], gu_base + k + 1)
                if k == 0:
                    nc.scalar.wait_ge(S["pre"], 96)
                if k >= 2:
                    nc.scalar.wait_ge(S["hdn"], hdn_base + k - 1)
                nc.scalar.activation(sg[bi][:], pg[:], AF.Silu,
                                     scale=gst[:, k:k + 1])
                nc.scalar.activation(su[bi][:], pu[:], AF.Copy,
                                     scale=ust[:, k:k + 1]
                                     ).then_inc(*S.inc("gu_ev"))
                nc.gpsimd.wait_ge(S["gu_ev"], ev_base + k + 1)
                nc.gpsimd.tensor_tensor(hdnT[:, k * TOK:(k + 1) * TOK],
                                        sg[bi][:], su[bi][:], ALU.mult
                                        ).then_inc(*S.inc("hdn"))

            f1_load(0)
            f1_cast(0)
            f1_load(1)
            f1_cast(1)
            for k in range(NIT):
                if k + 2 < NIT:
                    f1_load(k + 2)
                    f1_cast(k + 2)
                f1_comp(k)
            nc.all_engine_barrier()
            if debug == "ffn1":
                o1 = dbg("dbg_hdnT", [128, NIT * TOK], BF16)
                nc.sync.dma_start(o1[:], hdnT[:]).then_inc(*S.inc("outd", 16))
                nc.sync.wait_ge(S["outd"], S.cnt["outd"])
                nc.all_engine_barrier()

        yt_stack.close()
        with contextlib.ExitStack() as f2:
            res2 = f2.enter_context(nc.sbuf_tensor("res2", [128, 8 * D], F32))
            dsb = f2.enter_context(nc.sbuf_tensor("s_dsb", [128, D], F32))
            lt2 = f2.enter_context(nc.sbuf_tensor("lt2", [128, 64], F32))
            dw_st = [f2.enter_context(
                nc.sbuf_tensor(f"dw_st{i}", [128, 512], F32)) for i in range(3)]
            dw_bf = [f2.enter_context(
                nc.sbuf_tensor(f"dw_bf{i}", [128, 512], BF16)) for i in range(3)]
            dtmp = [f2.enter_context(
                nc.sbuf_tensor(f"dtmp{i}", [128, 512], F32)) for i in range(8)]
            psD = [f2.enter_context(nc.psum_tensor(f"psD{i}", [128, 512], F32))
                   for i in range(8)]

            nc.scalar.wait_ge(S["ysp0"], S.cnt["ysp0"])
            nc.scalar.wait_ge(S["ysp1"], S.cnt["ysp1"])
            nc.scalar.dma_start(dsb[:], d_dsb[:]).then_inc(*S.inc("dsbd", 16))
            for qt in range(8):
                nc.scalar.dma_start(res2[:, qt * D:(qt + 1) * D],
                                    d_ydr.ap()[qt * 128:(qt + 1) * 128, :]
                                    ).then_inc(*S.inc("dsbd", 16))
            nc.scalar.wait_ge(S["dsbd"], 144)
            nc.all_engine_barrier()

            f2c_base = S.cnt["f2c"]
            f2m_base = S.cnt["f2mm"]

            def f2_load(g):
                bi = g % 3
                hs_, k_ = divmod(g, NIT)
                if g >= 3:
                    nc.sync.wait_ge(S["f2c"], f2c_base + g - 2)
                nc.sync.dma_start(
                    dw_st[bi][:],
                    d_dwT[k_ * 128:(k_ + 1) * 128,
                          hs_ * 512:(hs_ + 1) * 512]
                ).then_inc(*S.inc(f"f2w{bi}", 16))

            def f2_cast(g):
                bi = g % 3
                nc.vector.wait_ge(S[f"f2w{bi}"], S.cnt[f"f2w{bi}"])
                if g >= 3:
                    nc.vector.wait_ge(S["f2mm"], f2m_base + g - 2)
                nc.vector.tensor_copy(dw_bf[bi][:], dw_st[bi][:]
                                      ).then_inc(*S.inc("f2c"))

            def f2_comp(g):
                bi = g % 3
                hs_, k_ = divmod(g, NIT)
                nc.tensor.wait_ge(S["f2c"], f2c_base + g + 1)
                if k_ == 0 and hs_ > 0:
                    nc.tensor.wait_ge(S["f2ev"], hs_)
                lastm = None
                for tt in range(8):
                    lastm = nc.tensor.matmul(
                        psD[tt][:],
                        hdnT[:, k_ * TOK + tt * 128:k_ * TOK + (tt + 1) * 128],
                        dw_bf[bi][:],
                        start=(k_ == 0), stop=(k_ == NIT - 1))
                lastm.then_inc(*S.inc("f2mm"))

            def f2_evac(hs_):
                nc.vector.wait_ge(S["f2mm"], f2m_base + (hs_ + 1) * NIT)
                for tt in range(8):
                    nc.vector.tensor_tensor(dtmp[tt][:], psD[tt][:],
                                            dsb[:, hs_ * 512:(hs_ + 1) * 512],
                                            ALU.mult)
                nc.vector.drain().then_inc(*S.inc("f2ev"))
                for tt in range(8):
                    seg = res2[:, tt * D + hs_ * 512:tt * D + (hs_ + 1) * 512]
                    nc.vector.tensor_tensor(seg, seg, dtmp[tt][:], ALU.add)
                nc.vector.drain()

            NG = 4 * NIT
            for g0 in range(3):
                f2_load(g0)
                f2_cast(g0)
            for g in range(NG):
                if g + 3 < NG:
                    f2_load(g + 3)
                    f2_cast(g + 3)
                f2_comp(g)
                if g % NIT == NIT - 1:
                    f2_evac(g // NIT)
            nc.all_engine_barrier()

            qstat = None
            for qt in range(8):
                rseg = res2[:, qt * D:(qt + 1) * D]
                if qt > 0:
                    nc.scalar.drain()
                qstat = nc.scalar.activation(dsb[:], rseg, AF.Square,
                                             accum_out=lt2[:, 8 + qt:9 + qt])
                nc.vector.tensor_reduce(lt2[:, qt:qt + 1], rseg,
                                        mybir.AxisListType.X, ALU.add)
            qstat.then_inc(*S.inc("ln2"))
            nc.vector.drain()
            nc.vector.wait_ge(S["ln2"], S.cnt["ln2"])
            nc.vector.tensor_scalar_mul(lt2[:, 16:24], lt2[:, 0:8], -1.0 / D)
            nc.vector.tensor_scalar_mul(lt2[:, 24:32], lt2[:, 8:16], 1.0 / D)
            nc.vector.drain()
            nc.vector.tensor_tensor(lt2[:, 32:40], lt2[:, 16:24],
                                    lt2[:, 16:24], ALU.mult)
            nc.vector.drain()
            nc.vector.tensor_tensor(lt2[:, 40:48], lt2[:, 24:32],
                                    lt2[:, 32:40], ALU.subtract)
            nc.vector.drain()
            nc.vector.tensor_scalar_add(lt2[:, 48:56], lt2[:, 40:48], EPS
                                        ).then_inc(*S.inc("ln2"))
            nc.scalar.wait_ge(S["ln2"], S.cnt["ln2"])
            nc.scalar.activation(lt2[:, 40:48], lt2[:, 48:56], AF.Ln)
            nc.scalar.drain()
            nc.scalar.activation(lt2[:, 48:56], lt2[:, 40:48], AF.Exp,
                                 scale=-0.5).then_inc(*S.inc("ln2"))
            nc.vector.wait_ge(S["ln2"], S.cnt["ln2"])
            nc.vector.tensor_tensor(lt2[:, 56:64], lt2[:, 16:24],
                                    lt2[:, 48:56], ALU.mult
                                    ).then_inc(*S.inc("ln2"))
            nc.scalar.wait_ge(S["ln2"], S.cnt["ln2"])
            for qt in range(8):
                rseg = res2[:, qt * D:(qt + 1) * D]
                nc.scalar.activation(rseg, rseg, AF.Identity,
                                     bias=lt2[:, 56 + qt:57 + qt],
                                     scale=lt2[:, 48 + qt:49 + qt]
                                     ).then_inc(*S.inc("ln2"))
                nc.sync.wait_ge(S["ln2"], S.cnt["ln2"])
                nc.sync.dma_start(d_out[qt * 128:(qt + 1) * 128, :], rseg
                                  ).then_inc(*S.inc("outd", 16))
            nc.sync.wait_ge(S["outd"], S.cnt["outd"])

        f1_right.close()

    return nc, sorted(dbg_outs)


def _rope_tables(t0, n):
    inv = 10000.0 ** (-(np.arange(32, dtype=np.float64) * 2) / 64.0)
    p = np.arange(128)
    t = np.arange(t0, t0 + n, dtype=np.float64)
    ang = t[None, :] * inv[p % 32][:, None]
    cos = np.cos(ang).astype(np.float32)
    sin = np.sin(ang).astype(np.float32)
    sgn = np.where((p % 64) < 32, -1.0, 1.0).astype(np.float32)
    return cos, np.ascontiguousarray(sin * sgn[:, None]).astype(np.float32)


def _prep_inputs(inp):
    c = np.ascontiguousarray

    def _blk(wT, nk):
        a = wT.reshape(2, 8, 128, nk, 128)
        return c(a.transpose(3, 2, 0, 1, 4).reshape(nk * 128, 2048))

    x = np.asarray(inp["L0_attn_x"], dtype=np.float32)
    cosK, sinK = _rope_tables(0, T)
    base = {
        "wqT": c(np.asarray(inp["L0_attn_Wq"]).T),
        "wkT": c(np.asarray(inp["L0_attn_Wk"]).T),
        "wvT": c(np.asarray(inp["L0_attn_Wv"]).T),
        "woT": c(np.asarray(inp["L0_attn_Wo"]).T),
        "bqt": c(np.asarray(inp["L0_attn_bq"]).reshape(16, 128).T),
        "bkt": c(np.asarray(inp["L0_attn_bk"]).reshape(4, 128).T),
        "bvb": c(np.broadcast_to(np.asarray(inp["L0_attn_bv"])[None, :],
                                 (128, 512))),
        "bob": c(np.broadcast_to(np.asarray(inp["L0_attn_bo"])[None, :],
                                 (128, D))),
        "gwT": c(np.asarray(inp["L0_ffn_gate_w"]).T),
        "uwT": c(np.asarray(inp["L0_ffn_up_w"]).T),
        "dwT": c(np.asarray(inp["L0_ffn_down_w"]).T),
        "gst": c(np.asarray(inp["L0_ffn_gate_s"]).reshape(NIT, 128).T),
        "ust": c(np.asarray(inp["L0_ffn_up_s"]).reshape(NIT, 128).T),
        "dsb": c(np.broadcast_to(np.asarray(inp["L0_ffn_down_s"])[None, :],
                                 (128, D))),
        "cosK": cosK, "sinK": sinK,
        "identb": np.eye(128).astype(ml_dtypes.bfloat16),
    }
    xT = [c(x[b].T) for b in range(B)]
    maps = []
    for core in range(N_CORES):
        b = core // 4
        q0 = (core % 4) * TOK
        cosQ, sinQ = _rope_tables(q0, TOK)
        m = dict(base)
        m["xT"] = xT[b]
        m["xqT"] = c(xT[b][:, q0:q0 + TOK])
        m["xres"] = c(x[b, q0:q0 + TOK])
        m["cosQ"] = cosQ
        m["sinQ"] = sinQ
        maps.append(m)
    return maps


def kernel(**inputs):
    try:
        from concourse.compiler_utils import (get_compiler_flags,
                                              set_compiler_flags)
        set_compiler_flags([
            f.replace("--enable-ldw-opt=false", "--enable-ldw-opt=true")
            for f in get_compiler_flags()])
    except Exception:
        pass
    maps = _prep_inputs(inputs)
    nc, _ = build_program()
    res = run_bass_kernel_spmd(nc, maps, core_ids=list(range(N_CORES)))
    out = np.empty((B, T, D), np.float32)
    for core in range(N_CORES):
        b = core // 4
        q0 = (core % 4) * TOK
        out[b, q0:q0 + TOK] = np.asarray(res.results[core]["out"])
    return out
